# version 04
# baseline (speedup 1.0000x reference)
import contextlib
import math
import numpy as np
import concourse.bass as bass
import concourse.mybir as mybir
from concourse.bass_utils import run_bass_kernel_spmd

F32 = mybir.dt.float32
BF16 = mybir.dt.bfloat16
AF = mybir.ActivationFunctionType
ALU = mybir.AluOpType
AX = mybir.AxisListType

T = 4096
D = 2048
HK, HV, AK, UV = 16, 32, 128, 128
KEY_DIM = HK * AK
VALUE_DIM = HV * UV
CONV_DIM = 2 * KEY_DIM + VALUE_DIM
CH = 128
NCH = T // CH
NG = T // 128
BIG = 1.0e6
C_SCALE = AK ** -0.5
LN_EPS = 1e-5
RMS_EPS = 1e-6
L2_EPS = 1e-6
NEXP = 7
FT = 1024
NFT = T // FT
ENGINES = ("sync", "tensor", "vector", "scalar", "gpsimd")


class Pipe:

    def __init__(self, sem, name):
        self.sem = sem
        self.name = name
        self.n = 0

    def at(self):
        return (self, self.n)


_DRAIN_OPS = frozenset([
    "tensor_tensor", "tensor_scalar", "tensor_copy", "tensor_reduce",
    "reduce_sum", "reduce_max", "scalar_tensor_tensor", "reciprocal",
    "memset", "memzero", "select", "bn_stats", "bn_aggr", "activation",
    "copy", "add", "mul", "square", "sqrt", "sign", "tensor_add",
    "tensor_sub", "tensor_mul", "tensor_scalar_add", "tensor_scalar_mul",
    "tensor_scalar_max", "tensor_scalar_min", "tensor_scalar_sub",
    "tensor_relu", "tensor_max", "transpose", "iota", "copy_predicated",
    "max", "max_index", "max_with_indices", "stream_shuffle",
])
_FLUSH_OPS = frozenset(["dma_start", "dma_start_transpose"])


class DrainProxy:

    def __init__(self, e):
        object.__setattr__(self, "_e", e)
        object.__setattr__(self, "_armed", False)
        object.__setattr__(self, "_par", False)
        object.__setattr__(self, "_par_done", False)

    def begin_step(self, par):
        object.__setattr__(self, "_par", par)
        object.__setattr__(self, "_par_done", False)

    def __getattr__(self, name):
        e = object.__getattribute__(self, "_e")
        attr = getattr(e, name)
        if name in _DRAIN_OPS:
            def wrapped(*a, **k):
                par = object.__getattribute__(self, "_par")
                done = object.__getattribute__(self, "_par_done")
                if object.__getattribute__(self, "_armed") and not (par and done):
                    e.drain()
                object.__setattr__(self, "_armed", True)
                object.__setattr__(self, "_par_done", True)
                return attr(*a, **k)
            return wrapped
        if name in _FLUSH_OPS:
            def wrapped2(*a, **k):
                if object.__getattribute__(self, "_armed"):
                    e.drain()
                    object.__setattr__(self, "_armed", False)
                return attr(*a, **k)
            return wrapped2
        return attr


class Sched:

    def __init__(self, nc, exitstack):
        self.nc = nc
        self.exit = exitstack
        self.steps = []
        self._pipes = {}

    def pipe(self, name):
        if name not in self._pipes:
            sem = self.exit.enter_context(self.nc.semaphore(name))
            self._pipes[name] = Pipe(sem, name)
        return self._pipes[name]

    def step(self, engine, fn, waits=(), incs=(), par=False):
        norm = []
        for item in incs:
            if len(item) == 2:
                p_, a_ = item
                norm.append((p_, a_, "last"))
            else:
                norm.append(item)
        self.steps.append((engine, fn, list(waits), norm, par))
        for p_, a_, mode in norm:
            if mode == "each":
                p_.n += a_ * self._count_instrs_hint
            else:
                p_.n += a_
        return tuple(p_.n for p_, _, _ in norm)

    _count_instrs_hint = 1

    def dma(self, engine, fn, waits=(), pipe=None, n=1):
        self._count_instrs_hint = n
        r = self.step(engine, fn, waits,
                      incs=([(pipe, 16, "each")] if pipe else ()))
        self._count_instrs_hint = 1
        return r

    def emit(self, blk):
        by_eng = {e: [] for e in ENGINES}
        for st in self.steps:
            by_eng[st[0]].append(st)
        for eng_name in ENGINES:
            steps = by_eng[eng_name]
            if not steps:
                continue

            def body(e, steps=steps, eng_name=eng_name):
                if eng_name in ("vector", "scalar", "gpsimd"):
                    e = DrainProxy(e)
                watermark = {}
                for _, fn, waits, incs, par_ in steps:
                    if isinstance(e, DrainProxy):
                        e.begin_step(par_)
                    for p_, val in waits:
                        if val > 0 and watermark.get(p_.name, -1) < val:
                            e.wait_ge(p_.sem, val)
                            watermark[p_.name] = val
                    out = fn(e)
                    if incs:
                        instrs = out if isinstance(out, (list, tuple)) else [out]
                        n_on_last = 0
                        for p_, a_, mode in incs:
                            if mode == "each":
                                for ins in instrs:
                                    ins.then_inc(p_.sem, a_)
                            elif n_on_last == 0:
                                instrs[-1].then_inc(p_.sem, a_)
                                n_on_last = 1
                            else:
                                if eng_name in ("vector", "scalar", "gpsimd"):
                                    (e._e if isinstance(e, DrainProxy) else e
                                     ).drain()
                                    if isinstance(e, DrainProxy):
                                        object.__setattr__(e, "_armed", False)
                                e.sem_inc(p_.sem, a_)

            getattr(blk, eng_name)(body)


class Builder:
    def __init__(self, debug=()):
        self.debug = set(debug)
        self.nc = bass.Bass("TRN2", target_bir_lowering=False)
        self.exit = contextlib.ExitStack()

    def dram(self, name, shape, dtype, kind="Internal"):
        k = "ExternalOutput" if name in self.debug else kind
        return self.nc.dram_tensor(name, shape, dtype, kind=k).ap()

    def declare(self):
        nc = self.nc
        dram = self.dram
        inp = lambda n, s: nc.dram_tensor(n, s, F32, kind="ExternalInput").ap()
        self.x_in = inp("hidden", [T, D])
        self.w_qkv = inp("w_qkv", [D, CONV_DIM])
        self.w_z = inp("w_z", [D, VALUE_DIM])
        self.w_ba = inp("w_ba", [D, 64])
        self.conv_w = inp("conv_w", [CONV_DIM, 4])
        self.dt_bias_r = inp("dt_bias_r", [1, HV])
        self.a_log_r = inp("a_log_r", [1, HV])
        self.norm_w_r = inp("norm_w_r", [1, UV])
        self.w_out = inp("w_out", [VALUE_DIM, D])
        self.ffn_w1 = inp("ffn_w1", [NEXP, D, D])
        self.ffn_wg = inp("ffn_wg", [NEXP, D, D])
        self.ffn_w2 = inp("ffn_w2", [NEXP, D, D])
        self.ffn_ws1 = inp("ffn_ws1", [D, D])
        self.ffn_wsg = inp("ffn_wsg", [D, D])
        self.ffn_ws2 = inp("ffn_ws2", [D, D])
        self.gate_w = inp("gate_w", [D, 8])
        self.gate_b_r = inp("gate_b_r", [1, 8])
        self.biases_r = inp("biases_r", [1, 8])
        self.c_masks = inp("c_masks", [4, 128, 128])
        self.y_out = nc.dram_tensor("y", [T, D], F32, kind="ExternalOutput").ap()

        self.s_qf = dram("s_qf", [HK, AK, T], BF16)
        self.s_kf = dram("s_kf", [HK, AK, T], BF16)
        self.s_kt = dram("s_kt", [HK, NG, 128, AK], BF16)
        self.s_vt = dram("s_vt", [HV, NG, 128, UV], BF16)
        self.s_z = dram("s_z", [T, VALUE_DIM], BF16)
        self.s_gj = dram("s_gj", [NCH, HV, CH], F32)
        self.s_gl = dram("s_gl", [NCH, HV], F32)
        self.s_y1 = dram("s_y1", [T, D], F32)
        self.s_wob2 = dram("s_wob2", [8, 128, 4, D], BF16)
        self.s_fab = dram("s_fab", [16, 16, 128, D], BF16)
        self.s_f2b = dram("s_f2b", [8, 16, 128, D], BF16)
        self.s_wg8 = dram("s_wg8", [NFT, 8, FT], BF16)

        A = nc.alloc_sbuf_tensor
        self.masks = A("masks", [128, 4 * 128], F32)
        self.normw_bc = A("normw_bc", [128, UV], F32)
        self.dtb_bc = A("dtb_bc", [128, HV], F32)
        self.negea_bc = A("negea_bc", [128, HV], F32)
        self.gateb_bc = A("gateb_bc", [128, 8], F32)
        self.biases_bc = A("biases_bc", [128, 8], F32)
        self.ba_tm = A("ba_tm", [128, NG * 64], F32)
        self.beta_s = A("beta_s", [128, NCH * HV], F32)
        self.negsc1_s = A("negsc1_s", [128, NCH * HV], F32)
        self.egjc_s = A("egjc_s", [128, NCH * HV], F32)
        self.egLmg_s = A("egLmg_s", [128, NCH * HV], F32)
        self.egL_s = A("egL_s", [128, NCH * HV], F32)
        self.gj_s = A("gj_s", [128, NCH * HV], F32)
        self.S_f = A("S_f", [128, HV * UV], F32)
        self.S_b = A("S_b", [128, HV * UV], BF16)
        self.ARENA_W = 38100
        self.arena = A("arena", [128, self.ARENA_W], F32)
        self.psum = [nc.alloc_psum_tensor(f"pb{i}", [128, 512], F32)
                     for i in range(8)]

        self.m_sl_big = self.masks[:, 0:128]
        self.m_u_negbig = self.masks[:, 128:256]
        self.m_triu = self.masks[:, 256:384]
        self.ones_col = self.masks[:, 383:384]
        self.ident_f = self.masks[:, 384:512]

    def carve(self, off, n):
        assert off + n <= self.ARENA_W, (off, n)
        return self.arena.ap()[:, off:off + n], off + n

    def carve_b(self, off, n_h):
        nw = (n_h + 1) // 2
        assert off + nw <= self.ARENA_W, (off, n_h)
        return self.arena.ap().bitcast(BF16)[:, off * 2: off * 2 + n_h], off + nw

    def pp(self, tens, j, h):
        return tens.ap()[:, j * HV + h: j * HV + h + 1]

    def build(self, phases=("a01", "a2", "a3", "b")):
        self.declare()
        S = Sched(self.nc, self.exit)
        self.S = S
        self.ph_consts()
        if "a01" in phases:
            self.ph_a01()
        if "a2" in phases:
            self.ph_a2()
        if "a3" in phases:
            self.ph_a3()
        if "b" in phases:
            self.ph_b()
        self.ph_final()
        with self.nc.Block() as blk:
            S.emit(blk)
        self.exit.close()
        return self.nc

    def ph_consts(self):
        s, S = self, self.S
        ld = S.pipe("c_ld")
        self.c_done = S.pipe("c_done")
        S.dma("sync", lambda e: [
            e.dma_start(s.masks[:].rearrange("p (k n) -> p k n", k=4),
                        s.c_masks.transpose([1, 0, 2])),
            e.dma_start(s.normw_bc[:], s.norm_w_r.to_broadcast([128, UV])),
            e.dma_start(s.dtb_bc[:], s.dt_bias_r.to_broadcast([128, HV])),
            e.dma_start(s.negea_bc[:], s.a_log_r.to_broadcast([128, HV])),
            e.dma_start(s.gateb_bc[:], s.gate_b_r.to_broadcast([128, 8])),
            e.dma_start(s.biases_bc[:], s.biases_r.to_broadcast([128, 8])),
        ], pipe=ld, n=6)
        c_act = S.pipe("c_act")
        S.step("scalar", lambda e: e.activation(s.negea_bc[:], s.negea_bc[:], AF.Exp),
               waits=[ld.at()], incs=[(c_act, 1)])
        S.step("vector",
               lambda e: e.tensor_scalar(s.negea_bc[:], s.negea_bc[:], -1.0, None,
                                         ALU.mult),
               waits=[c_act.at()], incs=[(self.c_done, 1)])

    def ph_a01(self):
        s, S = self, self.S
        off = 0
        xn_t, off = self.carve_b(off, 16 * 2048)
        xh, off = self.carve(off, 2 * D)
        sq, off = self.carve(off, D)
        xn_tm, off = self.carve_b(off, D)
        st_, off = self.carve(off, 16)
        wba_f, off = self.carve(off, 16 * 64)
        wba_b, off = self.carve_b(off, 16 * 64)
        wst, off = self.carve(off, 2 * 2048)
        wbb, off = self.carve_b(off, 2 * 2048)
        cw, off = self.carve(off, 2 * 4)
        cat, off = self.carve(off, 2 * 520)
        acc, off = self.carve(off, 2 * 512)
        sil, off = self.carve_b(off, 2 * 512)
        tmst, off = self.carve_b(off, 2 * 512)
        sqn, off = self.carve(off, 512)
        ssq, off = self.carve(off, 2 * 8)
        nfm, off = self.carve_b(off, 2 * 512)
        tails, off = self.carve(off, 64 * 4)
        b0f, off = self.carve(off, 2 * 512)
        b0b, off = self.carve_b(off, 2 * 512)
        self.a01_end = off
        xn_t3 = xn_t.rearrange("p (k t) -> p k t", k=16)
        wbb3 = wbb.rearrange("p (k n) -> p k n", k=16)
        wba_b3 = wba_b.rearrange("p (k n) -> p k n", k=16)

        ld2 = [S.pipe("a0_ld0"), S.pipe("a0_ld1")]
        sqp, v1, v2, tp = (S.pipe("a0_sq"), S.pipe("a0_v1"),
                           S.pipe("a0_v2"), S.pipe("a0_tp"))
        gld, gcast, gmm, gev = (S.pipe("g_ld"), S.pipe("g_cast"), S.pipe("g_mm"),
                                S.pipe("g_ev"))
        wld2 = [S.pipe("a1_wld0"), S.pipe("a1_wld1")]
        wcast, amm, aev, acv, asl = (
            S.pipe("a1_wcast"), S.pipe("a1_mm"),
            S.pipe("a1_ev"), S.pipe("a1_cv"), S.pipe("a1_sl"))
        atp2 = [S.pipe("a1_tp0"), S.pipe("a1_tp1")]
        atp_done = {}
        ntp2 = [S.pipe("a1_ntp0"), S.pipe("a1_ntp1")]
        sp_tm = [S.pipe("a1_sptm0"), S.pipe("a1_sptm1")]
        sp_fm = [S.pipe("a1_spfm0"), S.pipe("a1_spfm1")]
        n_qkv = [0]
        tm_guard = {}
        fm_guard = {}
        ntp_done = {}
        conv_done = {}
        nrm, ntp, nr2 = S.pipe("a1_nrm"), S.pipe("a1_ntp"), S.pipe("a1_nr2")
        qk_units = [0]
        b0p, b0c = S.pipe("b0_dma"), S.pipe("b0_cast")
        self.g_ev = gev
        self.a1_ev = aev

        S.step("vector", lambda e: e.memset(tails, 0.0))

        b0_jobs = []
        amats = ([(s.ffn_w1[e_], e_) for e_ in range(NEXP)]
                 + [(s.ffn_wg[e_], NEXP + e_) for e_ in range(NEXP)]
                 + [(s.ffn_ws1, 14), (s.ffn_wsg, 15)])
        for src_m, slot in amats:
            for r in range(16):
                for hh in range(4):
                    dst = s.s_fab[slot, hh * 4:(hh + 1) * 4, :,
                                  r * 128:(r + 1) * 128].transpose([1, 0, 2])
                    b0_jobs.append((src_m[r * 128:(r + 1) * 128,
                                          hh * 512:(hh + 1) * 512]
                                    .rearrange("p (h c) -> p h c", h=4), dst))
        mats2 = [(s.ffn_w2[e_], e_) for e_ in range(NEXP)] + [(s.ffn_ws2, 7)]
        for src_m, slot in mats2:
            for r in range(16):
                for hh in range(4):
                    dst = s.s_f2b[slot, hh * 4:(hh + 1) * 4, :,
                                  r * 128:(r + 1) * 128].transpose([1, 0, 2])
                    b0_jobs.append((src_m[r * 128:(r + 1) * 128,
                                          hh * 512:(hh + 1) * 512]
                                    .rearrange("p (h c) -> p h c", h=4), dst))
        for r in range(32):
            for hh in range(4):
                b0_jobs.append((s.w_out[r * 128:(r + 1) * 128,
                                        hh * 512:(hh + 1) * 512],
                                s.s_wob2[r // 4, :, r % 4,
                                         hh * 512:(hh + 1) * 512]))
        b0_i = [0]
        import os
        b0_disable = os.environ.get("NO_B0", "0") == "1"

        def b0_emit(n):
            if b0_disable:
                return
            for _ in range(n):
                if b0_i[0] >= len(b0_jobs):
                    return
                i = b0_i[0]
                b0_i[0] += 1
                srcap, dstap = b0_jobs[i]
                bb = (i % 2)
                fsl = b0f[:, bb * 512:(bb + 1) * 512]
                bsl = b0b[:, bb * 512:(bb + 1) * 512]
                w_ = [(b0c, i - 1)] if i >= 2 else []
                S.dma("gpsimd", lambda e, srcap=srcap, fsl=fsl: e.dma_start(
                    fsl if srcap.ndim == 2 else
                    fsl.rearrange("p (h c) -> p h c", h=4), srcap),
                    waits=w_, pipe=b0p)
                eng = "vector" if i % 2 == 0 else "scalar"
                fn = ((lambda e, fsl=fsl, bsl=bsl: e.tensor_copy(bsl, fsl))
                      if eng == "vector" else
                      (lambda e, fsl=fsl, bsl=bsl: e.copy(bsl, fsl)))
                S.step(eng, fn, waits=[b0p.at()], incs=[(b0c, 1)])
                S.dma("gpsimd", lambda e, dstap=dstap, bsl=bsl: e.dma_start(
                    dstap, bsl if dstap.ndim == 2 else
                    bsl.rearrange("p (h c) -> p h c", h=4)),
                    waits=[b0c.at()], pipe=b0p)

        S.dma("sync", lambda e: e.dma_start(
            wba_f.rearrange("p (k n) -> p k n", k=16),
            s.w_ba.rearrange("(k p) n -> p k n", p=128)), pipe=gld)
        S.step("vector", lambda e: e.tensor_copy(wba_b, wba_f),
               waits=[gld.at()], incs=[(gcast, 1)])

        n_units = [0]
        for stile in range(2):
            TB = stile * 2048
            for i in range(16):
                gi = stile * 16 + i
                b = (i % 2) * D
                xhb = xh[:, b:b + D]
                w_free = [(v2, gi - 1)] if gi >= 2 else []
                ldp = ld2[gi % 2]
                S.dma("sync", lambda e, gi=gi, xhb=xhb: e.dma_start(
                    xhb, s.x_in[gi * 128:(gi + 1) * 128, :]),
                    waits=w_free, pipe=ldp)
                S.step("scalar", lambda e, xhb=xhb: e.activation(
                    sq, xhb, AF.Square),
                    waits=[ldp.at(), (v1, gi)], incs=[(sqp, 1)])

                def vstats(e, xhb=xhb):
                    e.tensor_reduce(st_[:, 0:1], xhb, AX.X, ALU.add)
                    e.tensor_reduce(st_[:, 1:2], sq, AX.X, ALU.add)
                    e.tensor_scalar(st_[:, 2:3], st_[:, 0:1], 1.0 / D, None, ALU.mult)
                    e.tensor_scalar(st_[:, 3:4], st_[:, 1:2], 1.0 / D, None, ALU.mult)
                    e.tensor_tensor(st_[:, 4:5], st_[:, 2:3], st_[:, 2:3], ALU.mult)
                    e.tensor_tensor(st_[:, 5:6], st_[:, 3:4], st_[:, 4:5],
                                    ALU.subtract)
                    e.tensor_scalar(st_[:, 5:6], st_[:, 5:6], LN_EPS, None,
                                    ALU.add)
                    return e.tensor_scalar(st_[:, 7:8], st_[:, 2:3], -1.0, None,
                                           ALU.mult)
                S.step("vector", vstats, waits=[ldp.at(), sqp.at()],
                       incs=[(v1, 1)])

                def rstd(e):
                    e.activation(st_[:, 6:7], st_[:, 5:6], AF.Ln)
                    return e.activation(st_[:, 6:7], st_[:, 6:7], AF.Exp,
                                        scale=-0.5)
                S.step("scalar", rstd, waits=[v1.at()], incs=[(sqp, 1)])
                w_tp = [(tp, 256 * gi)] if gi >= 1 else []
                w_stile = [(amm, 384)] if (stile == 1 and i == 0) else []
                S.step("vector", lambda e, xhb=xhb: e.tensor_scalar(
                    xn_tm, xhb, st_[:, 7:8], st_[:, 6:7], ALU.add, ALU.mult),
                    waits=[sqp.at()] + w_tp + w_stile, incs=[(v2, 1)])
                S.dma("sync", lambda e, i=i: [
                    e.dma_start_transpose(xn_t3[:, k, i * 128:(i + 1) * 128],
                                          xn_tm[:, k * 128:(k + 1) * 128])
                    for k in range(16)], waits=[(v2, gi + 1)], pipe=tp, n=16)

            for i in range(16):
                gi = stile * 16 + i
                bank = self.psum[4 + (gi % 2)][:, 0:64]

                def gmmf(e, i=i, bank=bank):
                    for k in range(16):
                        ins = e.matmul(bank, xn_t3[:, k, i * 128:(i + 1) * 128],
                                       wba_b3[:, k, :], start=(k == 0),
                                       stop=(k == 15))
                    return ins
                w = [gcast.at(), (tp, 256 * 16 * (stile + 1))]
                if gi >= 2:
                    w.append((gev, gi - 1))
                S.step("tensor", gmmf, waits=w, incs=[(gmm, 1)])
                S.step("scalar", lambda e, gi=gi, bank=bank: e.copy(
                    s.ba_tm.ap()[:, gi * 64:(gi + 1) * 64], bank),
                    waits=[gmm.at()], incs=[(gev, 1)])

            for cb in range(96):
                is_qk = cb < 32
                is_v = 32 <= cb < 64
                is_z = cb >= 64
                wsl = wst[:, (cb % 2) * 2048:(cb % 2 + 1) * 2048]
                wbl = wbb[:, (cb % 2) * 2048:(cb % 2 + 1) * 2048]
                wbl3 = wbl.rearrange("p (k n) -> p k n", k=16)
                gcb = stile * 96 + cb
                wldp = wld2[gcb % 2]
                w_wfree = [(wcast, gcb - 1)] if gcb >= 2 else []
                if is_z:
                    srcw = s.w_z.rearrange("(k p) n -> p k n", p=128)[
                        :, :, (cb - 64) * 128:(cb - 63) * 128]
                else:
                    srcw = s.w_qkv.rearrange("(k p) n -> p k n", p=128)[
                        :, :, cb * 128:(cb + 1) * 128]
                S.dma("gpsimd", lambda e, srcw=srcw, wsl=wsl: e.dma_start(
                    wsl.rearrange("p (k n) -> p k n", k=16), srcw),
                    waits=w_wfree, pipe=wldp)
                if not is_z:
                    w_cw = ([(acv, 4 * (n_qkv[0] - 1))] if n_qkv[0] >= 2 else [])
                    S.dma("gpsimd", lambda e, cb=cb: e.dma_start(
                        cw[:, (cb % 2) * 4:(cb % 2) * 4 + 4],
                        s.conv_w[cb * 128:(cb + 1) * 128, :]),
                        waits=w_cw, pipe=wldp)
                    n_qkv[0] += 1
                w_cast = [wldp.at()]
                if gcb >= 2:
                    w_cast.append((amm, 4 * (gcb - 1)))
                S.step("vector", lambda e, wsl=wsl, wbl=wbl: e.tensor_copy(
                    wbl, wsl), waits=w_cast, incs=[(wcast, 1)])

                for tt in range(4):
                    u = n_units[0]
                    n_units[0] += 1
                    gtt = stile * 4 + tt
                    bank = self.psum[u % 4]
                    bb = u % 2
                    catl = cat[:, bb * 520:bb * 520 + 515]
                    accl = acc[:, bb * 512:(bb + 1) * 512]
                    sill = sil[:, bb * 512:(bb + 1) * 512]
                    tmstl = tmst[:, bb * 512:(bb + 1) * 512]
                    sqnl = sqn
                    ssql = ssq[:, bb * 8:bb * 8 + 4]
                    nfml = nfm[:, bb * 512:(bb + 1) * 512]

                    def mmf(e, tt=tt, bank=bank, wbl3=wbl3):
                        for k in range(16):
                            ins = e.matmul(bank[:], wbl3[:, k, :],
                                           xn_t3[:, k, tt * 512:(tt + 1) * 512],
                                           start=(k == 0), stop=(k == 15))
                        return ins
                    w = [wcast.at(), (tp, 256 * 16 * (stile + 1)), gev.at()]
                    if u >= 4:
                        w.append((aev, u - 3))
                    S.step("tensor", mmf, waits=w, incs=[(amm, 1)])

                    if not is_z:
                        w_ev = [amm.at()]
                        if ("cv", u % 2) in conv_done:
                            w_ev.append(conv_done[("cv", u % 2)])
                        S.step("scalar", lambda e, catl=catl, bank=bank: e.copy(
                            catl[:, 3:515], bank[:]),
                            waits=w_ev, incs=[(aev, 1)])

                        def convf(e, cb=cb, catl=catl, accl=accl, cwl=cw[
                                :, (cb % 2) * 4:(cb % 2) * 4 + 4]):
                            e.tensor_copy(catl[:, 0:3],
                                          tails[:, cb * 4:cb * 4 + 3])
                            e.tensor_scalar(accl, catl[:, 3:515],
                                            cwl[:, 3:4], None, ALU.mult)
                            e.scalar_tensor_tensor(accl, catl[:, 2:514],
                                                   cwl[:, 2:3], accl, ALU.mult,
                                                   ALU.add)
                            e.scalar_tensor_tensor(accl, catl[:, 1:513],
                                                   cwl[:, 1:2], accl, ALU.mult,
                                                   ALU.add)
                            e.scalar_tensor_tensor(accl, catl[:, 0:512],
                                                   cwl[:, 0:1], accl, ALU.mult,
                                                   ALU.add)
                            return e.tensor_copy(tails[:, cb * 4:cb * 4 + 3],
                                                 catl[:, 512:515])
                        S.step("vector", convf,
                               waits=[aev.at(), (asl, u - 1)], incs=[(acv, 1)])
                        conv_done[("cv", u % 2)] = acv.at()
                        w_sil = [acv.at()]
                        if ("atp", u % 2) in atp_done:
                            w_sil.append(atp_done[("atp", u % 2)])
                        S.step("scalar", lambda e, accl=accl, sill=sill:
                               e.activation(sill, accl, AF.Silu),
                               waits=w_sil, incs=[(asl, 1)])
                    else:
                        w_sil = [amm.at()]
                        if ("atp", u % 2) in atp_done:
                            w_sil.append(atp_done[("atp", u % 2)])
                        S.step("scalar", lambda e, sill=sill, bank=bank:
                               e.activation(sill, bank[:], AF.Silu),
                               waits=w_sil, incs=[(aev, 1), (asl, 1)])

                    w_tmst = []
                    if ("tm", u % 2) in tm_guard:
                        w_tmst.append(tm_guard[("tm", u % 2)])
                    if ("ntp", u % 2) in ntp_done:
                        w_tmst.append(ntp_done[("ntp", u % 2)])
                    S.dma("sync", lambda e, sill=sill, tmstl=tmstl: [
                        e.dma_start_transpose(
                            tmstl[:, ss * 128:(ss + 1) * 128],
                            sill[:, ss * 128:(ss + 1) * 128])
                        for ss in range(4)],
                        waits=w_tmst + [(asl, u + 1)], pipe=atp2[u % 2], n=4)
                    atp_done[("atp", u % 2)] = atp2[u % 2].at()

                    if is_qk:
                        kh = cb % 16
                        m_qk = qk_units[0]
                        qk_units[0] += 1
                        dst_t = None if cb < 16 else s.s_kt
                        dst_f = (s.s_qf if cb < 16 else s.s_kf)

                        def nrm1(e, tmstl=tmstl, sqnl=sqnl):
                            return e.activation(sqnl, tmstl, AF.Square)
                        S.step("scalar", nrm1,
                               waits=[atp_done[("atp", u % 2)], (nr2, m_qk)],
                               incs=[(nrm, 1)])

                        def nrm2(e, sqnl=sqnl, ssql=ssql):
                            e.tensor_reduce(
                                ssql, sqnl.rearrange("p (s c) -> p s c", s=4),
                                AX.X, ALU.add)
                            return e.tensor_scalar(ssql, ssql, L2_EPS, None,
                                                   ALU.add)
                        S.step("vector", nrm2, waits=[nrm.at()],
                               incs=[(nrm, 1), (nr2, 1)])
                        S.step("scalar", lambda e, ssql=ssql: e.activation(
                            ssql, ssql, AF.Sqrt),
                            waits=[nrm.at()], incs=[(nrm, 1)])

                        def nrm4(e, tmstl=tmstl, ssql=ssql):
                            e.reciprocal(ssql, ssql)
                            for ss in range(4):
                                ins = e.tensor_scalar(
                                    tmstl[:, ss * 128:(ss + 1) * 128],
                                    tmstl[:, ss * 128:(ss + 1) * 128],
                                    ssql[:, ss:ss + 1], None, ALU.mult)
                            return ins
                        S.step("vector", nrm4, waits=[nrm.at()], incs=[(nrm, 1)])
                        if dst_t is not None:
                            S.dma("gpsimd", lambda e, kh=kh, gtt=gtt,
                                  tmstl=tmstl,
                                  dst_t=dst_t: e.dma_start(
                                      dst_t[kh, gtt * 4:(gtt + 1) * 4].transpose(
                                          [1, 0, 2]),
                                      tmstl.rearrange("p (s c) -> p s c", s=4)),
                                waits=[nrm.at()], pipe=sp_tm[u % 2])
                            tm_guard[("tm", u % 2)] = sp_tm[u % 2].at()
                        ntpp = ntp2[u % 2]
                        w_ntp = [nrm.at()]
                        if ("fm", u % 2) in fm_guard:
                            w_ntp.append(fm_guard[("fm", u % 2)])
                        S.dma("scalar", lambda e, tmstl=tmstl, nfml=nfml: [
                            e.dma_start_transpose(
                                nfml[:, ss * 128:(ss + 1) * 128],
                                tmstl[:, ss * 128:(ss + 1) * 128])
                            for ss in range(4)],
                            waits=w_ntp, pipe=ntpp, n=4)
                        ntp_done[("ntp", u % 2)] = ntpp.at()
                        S.dma("gpsimd", lambda e, kh=kh, TB=TB, tt=tt,
                              nfml=nfml,
                              dst_f=dst_f: e.dma_start(
                                  dst_f[kh, :, TB + tt * 512:TB + (tt + 1) * 512],
                                  nfml),
                            waits=[ntpp.at()], pipe=sp_fm[u % 2])
                        fm_guard[("fm", u % 2)] = sp_fm[u % 2].at()
                    elif is_v:
                        vh = cb - 32
                        S.dma("gpsimd", lambda e, vh=vh, gtt=gtt, tmstl=tmstl:
                              e.dma_start(
                                  s.s_vt[vh, gtt * 4:(gtt + 1) * 4].transpose(
                                      [1, 0, 2]),
                                  tmstl.rearrange("p (s c) -> p s c", s=4)),
                            waits=[atp_done[("atp", u % 2)]],
                            pipe=sp_tm[u % 2])
                        tm_guard[("tm", u % 2)] = sp_tm[u % 2].at()
                    else:
                        zb = cb - 64
                        S.dma("gpsimd", lambda e, zb=zb, TB=TB, tt=tt,
                              tmstl=tmstl:
                              e.dma_start(
                                  s.s_z[TB + tt * 512:TB + (tt + 1) * 512,
                                        zb * 128:(zb + 1) * 128].rearrange(
                                      "(s p) c -> p s c", p=128),
                                  tmstl.rearrange("p (s c) -> p s c", s=4)),
                            waits=[atp_done[("atp", u % 2)]],
                            pipe=sp_tm[u % 2])
                        tm_guard[("tm", u % 2)] = sp_tm[u % 2].at()
                b0_emit(9)
        b0_emit(10 ** 9)
        self.a1_done_waits = [amm.at(), aev.at(), b0p.at(),
                              sp_tm[0].at(), sp_tm[1].at(),
                              sp_fm[0].at(), sp_fm[1].at()]

    def ph_a2(self):
        s, S = self, self.S
        off = self.a01_end
        lanes, off = self.carve(off, 4 * HV)
        gl_row, off = self.carve(off, HV)
        gjf, off = self.carve(off, 2 * 128)
        self.a2_end = off
        gjf_sb = gjf
        a2v, a2s, a2g, mm, ev, wr, rd = (S.pipe("a2_v"), S.pipe("a2_s"),
                                         S.pipe("a2_g"), S.pipe("a2_mm"),
                                         S.pipe("a2_ev"), S.pipe("a2_wr"),
                                         S.pipe("a2_rd"))
        self.a2_done = S.pipe("a2_done")
        for j in range(NCH):
            h0 = (j % 2) * 2 * HV
            x1 = lanes[:, h0:h0 + HV]
            t1 = lanes[:, h0 + HV:h0 + 2 * HV]
            bsl = s.ba_tm.ap()[:, j * 64:j * 64 + 32]
            asl = s.ba_tm.ap()[:, j * 64 + 32:j * 64 + 64]
            gj = s.gj_s.ap()[:, j * HV:(j + 1) * HV]
            w_lane = [(mm, 2 * j - 3)] if j >= 2 else []
            S.step("vector", lambda e, x1=x1, asl=asl: e.tensor_tensor(
                x1, asl, s.dtb_bc[:], ALU.add),
                waits=[self.g_ev.at(), self.c_done.at()] + w_lane, incs=[(a2v, 1)])

            def sp_chain(e, x1=x1, t1=t1):
                e.activation(t1, x1, AF.Abs)
                e.activation(t1, t1, AF.Exp, scale=-1.0)
                return e.activation(t1, t1, AF.Ln, bias=1.0)
            S.step("scalar", sp_chain, waits=[a2v.at()], incs=[(a2s, 1)])

            def gfin(e, x1=x1, t1=t1):
                e.tensor_scalar(x1, x1, 0.0, None, ALU.max)
                e.tensor_tensor(t1, x1, t1, ALU.add)
                return e.tensor_tensor(t1, t1, s.negea_bc[:], ALU.mult)
            S.step("vector", gfin, waits=[a2s.at()], incs=[(a2g, 1)])

            bank = self.psum[2 + (j % 2)]

            def cum(e, t1=t1, bank=bank):
                e.matmul(bank[:, 0:HV], s.m_triu, t1, start=True, stop=True)
                return e.matmul(bank[0:1, HV:2 * HV], s.ones_col, t1,
                                start=True, stop=True)
            w_bank = [(ev, j - 1)] if j >= 2 else []
            if j == 0:
                w_bank += [self.a1_ev.at(), S.pipe("g_mm").at()]
            S.step("tensor", cum, waits=[a2g.at()] + w_bank, incs=[(mm, 1)])

            def evac(e, gj=gj, bank=bank):
                e.tensor_copy(gj, bank[:, 0:HV])
                return e.tensor_copy(gl_row[0:1, :], bank[0:1, HV:2 * HV])
            S.step("vector", evac, waits=[mm.at(), (wr, 32 * max(0, j - 1))],
                   incs=[(ev, 1)])
            tbank = self.psum[6 + (j % 2)]

            def gjtp(e, gj=gj, tbank=tbank):
                return e.transpose(tbank[0:32, 0:128], gj, s.ident_f)
            S.step("tensor", gjtp, waits=[ev.at(), (wr, 32 * max(0, j - 1))],
                   incs=[(mm, 1)])
            S.step("scalar", lambda e, j=j, tbank=tbank: e.copy(
                gjf_sb[0:32, (j % 2) * 128:(j % 2 + 1) * 128], tbank[0:32, 0:128]),
                waits=[mm.at(), (wr, 32 * max(0, j - 1))], incs=[(ev, 1)])
            S.dma("sync", lambda e, j=j: [
                e.dma_start(s.s_gj[j], gjf_sb[0:32, (j % 2) * 128:(j % 2 + 1) * 128]),
                e.dma_start(s.s_gl[j:j + 1, :], gl_row[0:1, :])],
                waits=[ev.at()], pipe=wr, n=2)
            egl = s.egL_s.ap()[:, j * HV:(j + 1) * HV]
            S.dma("sync", lambda e, j=j, egl=egl: e.dma_start(
                egl, s.s_gl[j:j + 1, :].to_broadcast([128, HV])),
                waits=[wr.at()], pipe=rd)

            def deriv_s(e, j=j, gj=gj, bsl=bsl):
                jj = slice(j * HV, (j + 1) * HV)
                e.activation(s.beta_s.ap()[:, jj], bsl, AF.Sigmoid)
                e.activation(s.egjc_s.ap()[:, jj], gj, AF.Exp)
                return e.activation(s.negsc1_s.ap()[:, jj], gj, AF.Exp)
            S.step("scalar", deriv_s, waits=[ev.at(), rd.at()], incs=[(a2s, 1)])

            def deriv_v(e, j=j, gj=gj, egl=egl):
                jj = slice(j * HV, (j + 1) * HV)
                e.tensor_scalar(s.egjc_s.ap()[:, jj], s.egjc_s.ap()[:, jj],
                                C_SCALE, None, ALU.mult)
                e.tensor_tensor(s.egLmg_s.ap()[:, jj], egl, gj, ALU.subtract)
                return e.scalar_tensor_tensor(
                    s.negsc1_s.ap()[:, jj], s.negsc1_s.ap()[:, jj], -1.0,
                    s.beta_s.ap()[:, jj], ALU.mult, ALU.mult)
            S.step("vector", deriv_v, waits=[a2s.at()], incs=[(a2v, 1)])

            def deriv_s2(e, j=j, egl=egl):
                jj = slice(j * HV, (j + 1) * HV)
                e.activation(s.egLmg_s.ap()[:, jj], s.egLmg_s.ap()[:, jj], AF.Exp)
                return e.activation(egl, egl, AF.Exp)
            S.step("scalar", deriv_s2, waits=[a2v.at()], incs=[(self.a2_done, 1)])


    def ph_a3(self):
        s, S = self, self.S
        off = 0
        qf_t, off = self.carve_b(off, 2 * 2048)
        kf_t, off = self.carve_b(off, 2 * 2048)
        kt_t, off = self.carve_b(off, 2 * 2048)
        vt_t, off = self.carve_b(off, 2 * 4096)
        zt_t, off = self.carve_b(off, 2 * 4096)
        gjbc, off = self.carve(off, 2 * 512)
        A0_l, off = self.carve_b(off, 512)
        A0T_l, off = self.carve_b(off, 512)
        Bc_l, off = self.carve_b(off, 512)
        BTc_l, off = self.carve_b(off, 512)
        Bn_l, off = self.carve_b(off, 512)
        BTn_l, off = self.carve_b(off, 512)
        Xc_l, off = self.carve_b(off, 2 * 512)
        vb_l, off = self.carve_b(off, 512)
        kd_l, off = self.carve_b(off, 512)
        iT_l, off = self.carve_b(off, 512)
        X_l, off = self.carve(off, 512)
        em_l, off = self.carve(off, 512)
        em2_l, off = self.carve(off, 512)
        t1_l, off = self.carve(off, 512)
        xg_l, off = self.carve(off, 512)
        rs_l, off = self.carve(off, 16)
        nm_l, off = self.carve_b(off, 512)
        nfm, off = self.carve_b(off, 2 * 4096)
        att, off = self.carve(off, 2 * D)
        sq2, off = self.carve(off, D)
        xr_l, off = self.carve(off, 2 * D)
        wob_l, off = self.carve_b(off, 2 * 1024)
        assert off <= self.ARENA_W, off

        pt, pv, pa = S.pipe("d_pt"), S.pipe("d_pv"), S.pipe("d_pa")
        ld2 = [S.pipe("d_ld0"), S.pipe("d_ld1")]
        gj2 = [S.pipe("d_gj0"), S.pipe("d_gj1")]
        a0t2 = [S.pipe("d_a0t0"), S.pipe("d_a0t1")]
        ntp2 = [S.pipe("d_ntp0"), S.pipe("d_ntp1")]
        wob2 = [S.pipe("d_wob0"), S.pipe("d_wob1")]
        xld2 = [S.pipe("d_xld0"), S.pipe("d_xld1")]
        y1p2 = [S.pipe("d_y10"), S.pipe("d_y11")]
        PS = self.psum

        bar = list(self.a1_done_waits) + [
            p.at() for nm_, p in S._pipes.items()
            if nm_.startswith("a2_") or nm_ == "g_ev"]
        first = {e: True for e in ENGINES}

        def step(eng, fn, waits=(), par=False):
            w = list(waits)
            if first[eng]:
                w = w + bar
                first[eng] = False
            pp = {"tensor": pt, "vector": pv, "scalar": pa}.get(eng)
            return S.step(eng, fn, w, incs=[(pp, 1)] if pp else [], par=par)

        def dma(eng, fn, waits=(), pipe=None, n=1):
            w = list(waits)
            if first[eng]:
                w = w + bar
                first[eng] = False
            return S.dma(eng, fn, w, pipe=pipe, n=n)

        step("vector", lambda e: [e.memset(s.S_f.ap()[:], 0.0),
                                  e.memset(s.S_b.ap()[:], 0.0)])
        sinit_at = pv.at()

        sb_ready = {}
        ld_done = {}
        gj_done = {}
        a0t_prev = {}
        ntp_prev = {}
        wob_prev = {}
        xld_prev = {}
        y1_prev = {}
        ap_free = {}
        lv_free = {}
        gram_free = None
        ps6_free = sinit_at
        lv0_last = None
        out_last = None
        pss_last = None
        a0t_last = None
        ntp_last = None
        wout_last = None
        att_free = None
        nm_done = {}
        NL = 6
        self.y1_done = []

        for j in range(NCH):
            par = j % 2
            qf = qf_t[:, par * 2048:(par + 1) * 2048]
            kf = kf_t[:, par * 2048:(par + 1) * 2048]
            kt = kt_t[:, par * 2048:(par + 1) * 2048]
            vt = vt_t[:, par * 4096:(par + 1) * 4096]
            zt = zt_t[:, par * 4096:(par + 1) * 4096]
            ldp = ld2[par]
            w_ld = ld_done.get(j - 2, [])

            def loads(e, j=j, qf=qf, kf=kf, kt=kt, vt=vt, zt=zt):
                return [
                    e.dma_start(qf.rearrange("p (h c) -> p h c", h=16),
                                s.s_qf[:, :, j * 128:(j + 1) * 128]
                                .transpose([1, 0, 2])),
                    e.dma_start(kf.rearrange("p (h c) -> p h c", h=16),
                                s.s_kf[:, :, j * 128:(j + 1) * 128]
                                .transpose([1, 0, 2])),
                    e.dma_start(kt.rearrange("p (h c) -> p h c", h=16),
                                s.s_kt[:, j].transpose([1, 0, 2])),
                    e.dma_start(vt.rearrange("p (h c) -> p h c", h=32),
                                s.s_vt[:, j].transpose([1, 0, 2])),
                    e.dma_start(zt, s.s_z[j * 128:(j + 1) * 128, :])]
            dma("sync", loads, waits=w_ld, pipe=ldp, n=5)
            ld_at = ldp.at()

            for g in range(8):
                heads = [4 * g + i for i in range(4)]
                pairs = [2 * g, 2 * g + 1]
                gi = j * 8 + g
                gpar = gi % 2
                gjb = gjbc[:, gpar * 512:(gpar + 1) * 512]
                gjp = gj2[gpar]
                w_gj = [gj_done[gi - 2]] if (gi - 2) in gj_done else []
                if (gi - 2) in gj_done:
                    pass

                def gjload(e, j=j, g=g, gjb=gjb):
                    return [e.dma_start(
                        gjb[:, i * 128:(i + 1) * 128],
                        s.s_gj[j, 4 * g + i:4 * g + i + 1, :]
                        .to_broadcast([128, 128]))
                        for i in range(4)]
                dma("gpsimd", gjload, waits=w_gj, pipe=gjp, n=4)
                gj_at = gjp.at()

                def grams(e, kf=kf, qf=qf, pairs=pairs):
                    for i, p in enumerate(pairs):
                        kfp = kf[:, p * 128:(p + 1) * 128]
                        qfp = qf[:, p * 128:(p + 1) * 128]
                        e.matmul(PS[0][:, i * 256:i * 256 + 128], kfp, kfp,
                                 start=True, stop=True)
                        ins = e.matmul(PS[0][:, i * 256 + 128:i * 256 + 256],
                                       kfp, qfp, start=True, stop=True)
                    return ins
                w_gram = [ld_at]
                if gram_free is not None:
                    w_gram.append(gram_free)
                step("tensor", grams, waits=w_gram)
                gram_at = pt.at()

                def prepX(e, j=j, heads=heads, gjb=gjb):
                    for i, h in enumerate(heads):
                        ins = e.tensor_scalar(X_l[:, i * 128:(i + 1) * 128],
                                              gjb[:, i * 128:(i + 1) * 128],
                                              s.pp(s.gj_s, j, h), None,
                                              ALU.subtract)
                    return ins
                S.step("vector", prepX, [gj_at], incs=[(pv, 1)], par=True)

                def prepM(e):
                    for i in range(4):
                        e.tensor_tensor(em_l[:, i * 128:(i + 1) * 128],
                                        X_l[:, i * 128:(i + 1) * 128],
                                        s.m_sl_big, ALU.add)
                        ins = e.tensor_tensor(em2_l[:, i * 128:(i + 1) * 128],
                                              X_l[:, i * 128:(i + 1) * 128],
                                              s.m_u_negbig, ALU.add)
                    return ins
                S.step("vector", prepM, [], incs=[(pv, 1)], par=True)
                prep_at = pv.at()
                gj_done[gi] = (pv, pv.n)

                def exps(e):
                    e.activation(em_l, em_l, AF.Exp, scale=-1.0)
                    return e.activation(em2_l, em2_l, AF.Exp)
                step("scalar", exps, par=True, waits=[prep_at])
                exps_at = pa.at()

                def a0build(e, j=j, heads=heads):
                    for i, h in enumerate(heads):
                        pq = i // 2
                        e.scalar_tensor_tensor(
                            A0_l[:, i * 128:(i + 1) * 128],
                            PS[0][:, pq * 256:pq * 256 + 128],
                            s.pp(s.beta_s, j, h),
                            em_l[:, i * 128:(i + 1) * 128], ALU.mult, ALU.mult)
                        ins = e.tensor_tensor(
                            iT_l[:, i * 128:(i + 1) * 128],
                            PS[0][:, pq * 256 + 128:pq * 256 + 256],
                            em2_l[:, i * 128:(i + 1) * 128], ALU.mult)
                    return ins
                w_a0 = [exps_at, gram_at]
                if a0t_last is not None:
                    w_a0.append(a0t_last)
                if lv0_last is not None:
                    w_a0.append(lv0_last)
                if out_last is not None:
                    w_a0.append(out_last)
                step("vector", a0build, par=True, waits=w_a0)
                a0_at = pv.at()
                gram_free = a0_at

                a0tp = a0t2[gpar]
                w_a0t = [a0_at]
                if lv0_last is not None:
                    w_a0t.append(lv0_last)
                if gpar in a0t_prev:
                    w_a0t.append(a0t_prev[gpar])

                def a0tpf(e):
                    return [e.dma_start_transpose(
                        A0T_l[:, i * 128:(i + 1) * 128],
                        A0_l[:, i * 128:(i + 1) * 128]) for i in range(4)]
                dma("sync", a0tpf, waits=w_a0t, pipe=a0tp, n=4)
                a0t_prev[gpar] = a0tp.at()
                a0t_last = a0tp.at()
                a0t_at = a0tp.at()

                def vbf(e, j=j, heads=heads, vt=vt):
                    for i, h in enumerate(heads):
                        ins = e.tensor_scalar(
                            vb_l[:, i * 128:(i + 1) * 128],
                            vt[:, h * 128:(h + 1) * 128],
                            s.pp(s.beta_s, j, h), None, ALU.mult)
                    return ins
                step("vector", vbf, par=True, waits=[ld_at])
                vb_at = pv.at()

                def psR(e, kf=kf, heads=heads, pairs=pairs):
                    for i, h in enumerate(heads):
                        p = pairs[i // 2]
                        ins = e.matmul(
                            PS[6][:, i * 128:(i + 1) * 128],
                            kf[:, p * 128:(p + 1) * 128],
                            s.S_b.ap()[:, h * 128:(h + 1) * 128],
                            start=True, stop=True)
                    return ins
                w_psr = [ld_at, ps6_free]
                if (j - 1, g) in sb_ready:
                    w_psr.append(sb_ready[(j - 1, g)])
                else:
                    w_psr.append(sinit_at)
                if ap_free.get(1) is not None:
                    w_psr.append(ap_free[1])
                step("tensor", psR, waits=w_psr)
                psr_at = pt.at()

                def xc0(e, j=j, heads=heads):
                    for i, h in enumerate(heads):
                        ins = e.scalar_tensor_tensor(
                            Xc_l[:, i * 128:(i + 1) * 128],
                            PS[6][:, i * 128:(i + 1) * 128],
                            s.pp(s.negsc1_s, j, h),
                            vb_l[:, i * 128:(i + 1) * 128], ALU.mult, ALU.add)
                    return ins
                step("vector", xc0, par=True, waits=[psr_at, vb_at])
                xc_at = pv.at()
                ps6_free = xc_at

                cur_B, cur_BT = A0_l, A0T_l
                cur_ready = a0t_at
                nxt = [(Bn_l, BTn_l), (Bc_l, BTc_l)]
                for k in range(7):
                    apb = PS[5 + (k % 2)]
                    xin = Xc_l[:, (k % 2) * 512:(k % 2) * 512 + 512]
                    xout = Xc_l[:, ((k + 1) % 2) * 512:((k + 1) % 2) * 512 + 512]

                    def apf(e, cur_BT=cur_BT, xin=xin, apb=apb):
                        for i in range(4):
                            ins = e.matmul(apb[:, i * 128:(i + 1) * 128],
                                           cur_BT[:, i * 128:(i + 1) * 128],
                                           xin[:, i * 128:(i + 1) * 128],
                                           start=True, stop=True)
                        return ins
                    w_ap = [cur_ready, xc_at]
                    if ap_free.get(k % 2) is not None:
                        w_ap.append(ap_free[k % 2])
                    if k % 2 == 1:
                        w_ap.append(ps6_free)
                    step("tensor", apf, waits=w_ap)
                    ap_at = pt.at()

                    def apv(e, xin=xin, xout=xout, apb=apb, k=k):
                        return e.tensor_tensor(
                            xout, xin, apb[:],
                            ALU.subtract if k == 0 else ALU.add)
                    step("vector", apv, waits=[ap_at])
                    xc_at = pv.at()
                    ap_free[k % 2] = xc_at
                    if k % 2 == 1:
                        ps6_free = xc_at

                    if k < NL:
                        lvb = (1 + (k % 2) * 2, 2 + (k % 2) * 2)
                        need_B = (k + 1) < NL
                        Bnxt, BTnxt = nxt[k % 2]

                        def lvf(e, cur_B=cur_B, cur_BT=cur_BT, lvb=lvb,
                                need_B=need_B):
                            for i in range(4):
                                if need_B:
                                    e.matmul(
                                        PS[lvb[0]][:, i * 128:(i + 1) * 128],
                                        cur_BT[:, i * 128:(i + 1) * 128],
                                        cur_B[:, i * 128:(i + 1) * 128],
                                        start=True, stop=True)
                                ins = e.matmul(
                                    PS[lvb[1]][:, i * 128:(i + 1) * 128],
                                    cur_B[:, i * 128:(i + 1) * 128],
                                    cur_BT[:, i * 128:(i + 1) * 128],
                                    start=True, stop=True)
                            return ins
                        w_lv = [cur_ready]
                        if lv_free.get(k % 2) is not None:
                            w_lv.append(lv_free[k % 2])
                        step("tensor", lvf, waits=w_lv)
                        lv_at = pt.at()
                        if k == 0:
                            lv0_last = lv_at

                        def lvev(e, lvb=lvb, Bnxt=Bnxt, BTnxt=BTnxt,
                                 need_B=need_B):
                            if need_B:
                                e.copy(Bnxt, PS[lvb[0]][:])
                            return e.copy(BTnxt, PS[lvb[1]][:])
                        step("scalar", lvev, par=True, waits=[lv_at])
                        lvev_at = pa.at()
                        lv_free[k % 2] = lvev_at
                        cur_B, cur_BT = Bnxt, BTnxt
                        cur_ready = lvev_at
                v_new = Xc_l[:, 512:1024]
                vnew_at = xc_at

                def qSf(e, qf=qf, heads=heads, pairs=pairs):
                    for i, h in enumerate(heads):
                        p = pairs[i // 2]
                        ins = e.matmul(
                            PS[6][:, i * 128:(i + 1) * 128],
                            qf[:, p * 128:(p + 1) * 128],
                            s.S_b.ap()[:, h * 128:(h + 1) * 128],
                            start=True, stop=True)
                    return ins
                w_qs = [ps6_free, ap_free[1]]
                if (j - 1, g) in sb_ready:
                    w_qs.append(sb_ready[(j - 1, g)])
                else:
                    w_qs.append(sinit_at)
                step("tensor", qSf, waits=w_qs)
                qs_at = pt.at()

                def t1f(e, j=j, heads=heads):
                    for i, h in enumerate(heads):
                        ins = e.tensor_scalar(
                            t1_l[:, i * 128:(i + 1) * 128],
                            PS[6][:, i * 128:(i + 1) * 128],
                            s.pp(s.egjc_s, j, h), None, ALU.mult)
                    return ins
                step("vector", t1f, par=True, waits=[qs_at])
                t1_at = pv.at()

                def outf(e, v_new=v_new):
                    for i in range(4):
                        ins = e.matmul(PS[6][:, i * 128:(i + 1) * 128],
                                       iT_l[:, i * 128:(i + 1) * 128],
                                       v_new[:, i * 128:(i + 1) * 128],
                                       start=True, stop=True)
                    return ins
                step("tensor", outf, waits=[t1_at, vnew_at, a0_at])
                out_at = pt.at()
                out_last = out_at

                def kdf(e, j=j, heads=heads, kt=kt, pairs=pairs):
                    for i, h in enumerate(heads):
                        p = pairs[i // 2]
                        ins = e.tensor_scalar(
                            kd_l[:, i * 128:(i + 1) * 128],
                            kt[:, p * 128:(p + 1) * 128],
                            s.pp(s.egLmg_s, j, h), None, ALU.mult)
                    return ins
                w_kd = [ld_at]
                if pss_last is not None:
                    w_kd.append(pss_last)
                step("vector", kdf, par=True, waits=w_kd)
                kd_at = pv.at()

                def psSf(e, v_new=v_new):
                    for i in range(4):
                        ins = e.matmul(PS[0][:, i * 128:(i + 1) * 128],
                                       kd_l[:, i * 128:(i + 1) * 128],
                                       v_new[:, i * 128:(i + 1) * 128],
                                       start=True, stop=True)
                    return ins
                step("tensor", psSf, waits=[kd_at, vnew_at, gram_free])
                pss_at = pt.at()
                pss_last = pss_at

                def supd1(e, j=j, heads=heads):
                    for i, h in enumerate(heads):
                        ins = e.scalar_tensor_tensor(
                            s.S_f.ap()[:, h * 128:(h + 1) * 128],
                            s.S_f.ap()[:, h * 128:(h + 1) * 128],
                            s.pp(s.egL_s, j, h),
                            PS[0][:, i * 128:(i + 1) * 128], ALU.mult, ALU.add)
                    return ins
                S.step("vector", supd1, [pss_at], incs=[(pv, 1)], par=True)

                def supd(e, heads=heads):
                    return e.tensor_copy(
                        s.S_b.ap()[:, heads[0] * 128:(heads[0] + 4) * 128],
                        s.S_f.ap()[:, heads[0] * 128:(heads[0] + 4) * 128])
                step("vector", supd, waits=[])
                sb_ready[(j, g)] = pv.at()
                gram_free = pv.at()

                def xgf1(e):
                    for i in range(4):
                        ins = e.scalar_tensor_tensor(
                            t1_l[:, i * 128:(i + 1) * 128],
                            PS[6][:, i * 128:(i + 1) * 128],
                            C_SCALE,
                            t1_l[:, i * 128:(i + 1) * 128], ALU.mult, ALU.add)
                    return ins
                S.step("vector", xgf1, [out_at, t1_at], incs=[(pv, 1)], par=True)

                def xgf(e, g=g, zt=zt):
                    return e.tensor_tensor(
                        xg_l, t1_l, zt[:, g * 512:(g + 1) * 512], ALU.mult)
                step("vector", xgf, waits=[])
                xg_at = pv.at()
                ps6_free = xg_at

                step("scalar", lambda e: e.activation(em_l, xg_l, AF.Square),
                     waits=[xg_at])
                sq_at = pa.at()

                def redf(e):
                    e.tensor_reduce(rs_l[:, 0:4],
                                    em_l.rearrange("p (s c) -> p s c", s=4),
                                    AX.X, ALU.add)
                    return e.tensor_scalar(rs_l[:, 0:4], rs_l[:, 0:4],
                                           RMS_EPS * UV, None, ALU.add)
                step("vector", redf, waits=[sq_at])
                red_at = pv.at()

                def rsqf(e):
                    e.activation(rs_l[:, 4:8], rs_l[:, 0:4], AF.Ln,
                                 scale=1.0 / UV)
                    return e.activation(rs_l[:, 4:8], rs_l[:, 4:8], AF.Exp,
                                        scale=-0.5)
                step("scalar", rsqf, waits=[red_at])
                rsq_at = pa.at()

                def nrmf(e):
                    for i in range(4):
                        ins = e.scalar_tensor_tensor(
                            nm_l[:, i * 128:(i + 1) * 128],
                            xg_l[:, i * 128:(i + 1) * 128],
                            rs_l[:, 4 + i:5 + i],
                            s.normw_bc[:], ALU.mult, ALU.mult)
                    return ins
                w_nrm = [rsq_at]
                if ntp_last is not None:
                    w_nrm.append(ntp_last)
                step("vector", nrmf, par=True, waits=w_nrm)
                nrm_at = pv.at()

                ntpp = ntp2[gpar]
                w_ntp = [nrm_at]
                if gpar in ntp_prev:
                    w_ntp.append(ntp_prev[gpar])
                if wout_last is not None:
                    w_ntp.append(wout_last)
                half = j % 2

                def ntpf(e, g=g, half=half):
                    return [e.dma_start_transpose(
                        nfm[:, half * 4096 + (4 * g + i) * 128:
                            half * 4096 + (4 * g + i + 1) * 128],
                        nm_l[:, i * 128:(i + 1) * 128]) for i in range(4)]
                dma("sync", ntpf, waits=w_ntp, pipe=ntpp, n=4)
                ntp_prev[gpar] = ntpp.at()
                ntp_last = ntpp.at()

            ld_done[j] = [(pt, pt.n), (pv, pv.n)]
            nm_done[j] = (ntp2[0].at(), ntp2[1].at())

            if j % 2 == 1:
                m2 = j // 2
                w_nfm = [nm_done[j - 1][0], nm_done[j - 1][1],
                         nm_done[j][0], nm_done[j][1]]
                ev_at = None
                for ds in range(8):
                    for vq in range(8):
                        wpar = vq % 2
                        wobp = wob2[wpar]
                        wsl = wob_l[:, wpar * 1024:(wpar + 1) * 1024]
                        w_wob = []
                        if (m2 * 64 + ds * 8 + vq) >= 2:
                            w_wob.append(wob_prev[wpar])

                        def wobl(e, ds=ds, vq=vq, wsl=wsl):
                            return e.dma_start(
                                wsl.rearrange("p (v c) -> p v c", v=4),
                                s.s_wob2[vq, :, :, ds * 256:(ds + 1) * 256])
                        dma("sync", wobl, waits=w_wob, pipe=wobp)
                        wob_at = wobp.at()

                        def wmm(e, ds=ds, vq=vq, wsl=wsl):
                            tgb = [PS[7], PS[1]]
                            for i in range(4):
                                for tg in range(2):
                                    ins = e.matmul(
                                        tgb[tg][:, 0:256],
                                        nfm[:, tg * 4096 +
                                            (vq * 4 + i) * 128:
                                            tg * 4096 + (vq * 4 + i + 1) * 128],
                                        wsl[:, i * 256:(i + 1) * 256],
                                        start=(vq == 0 and i == 0),
                                        stop=(vq == 7 and i == 3))
                            return ins
                        w_mm = [wob_at] + w_nfm
                        if vq == 0 and ev_at is not None:
                            w_mm.append(ev_at)
                        if vq == 0 and att_free is not None:
                            w_mm.append(att_free)
                        if ds == 0 and vq == 0:
                            w_mm += [v for v in (lv_free.get(0), lv_free.get(1))
                                     if v is not None]
                        step("tensor", wmm, waits=w_mm)
                        wob_prev[wpar] = (pt, pt.n)
                    wmm_at = pt.at()

                    def wev(e, ds=ds):
                        e.copy(att[:, 0 * 2048 + ds * 256:
                                   0 * 2048 + (ds + 1) * 256],
                               PS[7][:, 0:256])
                        return e.copy(att[:, 1 * 2048 + ds * 256:
                                          1 * 2048 + (ds + 1) * 256],
                                      PS[1][:, 0:256])
                    step("scalar", wev, waits=[wmm_at])
                    ev_at = pa.at()
                wout_last = pt.at()
                lv_free[0] = ev_at

                for tg in range(2):
                    grp = 2 * m2 + tg
                    xpar = grp % 2
                    xlp = xld2[xpar]
                    xrl = xr_l[:, xpar * D:(xpar + 1) * D]
                    y1l = xrl
                    w_x = []
                    if xpar in y1_prev:
                        w_x.append(y1_prev[xpar])

                    def xload(e, grp=grp, xrl=xrl):
                        return e.dma_start(
                            xrl, s.x_in[grp * 128:(grp + 1) * 128, :])
                    dma("sync", xload, waits=w_x, pipe=xlp)
                    x_at = xlp.at()
                    atg = att[:, tg * 2048:(tg + 1) * 2048]

                    def lnsq(e, atg=atg):
                        return e.activation(sq2, atg, AF.Square)
                    step("scalar", lnsq, waits=[ev_at])
                    lsq_at = pa.at()

                    def lnst(e, atg=atg):
                        e.tensor_reduce(rs_l[:, 8:9], atg, AX.X, ALU.add)
                        e.tensor_reduce(rs_l[:, 9:10], sq2, AX.X, ALU.add)
                        e.tensor_scalar(rs_l[:, 10:11], rs_l[:, 8:9], 1.0 / D,
                                        None, ALU.mult)
                        e.tensor_scalar(rs_l[:, 11:12], rs_l[:, 9:10], 1.0 / D,
                                        None, ALU.mult)
                        e.tensor_tensor(rs_l[:, 12:13], rs_l[:, 10:11],
                                        rs_l[:, 10:11], ALU.mult)
                        e.tensor_tensor(rs_l[:, 11:12], rs_l[:, 11:12],
                                        rs_l[:, 12:13], ALU.subtract)
                        e.tensor_scalar(rs_l[:, 11:12], rs_l[:, 11:12], LN_EPS,
                                        None, ALU.add)
                        return e.tensor_scalar(rs_l[:, 10:11], rs_l[:, 10:11],
                                               -1.0, None, ALU.mult)
                    step("vector", lnst, waits=[lsq_at])
                    lst_at = pv.at()

                    def lnrs(e):
                        e.activation(rs_l[:, 11:12], rs_l[:, 11:12], AF.Ln)
                        return e.activation(rs_l[:, 11:12], rs_l[:, 11:12],
                                            AF.Exp, scale=-0.5)
                    step("scalar", lnrs, waits=[lst_at])
                    lrs_at = pa.at()

                    def lnap(e, atg=atg, xrl=xrl):
                        e.tensor_scalar(atg, atg, rs_l[:, 10:11],
                                        rs_l[:, 11:12], ALU.add, ALU.mult)
                        return e.tensor_tensor(xrl, atg, xrl, ALU.add)
                    w_lap = [lrs_at, x_at]
                    if xpar in y1_prev:
                        w_lap.append(y1_prev[xpar])
                    step("vector", lnap, waits=w_lap)
                    lap_at = pv.at()
                    if tg == 1:
                        att_free = lap_at

                    def y1st(e, grp=grp, y1l=y1l):
                        return e.dma_start(
                            s.s_y1[grp * 128:(grp + 1) * 128, :], y1l)
                    dma("sync", y1st, waits=[lap_at], pipe=y1p2[xpar])
                    y1_prev[xpar] = y1p2[xpar].at()
                    self.y1_done.append(y1p2[xpar].at())

    def ph_b(self):
        s, S = self, self.S
        off = 0
        xn2, off = self.carve_b(off, 16 * 1024)
        oacc, off = self.carve_b(off, 16 * 1024)
        pt_off = off
        Pt, off = self.carve_b(off, 16 * 1024)
        sq2 = self.arena.ap()[:, pt_off:pt_off + D]
        y1t, off = self.carve(off, 2 * D)
        x2tm, off = self.carve_b(off, D)
        st3, off = self.carve(off, 32)
        w1c, off = self.carve_b(off, 2 * 2048)
        wgc, off = self.carve_b(off, 2 * 2048)
        w2c, off = self.carve_b(off, 2 * 2048)
        wbc, off = self.carve_b(off, 2 * 1024)
        gw_f, off = self.carve(off, 128)
        gw_b, off = self.carve_b(off, 128)
        gl_t, off = self.carve(off, 10 * 8)
        l9b, off = self.carve_b(off, 8)
        t1b, off = self.carve_b(off, 512)
        gbb, off = self.carve_b(off, 512)
        assert off <= self.ARENA_W, off
        xn2_3 = xn2.rearrange("p (k t) -> p k t", k=16)
        oacc3 = oacc.rearrange("p (k t) -> p k t", k=16)
        Pt3 = Pt.rearrange("p (k t) -> p k t", k=16)

        pt, pv, pa = S.pipe("b_pt"), S.pipe("b_pv"), S.pipe("b_pa")
        yld2 = [S.pipe("b_yld0"), S.pipe("b_yld1")]
        xtp = S.pipe("b_xtp")
        wld2 = [S.pipe("b_wld0"), S.pipe("b_wld1")]
        w2ld2 = [S.pipe("b_w2ld0"), S.pipe("b_w2ld1")]
        wgl = S.pipe("b_wgl")
        wwr = S.pipe("b_wwr")
        wbc2p = [S.pipe("b_wbc0"), S.pipe("b_wbc1")]
        yst2 = [S.pipe("b_yst0"), S.pipe("b_yst1")]
        PS = self.psum

        bar = [p.at() for p in S._pipes.values() if p.n > 0]
        first = {e: True for e in ENGINES}

        def step(eng, fn, waits=(), par=False):
            w = list(waits)
            if first[eng]:
                w = w + bar
                first[eng] = False
            pp = {"tensor": pt, "vector": pv, "scalar": pa}.get(eng)
            return S.step(eng, fn, w, incs=[(pp, 1)] if pp else [], par=par)

        def dma(eng, fn, waits=(), pipe=None, n=1):
            w = list(waits)
            if first[eng]:
                w = w + bar
                first[eng] = False
            return S.dma(eng, fn, w, pipe=pipe, n=n)

        dma("sync", lambda e: e.dma_start(
            gw_f.rearrange("p (k n) -> p k n", k=16),
            s.gate_w.rearrange("(k p) n -> p k n", p=128)), pipe=wgl)
        step("vector", lambda e: e.tensor_copy(gw_b, gw_f), waits=[wgl.at()])
        gwc_at = pv.at()
        gw_b3 = gw_b.rearrange("p (k n) -> p k n", k=16)

        y1_prev = {}
        wld_prev = {}
        w2_prev = {}
        wbc_prev = {}
        yst_prev = {}
        xtp_last = None
        pe_xn_last = None
        oacc_free = None

        for t in range(NFT):
            for gg in range(8):
                grp = t * 8 + gg
                par = grp % 2
                ylp = yld2[par]
                yl = y1t[:, par * D:(par + 1) * D]
                w_y = [y1_prev[par]] if par in y1_prev else []

                def yload(e, grp=grp, yl=yl):
                    return e.dma_start(yl, s.s_y1[grp * 128:(grp + 1) * 128, :])
                dma("sync", yload, waits=w_y, pipe=ylp)
                y_at = ylp.at()

                w_sq = [y_at]
                if pa.n:
                    w_sq.append((pa, pa.n))
                if pe_xn_last is not None:
                    w_sq.append(pe_xn_last)
                step("scalar", lambda e, yl=yl: e.activation(sq2, yl, AF.Square),
                     waits=w_sq)
                sq_at = pa.at()

                def lnst(e, yl=yl):
                    e.tensor_reduce(st3[:, 0:1], yl, AX.X, ALU.add)
                    e.tensor_reduce(st3[:, 1:2], sq2, AX.X, ALU.add)
                    e.tensor_scalar(st3[:, 2:3], st3[:, 0:1], 1.0 / D, None,
                                    ALU.mult)
                    e.tensor_scalar(st3[:, 3:4], st3[:, 1:2], 1.0 / D, None,
                                    ALU.mult)
                    e.tensor_tensor(st3[:, 4:5], st3[:, 2:3], st3[:, 2:3],
                                    ALU.mult)
                    e.tensor_tensor(st3[:, 5:6], st3[:, 3:4], st3[:, 4:5],
                                    ALU.subtract)
                    e.tensor_scalar(st3[:, 5:6], st3[:, 5:6], LN_EPS, None,
                                    ALU.add)
                    return e.tensor_scalar(st3[:, 6:7], st3[:, 2:3], -1.0, None,
                                           ALU.mult)
                step("vector", lnst, waits=[sq_at])
                lst_at = pv.at()

                def lnrs(e):
                    e.activation(st3[:, 5:6], st3[:, 5:6], AF.Ln)
                    return e.activation(st3[:, 5:6], st3[:, 5:6], AF.Exp,
                                        scale=-0.5)
                step("scalar", lnrs, waits=[lst_at])
                lrs_at = pa.at()

                w_ap = [lrs_at]
                if xtp_last is not None:
                    w_ap.append(xtp_last)
                step("vector", lambda e, yl=yl: e.tensor_scalar(
                    x2tm, yl, st3[:, 6:7], st3[:, 5:6], ALU.add, ALU.mult),
                    waits=w_ap)
                ap_at = pv.at()

                w_tp = [ap_at]
                if gg == 0 and pe_xn_last is not None:
                    w_tp.append(pe_xn_last)

                def xtpf(e, gg=gg):
                    return [e.dma_start_transpose(
                        xn2_3[:, k, gg * 128:(gg + 1) * 128],
                        x2tm[:, k * 128:(k + 1) * 128]) for k in range(16)]
                dma("sync", xtpf, waits=w_tp, pipe=xtp, n=16)
                xtp_last = xtp.at()

                def gmm(e, gg=gg):
                    for k in range(16):
                        ins = e.matmul(PS[6][:, 0:8],
                                       xn2_3[:, k, gg * 128:(gg + 1) * 128],
                                       gw_b3[:, k, :], start=(k == 0),
                                       stop=(k == 15))
                    return ins
                w_gmm = [xtp_last, gwc_at]
                if pv.n:
                    w_gmm.append((pv, pv.n))
                step("tensor", gmm, waits=w_gmm)
                gmm_at = pt.at()

                L = lambda i: gl_t[:, i * 8:(i + 1) * 8]

                def gch1(e):
                    e.tensor_tensor(L(0), PS[6][:, 0:8], s.gateb_bc[:], ALU.add)
                    e.tensor_reduce(L(1)[:, 0:1], L(0), AX.X, ALU.max)
                    return e.tensor_scalar(L(1)[:, 1:2], L(1)[:, 0:1], -1.0,
                                           None, ALU.mult)
                step("vector", gch1, waits=[gmm_at, (pa, pa.n)])
                g1_at = pv.at()
                step("scalar", lambda e: e.activation(
                    L(2), L(0), AF.Exp, bias=L(1)[:, 1:2]), waits=[g1_at])
                g2_at = pa.at()

                def gch2(e):
                    e.tensor_reduce(L(1)[:, 2:3], L(2), AX.X, ALU.add)
                    e.reciprocal(L(1)[:, 3:4], L(1)[:, 2:3])
                    e.tensor_scalar(L(3), L(2), L(1)[:, 3:4], None, ALU.mult)
                    e.tensor_tensor(L(4), L(3), s.biases_bc[:], ALU.add)
                    e.tensor_reduce(L(1)[:, 4:5], L(4), AX.X, ALU.max)
                    e.tensor_scalar(L(5), L(4), L(1)[:, 4:5], None, ALU.is_ge)
                    e.scalar_tensor_tensor(L(6), L(5), -BIG, L(4), ALU.mult,
                                           ALU.add)
                    e.tensor_reduce(L(1)[:, 5:6], L(6), AX.X, ALU.max)
                    e.tensor_scalar(L(7), L(6), L(1)[:, 5:6], None, ALU.is_ge)
                    e.scalar_tensor_tensor(L(6), L(7), -BIG, L(6), ALU.mult,
                                           ALU.add)
                    e.tensor_reduce(L(1)[:, 6:7], L(6), AX.X, ALU.max)
                    e.tensor_scalar(L(8), L(4), L(1)[:, 6:7], None, ALU.is_ge)
                    e.tensor_tensor(L(9), L(8), L(3), ALU.mult)
                    e.tensor_reduce(L(1)[:, 7:8], L(9), AX.X, ALU.add)
                    e.reciprocal(L(1)[:, 7:8], L(1)[:, 7:8])
                    e.tensor_scalar(L(9), L(9), L(1)[:, 7:8], None, ALU.mult)
                    return e.tensor_copy(l9b, L(9))
                step("vector", gch2, waits=[g2_at, wwr.at()])
                g3_at = pv.at()

                def wwrite(e, t=t, grp=grp, gg=gg):
                    return [e.dma_start(
                        s.s_wg8[t, ee, gg * 128:(gg + 1) * 128],
                        l9b[:, ee:ee + 1]) for ee in range(8)]
                dma("sync", wwrite, waits=[g3_at], pipe=wwr, n=8)
                y1_prev[par] = wwr.at()
            ln_done = [(pv, pv.n), (pa, pa.n), xtp_last, wwr.at()]

            for e_ in range(8):
                shared = (e_ == 0)
                wpar = e_ % 2
                wsl = wbc[:, wpar * 1024:(wpar + 1) * 1024]
                if not shared:
                    w_wbc = [wwr.at()]
                    if wbc_prev.get(wpar) is not None:
                        w_wbc.append(wbc_prev[wpar])

                    def wbcl(e, t=t, e_=e_, wsl=wsl):
                        return e.dma_start(
                            wsl, s.s_wg8[t, e_ - 1:e_, :].to_broadcast(
                                [128, 1024]))
                    dma("gpsimd", wbcl, waits=w_wbc, pipe=wbc2p[wpar])
                    wbc_at = wbc2p[wpar].at()
                else:
                    wbc_at = None

                slots = ((14, 15, 7) if shared
                         else (e_ - 1, NEXP + e_ - 1, e_ - 1))
                for hb in range(16):
                    u = e_ * 16 + hb
                    lpar = u % 2
                    w1l = w1c[:, lpar * 2048:(lpar + 1) * 2048]
                    wgll = wgc[:, lpar * 2048:(lpar + 1) * 2048]
                    wlp = wld2[lpar]
                    w_wl = [wld_prev[lpar]] if lpar in wld_prev else []

                    def wload(e, slots=slots, hb=hb, w1l=w1l, wgll=wgll):
                        return [
                            e.dma_start(w1l, s.s_fab[slots[0], hb]),
                            e.dma_start(wgll, s.s_fab[slots[1], hb])]
                    dma("sync", wload, waits=w_wl, pipe=wlp, n=2)
                    wl_at = wlp.at()

                    w1l3 = w1l.rearrange("p (k n) -> p k n", k=16)
                    wgl3 = wgll.rearrange("p (k n) -> p k n", k=16)

                    def mmAG(e, w1l3=w1l3, wgl3=wgl3):
                        for half in range(2):
                            for k in range(16):
                                e.matmul(PS[0 + half][:],
                                         w1l3[:, k, :],
                                         xn2_3[:, k, half * 512:(half + 1) * 512],
                                         start=(k == 0), stop=(k == 15))
                        for half in range(2):
                            for k in range(16):
                                ins = e.matmul(
                                    PS[2 + half][:],
                                    wgl3[:, k, :],
                                    xn2_3[:, k, half * 512:(half + 1) * 512],
                                    start=(k == 0), stop=(k == 15))
                        return ins
                    w_mm = [wl_at, ln_done[2]]
                    if u >= 1:
                        w_mm.append(pbuild_at)
                    if u == 0:
                        w_mm += ln_done[:2]
                    step("tensor", mmAG, waits=w_mm)
                    mm_at = pt.at()
                    wld_prev[lpar] = (pt, pt.n)

                    def pbuild(e, hb=hb, wsl=wsl, shared=shared):
                        for half in range(2):
                            e.activation(t1b, PS[0 + half][:], AF.Silu)
                            if shared:
                                e.copy(gbb, PS[2 + half][:])
                            else:
                                pass
                            ins = e.activation(t1b, t1b, AF.Identity)
                        return ins
                    def pact(e):
                        e.activation(t1b, PS[0][:], AF.Silu)
                        return e.activation(t1b[:, 0:0] if False else gbb,
                                            PS[1][:], AF.Silu)
                    def silu0(e):
                        return e.activation(t1b, PS[0][:], AF.Silu)
                    step("scalar", silu0, waits=[mm_at])
                    s0_at = pa.at()

                    def gp0(e, hb=hb, wsl=wsl, shared=shared):
                        if shared:
                            e.tensor_copy(gbb, PS[2][:])
                        else:
                            e.tensor_tensor(gbb, PS[2][:], wsl[:, 0:512],
                                            ALU.mult)
                        return e.tensor_tensor(Pt3[:, hb, 0:512], t1b, gbb,
                                               ALU.mult)
                    w_g0 = [mm_at, s0_at]
                    if wbc_at is not None:
                        w_g0.append(wbc_at)
                    step("vector", gp0, waits=w_g0)

                    def silu1(e):
                        return e.activation(t1b, PS[1][:], AF.Silu)
                    step("scalar", silu1, waits=[(pv, pv.n)])
                    s1_at = pa.at()

                    def gp1(e, hb=hb, wsl=wsl, shared=shared):
                        if shared:
                            e.tensor_copy(gbb, PS[3][:])
                        else:
                            e.tensor_tensor(gbb, PS[3][:], wsl[:, 512:1024],
                                            ALU.mult)
                        return e.tensor_tensor(Pt3[:, hb, 512:1024], t1b, gbb,
                                               ALU.mult)
                    step("vector", gp1, waits=[s1_at])
                    pbuild_at = pv.at()
                P_at = pbuild_at
                wbc_prev[e_ % 2] = (pv, pv.n)

                for db in range(16):
                    u2 = e_ * 16 + db
                    lpar = u2 % 2
                    w2l = w2c[:, lpar * 2048:(lpar + 1) * 2048]
                    w2p = w2ld2[lpar]
                    w_w2 = [w2_prev[lpar]] if lpar in w2_prev else []

                    def w2load(e, slots=slots, db=db, w2l=w2l):
                        return e.dma_start(w2l, s.s_f2b[slots[2], db])
                    dma("sync", w2load, waits=w_w2, pipe=w2p)
                    w2_at = w2p.at()
                    w2l3 = w2l.rearrange("p (k n) -> p k n", k=16)

                    def mmO(e, w2l3=w2l3):
                        for half in range(2):
                            for k in range(16):
                                ins = e.matmul(
                                    PS[4 + half][:], w2l3[:, k, :],
                                    Pt3[:, k, half * 512:(half + 1) * 512],
                                    start=(k == 0), stop=(k == 15))
                        return ins
                    w_mo = [w2_at, P_at]
                    if db >= 1 or e_ >= 1:
                        w_mo.append(oadd_at)
                    if oacc_free is not None and e_ == 0 and db == 0:
                        w_mo.append(oacc_free)
                    step("tensor", mmO, waits=w_mo)
                    mo_at = pt.at()
                    w2_prev[lpar] = (pt, pt.n)

                    def oadd(e, db=db, shared=shared):
                        if shared:
                            e.tensor_copy(oacc3[:, db, 0:512], PS[4][:])
                            return e.tensor_copy(oacc3[:, db, 512:1024],
                                                 PS[5][:])
                        e.tensor_tensor(oacc3[:, db, 0:512],
                                        oacc3[:, db, 0:512], PS[4][:], ALU.add)
                        return e.tensor_tensor(oacc3[:, db, 512:1024],
                                               oacc3[:, db, 512:1024],
                                               PS[5][:], ALU.add)
                    step("vector", oadd, par=True, waits=[mo_at])
                    oadd_at = pv.at()
            pe_xn_last = (pt, pt.n)
            if "dbg_oacc" in self.debug:
                if not hasattr(self, "_dbg_oacc"):
                    self._dbg_oacc = self.dram("dbg_oacc", [NFT, 128, 16 * 1024],
                                               BF16)
                    self._dbg_op = S.pipe("dbg_oacc_p")
                dma("gpsimd", lambda e, t=t: e.dma_start(
                    self._dbg_oacc[t], oacc), waits=[oadd_at],
                    pipe=self._dbg_op)

            for gg in range(8):
                grp = t * 8 + gg
                par = grp % 2
                ylp = yld2[par]
                yl = y1t[:, par * D:(par + 1) * D]

                def yload2(e, grp=grp, yl=yl):
                    return e.dma_start(yl, s.s_y1[grp * 128:(grp + 1) * 128, :])
                dma("sync", yload2, waits=[y1_prev[par], oadd_at], pipe=ylp)
                y_at = ylp.at()

                def otpf(e, gg=gg):
                    return [e.dma_start_transpose(
                        x2tm[:, k * 128:(k + 1) * 128],
                        oacc3[:, k, gg * 128:(gg + 1) * 128]) for k in range(16)]
                w_otp = [oadd_at, (pv, pv.n)]
                if xtp_last is not None:
                    w_otp.append(xtp_last)
                dma("sync", otpf, waits=w_otp, pipe=xtp, n=16)
                xtp_last = xtp.at()

                def cst(e):
                    return e.tensor_copy(sq2, x2tm)
                w_cst = [xtp_last]
                if pa.n:
                    w_cst.append((pa, pa.n))
                step("vector", cst, waits=w_cst)
                cst_at = pv.at()

                def lnsq2(e):
                    return e.activation(st3[:, 8:9] if False else sq2, sq2,
                                        AF.Square, accum_out=None)
                def lnst2a(e):
                    return e.tensor_reduce(st3[:, 8:9], sq2, AX.X, ALU.add)
                step("vector", lnst2a, waits=[cst_at])

                def lnsq2b(e):
                    return e.activation(y1t[:, 0:0] if False else sq2, sq2,
                                        AF.Square)
                step("scalar", lnsq2b, waits=[(pv, pv.n)])
                sqb_at = pa.at()

                def lnst2b(e):
                    e.tensor_reduce(st3[:, 9:10], sq2, AX.X, ALU.add)
                    e.tensor_scalar(st3[:, 10:11], st3[:, 8:9], 1.0 / D, None,
                                    ALU.mult)
                    e.tensor_scalar(st3[:, 11:12], st3[:, 9:10], 1.0 / D, None,
                                    ALU.mult)
                    e.tensor_tensor(st3[:, 12:13], st3[:, 10:11], st3[:, 10:11],
                                    ALU.mult)
                    e.tensor_tensor(st3[:, 11:12], st3[:, 11:12], st3[:, 12:13],
                                    ALU.subtract)
                    e.tensor_scalar(st3[:, 11:12], st3[:, 11:12], LN_EPS, None,
                                    ALU.add)
                    return e.tensor_scalar(st3[:, 10:11], st3[:, 10:11], -1.0,
                                           None, ALU.mult)
                step("vector", lnst2b, waits=[sqb_at])
                l2b_at = pv.at()

                def lnrs2(e):
                    e.activation(st3[:, 11:12], st3[:, 11:12], AF.Ln)
                    return e.activation(st3[:, 11:12], st3[:, 11:12], AF.Exp,
                                        scale=-0.5)
                step("scalar", lnrs2, waits=[l2b_at])
                lrs2_at = pa.at()

                def fin(e, yl=yl):
                    e.tensor_copy(sq2, x2tm)
                    e.tensor_scalar(sq2, sq2, st3[:, 10:11], st3[:, 11:12],
                                    ALU.add, ALU.mult)
                    return e.tensor_tensor(yl, sq2, yl, ALU.add)
                w_fin = [lrs2_at, y_at]
                if par in yst_prev:
                    w_fin.append(yst_prev[par])
                step("vector", fin, waits=w_fin)
                fin_at = pv.at()

                def yst(e, grp=grp, yl=yl):
                    return e.dma_start(s.y_out[grp * 128:(grp + 1) * 128, :],
                                       yl)
                ystp = yst2[par]
                dma("sync", yst, waits=[fin_at], pipe=ystp)
                y1_prev[par] = ystp.at()
                yst_prev[par] = ystp.at()
            oacc_free = (pv, pv.n)
        self.fin_waits = [yst2[0].at(), yst2[1].at()]

    def ph_final(self):
        s, S = self, self.S
        if "dbg_scal" in self.debug:
            dbg = self.dram("dbg_scal", [7, 128, NG * 64], F32)
            dpipe = S.pipe("dbg")
            waits = [p.at() for p in S._pipes.values() if p.n > 0]
            tens = [s.beta_s, s.negsc1_s, s.egjc_s, s.egLmg_s, s.egL_s, s.gj_s,
                    s.ba_tm]
            S.dma("gpsimd", lambda e: [
                e.dma_start(dbg[i, :, :t.ap().shape[1]], t.ap()[:])
                for i, t in enumerate(tens)], waits=waits, pipe=dpipe, n=7)
        waits = [p.at() for p in S._pipes.values() if p.n > 0]
        S.step("sync", lambda e: e.nop(), waits=waits)


def stage_inputs(inputs):
    f32 = np.float32
    sl = np.tril(np.ones((128, 128), f32), -1)
    ui = np.triu(np.ones((128, 128), f32))
    c_masks = np.stack([
        BIG * (1.0 - sl),
        -BIG * (1.0 - ui),
        ui,
        np.eye(128, dtype=f32),
    ]).astype(f32)
    pad_col = np.zeros((D, 1), f32)
    neg = np.full((1,), -1.0e30, f32)
    shared = {
        "w_qkv": np.ascontiguousarray(inputs["L0_attn_W_qkv"]),
        "w_z": np.ascontiguousarray(inputs["L0_attn_W_z"]),
        "w_ba": np.ascontiguousarray(
            np.concatenate([inputs["L0_attn_W_b"], inputs["L0_attn_W_a"]], 1)),
        "conv_w": np.ascontiguousarray(inputs["L0_attn_conv_weight"]),
        "dt_bias_r": np.ascontiguousarray(inputs["L0_attn_dt_bias"][None, :]),
        "a_log_r": np.ascontiguousarray(inputs["L0_attn_A_log"][None, :]),
        "norm_w_r": np.ascontiguousarray(inputs["L0_attn_norm_w"][None, :]),
        "w_out": np.ascontiguousarray(inputs["L0_attn_W_out"]),
        "ffn_w1": np.ascontiguousarray(inputs["L0_ffn_W1"]),
        "ffn_wg": np.ascontiguousarray(inputs["L0_ffn_Wg"]),
        "ffn_w2": np.ascontiguousarray(inputs["L0_ffn_W2"]),
        "ffn_ws1": np.ascontiguousarray(inputs["L0_ffn_Ws1"]),
        "ffn_wsg": np.ascontiguousarray(inputs["L0_ffn_Wsg"]),
        "ffn_ws2": np.ascontiguousarray(inputs["L0_ffn_Ws2"]),
        "gate_w": np.ascontiguousarray(
            np.concatenate([inputs["L0_ffn_gate_w"], pad_col], 1)),
        "gate_b_r": np.ascontiguousarray(
            np.concatenate([inputs["L0_ffn_gate_b"], neg])[None, :]),
        "biases_r": np.ascontiguousarray(
            np.concatenate([inputs["L0_ffn_biases"], neg])[None, :]),
        "c_masks": c_masks,
    }
    shared = {k: np.asarray(v, f32) for k, v in shared.items()}
    hid = np.asarray(inputs["L0_attn_hidden"], f32)
    in_maps = []
    for c in range(8):
        m = dict(shared)
        m["hidden"] = np.ascontiguousarray(hid[c])
        in_maps.append(m)
    return in_maps


_NC_CACHE = {}


def kernel(**inputs):
    key = "full"
    if key not in _NC_CACHE:
        _NC_CACHE[key] = Builder().build()
    nc = _NC_CACHE[key]
    in_maps = stage_inputs(inputs)
    res = run_bass_kernel_spmd(nc, in_maps, core_ids=list(range(8)))
    return np.stack([np.asarray(r["y"], np.float32) for r in res.results])




# version 06
# speedup vs baseline: 1.0601x; 1.0601x over previous
import contextlib
import math
import numpy as np
import concourse.bass as bass
import concourse.mybir as mybir
from concourse.bass_utils import run_bass_kernel_spmd

F32 = mybir.dt.float32
BF16 = mybir.dt.bfloat16
AF = mybir.ActivationFunctionType
ALU = mybir.AluOpType
AX = mybir.AxisListType

T = 4096
D = 2048
HK, HV, AK, UV = 16, 32, 128, 128
KEY_DIM = HK * AK
VALUE_DIM = HV * UV
CONV_DIM = 2 * KEY_DIM + VALUE_DIM
CH = 128
NCH = T // CH
NG = T // 128
BIG = 1.0e6
C_SCALE = AK ** -0.5
LN_EPS = 1e-5
RMS_EPS = 1e-6
L2_EPS = 1e-6
NEXP = 7
FT = 1024
NFT = T // FT
ENGINES = ("sync", "tensor", "vector", "scalar", "gpsimd")


class Pipe:

    def __init__(self, sem, name):
        self.sem = sem
        self.name = name
        self.n = 0

    def at(self):
        return (self, self.n)


_DRAIN_OPS = frozenset([
    "tensor_tensor", "tensor_scalar", "tensor_copy", "tensor_reduce",
    "reduce_sum", "reduce_max", "scalar_tensor_tensor", "reciprocal",
    "memset", "memzero", "select", "bn_stats", "bn_aggr", "activation",
    "copy", "add", "mul", "square", "sqrt", "sign", "tensor_add",
    "tensor_sub", "tensor_mul", "tensor_scalar_add", "tensor_scalar_mul",
    "tensor_scalar_max", "tensor_scalar_min", "tensor_scalar_sub",
    "tensor_relu", "tensor_max", "transpose", "iota", "copy_predicated",
    "max", "max_index", "max_with_indices", "stream_shuffle",
])
_FLUSH_OPS = frozenset(["dma_start", "dma_start_transpose"])


class DrainProxy:

    def __init__(self, e):
        object.__setattr__(self, "_e", e)
        object.__setattr__(self, "_armed", False)
        object.__setattr__(self, "_par", False)
        object.__setattr__(self, "_par_done", False)

    def begin_step(self, par):
        object.__setattr__(self, "_par", par)
        object.__setattr__(self, "_par_done", False)

    def __getattr__(self, name):
        e = object.__getattribute__(self, "_e")
        attr = getattr(e, name)
        if name in _DRAIN_OPS:
            def wrapped(*a, **k):
                par = object.__getattribute__(self, "_par")
                done = object.__getattribute__(self, "_par_done")
                if object.__getattribute__(self, "_armed") and not (par and done):
                    e.drain()
                object.__setattr__(self, "_armed", True)
                object.__setattr__(self, "_par_done", True)
                return attr(*a, **k)
            return wrapped
        if name in _FLUSH_OPS:
            def wrapped2(*a, **k):
                if object.__getattribute__(self, "_armed"):
                    e.drain()
                    object.__setattr__(self, "_armed", False)
                return attr(*a, **k)
            return wrapped2
        return attr


class Sched:

    def __init__(self, nc, exitstack):
        self.nc = nc
        self.exit = exitstack
        self.steps = []
        self._pipes = {}

    def pipe(self, name):
        if name not in self._pipes:
            sem = self.exit.enter_context(self.nc.semaphore(name))
            self._pipes[name] = Pipe(sem, name)
        return self._pipes[name]

    def step(self, engine, fn, waits=(), incs=(), par=False):
        norm = []
        for item in incs:
            if len(item) == 2:
                p_, a_ = item
                norm.append((p_, a_, "last"))
            else:
                norm.append(item)
        self.steps.append((engine, fn, list(waits), norm, par))
        for p_, a_, mode in norm:
            if mode == "each":
                p_.n += a_ * self._count_instrs_hint
            else:
                p_.n += a_
        return tuple(p_.n for p_, _, _ in norm)

    _count_instrs_hint = 1

    def dma(self, engine, fn, waits=(), pipe=None, n=1):
        self._count_instrs_hint = n
        r = self.step(engine, fn, waits,
                      incs=([(pipe, 16, "each")] if pipe else ()))
        self._count_instrs_hint = 1
        return r

    def emit(self, blk):
        by_eng = {e: [] for e in ENGINES}
        for st in self.steps:
            by_eng[st[0]].append(st)
        for eng_name in ENGINES:
            steps = by_eng[eng_name]
            if not steps:
                continue

            def body(e, steps=steps, eng_name=eng_name):
                if eng_name in ("vector", "scalar", "gpsimd"):
                    e = DrainProxy(e)
                watermark = {}
                for _, fn, waits, incs, par_ in steps:
                    if isinstance(e, DrainProxy):
                        e.begin_step(par_)
                    for p_, val in waits:
                        if val > 0 and watermark.get(p_.name, -1) < val:
                            e.wait_ge(p_.sem, val)
                            watermark[p_.name] = val
                    out = fn(e)
                    if incs:
                        instrs = out if isinstance(out, (list, tuple)) else [out]
                        n_on_last = 0
                        for p_, a_, mode in incs:
                            if mode == "each":
                                for ins in instrs:
                                    ins.then_inc(p_.sem, a_)
                            elif n_on_last == 0:
                                instrs[-1].then_inc(p_.sem, a_)
                                n_on_last = 1
                            else:
                                if eng_name in ("vector", "scalar", "gpsimd"):
                                    (e._e if isinstance(e, DrainProxy) else e
                                     ).drain()
                                    if isinstance(e, DrainProxy):
                                        object.__setattr__(e, "_armed", False)
                                e.sem_inc(p_.sem, a_)

            getattr(blk, eng_name)(body)


class Builder:
    def __init__(self, debug=()):
        self.debug = set(debug)
        self.nc = bass.Bass("TRN2", target_bir_lowering=False)
        self.exit = contextlib.ExitStack()

    def dram(self, name, shape, dtype, kind="Internal"):
        k = "ExternalOutput" if name in self.debug else kind
        return self.nc.dram_tensor(name, shape, dtype, kind=k).ap()

    def declare(self):
        nc = self.nc
        dram = self.dram
        inp = lambda n, s: nc.dram_tensor(n, s, F32, kind="ExternalInput").ap()
        self.x_in = inp("hidden", [T, D])
        self.w_qkv = inp("w_qkv", [D, CONV_DIM])
        self.w_z = inp("w_z", [D, VALUE_DIM])
        self.w_ba = inp("w_ba", [D, 64])
        self.conv_w = inp("conv_w", [CONV_DIM, 4])
        self.dt_bias_r = inp("dt_bias_r", [1, HV])
        self.a_log_r = inp("a_log_r", [1, HV])
        self.norm_w_r = inp("norm_w_r", [1, UV])
        self.w_out = inp("w_out", [VALUE_DIM, D])
        self.ffn_w1 = inp("ffn_w1", [NEXP, D, D])
        self.ffn_wg = inp("ffn_wg", [NEXP, D, D])
        self.ffn_w2 = inp("ffn_w2", [NEXP, D, D])
        self.ffn_ws1 = inp("ffn_ws1", [D, D])
        self.ffn_wsg = inp("ffn_wsg", [D, D])
        self.ffn_ws2 = inp("ffn_ws2", [D, D])
        self.gate_w = inp("gate_w", [D, 8])
        self.gate_b_r = inp("gate_b_r", [1, 8])
        self.biases_r = inp("biases_r", [1, 8])
        self.c_masks = inp("c_masks", [4, 128, 128])
        self.y_out = nc.dram_tensor("y", [T, D], F32, kind="ExternalOutput").ap()

        self.s_qf = dram("s_qf", [HK, AK, T], BF16)
        self.s_kf = dram("s_kf", [HK, AK, T], BF16)
        self.s_kt = dram("s_kt", [HK, NG, 128, AK], BF16)
        self.s_vt = dram("s_vt", [HV, NG, 128, UV], BF16)
        self.s_z = dram("s_z", [T, VALUE_DIM], BF16)
        self.s_gj = dram("s_gj", [NCH, HV, CH], F32)
        self.s_gl = dram("s_gl", [NCH, HV], F32)
        self.s_y1 = dram("s_y1", [T, D], F32)
        self.s_wob2 = dram("s_wob2", [8, 128, 4, D], BF16)
        self.s_fab = dram("s_fab", [16, 16, 128, D], BF16)
        self.s_f2b = dram("s_f2b", [8, 16, 128, D], BF16)
        self.s_wg8 = dram("s_wg8", [NFT, 8, FT], BF16)

        A = nc.alloc_sbuf_tensor
        self.masks = A("masks", [128, 4 * 128], F32)
        self.normw_bc = A("normw_bc", [128, UV], F32)
        self.dtb_bc = A("dtb_bc", [128, HV], F32)
        self.negea_bc = A("negea_bc", [128, HV], F32)
        self.gateb_bc = A("gateb_bc", [128, 8], F32)
        self.biases_bc = A("biases_bc", [128, 8], F32)
        self.ba_tm = A("ba_tm", [128, NG * 64], F32)
        self.beta_s = A("beta_s", [128, NCH * HV], F32)
        self.negsc1_s = A("negsc1_s", [128, NCH * HV], F32)
        self.egjc_s = A("egjc_s", [128, NCH * HV], F32)
        self.egLmg_s = A("egLmg_s", [128, NCH * HV], F32)
        self.egL_s = A("egL_s", [128, NCH * HV], F32)
        self.gj_s = A("gj_s", [128, NCH * HV], F32)
        self.S_f = A("S_f", [128, HV * UV], F32)
        self.S_b = A("S_b", [128, HV * UV], BF16)
        self.ARENA_W = 38100
        self.arena = A("arena", [128, self.ARENA_W], F32)
        self.psum = [nc.alloc_psum_tensor(f"pb{i}", [128, 512], F32)
                     for i in range(8)]

        self.m_sl_big = self.masks[:, 0:128]
        self.m_u_negbig = self.masks[:, 128:256]
        self.m_triu = self.masks[:, 256:384]
        self.ones_col = self.masks[:, 383:384]
        self.ident_f = self.masks[:, 384:512]

    def carve(self, off, n):
        assert off + n <= self.ARENA_W, (off, n)
        return self.arena.ap()[:, off:off + n], off + n

    def carve_b(self, off, n_h):
        nw = (n_h + 1) // 2
        assert off + nw <= self.ARENA_W, (off, n_h)
        return self.arena.ap().bitcast(BF16)[:, off * 2: off * 2 + n_h], off + nw

    def pp(self, tens, j, h):
        return tens.ap()[:, j * HV + h: j * HV + h + 1]

    def build(self, phases=("a01", "a2", "a3", "b")):
        self.declare()
        S = Sched(self.nc, self.exit)
        self.S = S
        self.ph_consts()
        if "a01" in phases:
            self.ph_a01()
        if "a2" in phases:
            self.ph_a2()
        if "a3" in phases:
            self.ph_a3()
        if "b" in phases:
            self.ph_b()
        self.ph_final()
        with self.nc.Block() as blk:
            S.emit(blk)
        self.exit.close()
        return self.nc

    def ph_consts(self):
        s, S = self, self.S
        ld = S.pipe("c_ld")
        self.c_done = S.pipe("c_done")
        S.dma("sync", lambda e: [
            e.dma_start(s.masks[:].rearrange("p (k n) -> p k n", k=4),
                        s.c_masks.transpose([1, 0, 2])),
            e.dma_start(s.normw_bc[:], s.norm_w_r.to_broadcast([128, UV])),
            e.dma_start(s.dtb_bc[:], s.dt_bias_r.to_broadcast([128, HV])),
            e.dma_start(s.negea_bc[:], s.a_log_r.to_broadcast([128, HV])),
            e.dma_start(s.gateb_bc[:], s.gate_b_r.to_broadcast([128, 8])),
            e.dma_start(s.biases_bc[:], s.biases_r.to_broadcast([128, 8])),
        ], pipe=ld, n=6)
        c_act = S.pipe("c_act")
        S.step("scalar", lambda e: e.activation(s.negea_bc[:], s.negea_bc[:], AF.Exp),
               waits=[ld.at()], incs=[(c_act, 1)])
        S.step("vector",
               lambda e: e.tensor_scalar(s.negea_bc[:], s.negea_bc[:], -1.0, None,
                                         ALU.mult),
               waits=[c_act.at()], incs=[(self.c_done, 1)])

    def ph_a01(self):
        s, S = self, self.S
        off = 0
        xn_t, off = self.carve_b(off, 16 * 2048)
        xh, off = self.carve(off, 2 * D)
        sq, off = self.carve(off, D)
        xn_tm, off = self.carve_b(off, D)
        st_, off = self.carve(off, 16)
        wba_f, off = self.carve(off, 16 * 64)
        wba_b, off = self.carve_b(off, 16 * 64)
        wst, off = self.carve(off, 2 * 2048)
        wbb, off = self.carve_b(off, 2 * 2048)
        cw, off = self.carve(off, 2 * 4)
        acc, off = self.carve(off, 3 * 512)
        sil, off = self.carve_b(off, 3 * 512)
        tmst, off = self.carve_b(off, 3 * 512)
        sqn, off = self.carve(off, 512)
        ssq, off = self.carve(off, 2 * 8)
        nfm, off = self.carve_b(off, 3 * 512)
        tails, off = self.carve(off, 64 * 4)
        b0f, off = self.carve(off, 2 * 512)
        b0b, off = self.carve_b(off, 2 * 512)
        self.a01_end = off
        xn_t3 = xn_t.rearrange("p (k t) -> p k t", k=16)
        wbb3 = wbb.rearrange("p (k n) -> p k n", k=16)
        wba_b3 = wba_b.rearrange("p (k n) -> p k n", k=16)

        ld2 = [S.pipe("a0_ld0"), S.pipe("a0_ld1")]
        sqp, v1, v2, tp = (S.pipe("a0_sq"), S.pipe("a0_v1"),
                           S.pipe("a0_v2"), S.pipe("a0_tp"))
        gld, gcast, gmm, gev = (S.pipe("g_ld"), S.pipe("g_cast"), S.pipe("g_mm"),
                                S.pipe("g_ev"))
        wld2 = [S.pipe("a1_wld0"), S.pipe("a1_wld1")]
        wcast, amm, acv, asl = (
            S.pipe("a1_wcast"), S.pipe("a1_mm"),
            S.pipe("a1_cv"), S.pipe("a1_sl"))
        atp2 = [S.pipe("a1_tp0"), S.pipe("a1_tp1"), S.pipe("a1_tp2")]
        atp_done = {}
        ntp2 = [S.pipe("a1_ntp0"), S.pipe("a1_ntp1"), S.pipe("a1_ntp2")]
        sp_tm = [S.pipe("a1_sptm0"), S.pipe("a1_sptm1"), S.pipe("a1_sptm2")]
        sp_fm = [S.pipe("a1_spfm0"), S.pipe("a1_spfm1"), S.pipe("a1_spfm2")]
        n_qkv = [0]
        tm_guard = {}
        fm_guard = {}
        ntp_done = {}
        bank_free = {}
        nrm, ntp, nr2 = S.pipe("a1_nrm"), S.pipe("a1_ntp"), S.pipe("a1_nr2")
        qk_units = [0]
        b0p, b0c = S.pipe("b0_dma"), S.pipe("b0_cast")
        self.g_ev = gev

        S.step("vector", lambda e: e.memset(tails, 0.0))

        b0_jobs = []
        amats = ([(s.ffn_w1[e_], e_) for e_ in range(NEXP)]
                 + [(s.ffn_wg[e_], NEXP + e_) for e_ in range(NEXP)]
                 + [(s.ffn_ws1, 14), (s.ffn_wsg, 15)])
        for src_m, slot in amats:
            for r in range(16):
                for hh in range(4):
                    dst = s.s_fab[slot, hh * 4:(hh + 1) * 4, :,
                                  r * 128:(r + 1) * 128].transpose([1, 0, 2])
                    b0_jobs.append((src_m[r * 128:(r + 1) * 128,
                                          hh * 512:(hh + 1) * 512]
                                    .rearrange("p (h c) -> p h c", h=4), dst))
        mats2 = [(s.ffn_w2[e_], e_) for e_ in range(NEXP)] + [(s.ffn_ws2, 7)]
        for src_m, slot in mats2:
            for r in range(16):
                for hh in range(4):
                    dst = s.s_f2b[slot, hh * 4:(hh + 1) * 4, :,
                                  r * 128:(r + 1) * 128].transpose([1, 0, 2])
                    b0_jobs.append((src_m[r * 128:(r + 1) * 128,
                                          hh * 512:(hh + 1) * 512]
                                    .rearrange("p (h c) -> p h c", h=4), dst))
        for r in range(32):
            for hh in range(4):
                b0_jobs.append((s.w_out[r * 128:(r + 1) * 128,
                                        hh * 512:(hh + 1) * 512],
                                s.s_wob2[r // 4, :, r % 4,
                                         hh * 512:(hh + 1) * 512]))
        b0_i = [0]
        import os
        b0_disable = os.environ.get("NO_B0", "0") == "1"

        def b0_emit(n):
            if b0_disable:
                return
            for _ in range(n):
                if b0_i[0] >= len(b0_jobs):
                    return
                i = b0_i[0]
                b0_i[0] += 1
                srcap, dstap = b0_jobs[i]
                bb = (i % 2)
                fsl = b0f[:, bb * 512:(bb + 1) * 512]
                bsl = b0b[:, bb * 512:(bb + 1) * 512]
                w_ = [(b0c, i - 1)] if i >= 2 else []
                S.dma("gpsimd", lambda e, srcap=srcap, fsl=fsl: e.dma_start(
                    fsl if srcap.ndim == 2 else
                    fsl.rearrange("p (h c) -> p h c", h=4), srcap),
                    waits=w_, pipe=b0p)
                eng = "vector" if i % 2 == 0 else "scalar"
                fn = ((lambda e, fsl=fsl, bsl=bsl: e.tensor_copy(bsl, fsl))
                      if eng == "vector" else
                      (lambda e, fsl=fsl, bsl=bsl: e.copy(bsl, fsl)))
                S.step(eng, fn, waits=[b0p.at()], incs=[(b0c, 1)])
                S.dma("gpsimd", lambda e, dstap=dstap, bsl=bsl: e.dma_start(
                    dstap, bsl if dstap.ndim == 2 else
                    bsl.rearrange("p (h c) -> p h c", h=4)),
                    waits=[b0c.at()], pipe=b0p)

        S.dma("sync", lambda e: e.dma_start(
            wba_f.rearrange("p (k n) -> p k n", k=16),
            s.w_ba.rearrange("(k p) n -> p k n", p=128)), pipe=gld)
        S.step("vector", lambda e: e.tensor_copy(wba_b, wba_f),
               waits=[gld.at()], incs=[(gcast, 1)])

        n_units = [0]
        for stile in range(2):
            TB = stile * 2048
            for i in range(16):
                gi = stile * 16 + i
                b = (i % 2) * D
                xhb = xh[:, b:b + D]
                w_free = [(v2, gi - 1)] if gi >= 2 else []
                ldp = ld2[gi % 2]
                S.dma("sync", lambda e, gi=gi, xhb=xhb: e.dma_start(
                    xhb, s.x_in[gi * 128:(gi + 1) * 128, :]),
                    waits=w_free, pipe=ldp)
                S.step("scalar", lambda e, xhb=xhb: e.activation(
                    sq, xhb, AF.Square),
                    waits=[ldp.at(), (v1, gi)], incs=[(sqp, 1)])

                def vstats(e, xhb=xhb):
                    e.tensor_reduce(st_[:, 0:1], xhb, AX.X, ALU.add)
                    e.tensor_reduce(st_[:, 1:2], sq, AX.X, ALU.add)
                    e.tensor_scalar(st_[:, 2:3], st_[:, 0:1], 1.0 / D, None, ALU.mult)
                    e.tensor_scalar(st_[:, 3:4], st_[:, 1:2], 1.0 / D, None, ALU.mult)
                    e.tensor_tensor(st_[:, 4:5], st_[:, 2:3], st_[:, 2:3], ALU.mult)
                    e.tensor_tensor(st_[:, 5:6], st_[:, 3:4], st_[:, 4:5],
                                    ALU.subtract)
                    e.tensor_scalar(st_[:, 5:6], st_[:, 5:6], LN_EPS, None,
                                    ALU.add)
                    return e.tensor_scalar(st_[:, 7:8], st_[:, 2:3], -1.0, None,
                                           ALU.mult)
                S.step("vector", vstats, waits=[ldp.at(), sqp.at()],
                       incs=[(v1, 1)])

                def rstd(e):
                    e.activation(st_[:, 6:7], st_[:, 5:6], AF.Ln)
                    return e.activation(st_[:, 6:7], st_[:, 6:7], AF.Exp,
                                        scale=-0.5)
                S.step("scalar", rstd, waits=[v1.at()], incs=[(sqp, 1)])
                w_tp = [(tp, 256 * gi)] if gi >= 1 else []
                w_stile = [(amm, 384)] if (stile == 1 and i == 0) else []
                S.step("vector", lambda e, xhb=xhb: e.tensor_scalar(
                    xn_tm, xhb, st_[:, 7:8], st_[:, 6:7], ALU.add, ALU.mult),
                    waits=[sqp.at()] + w_tp + w_stile, incs=[(v2, 1)])
                S.dma("sync", lambda e, i=i: [
                    e.dma_start_transpose(xn_t3[:, k, i * 128:(i + 1) * 128],
                                          xn_tm[:, k * 128:(k + 1) * 128])
                    for k in range(16)], waits=[(v2, gi + 1)], pipe=tp, n=16)

            for i in range(16):
                gi = stile * 16 + i
                bank = self.psum[4 + (gi % 2)][:, 0:64]

                def gmmf(e, i=i, bank=bank):
                    for k in range(16):
                        ins = e.matmul(bank, xn_t3[:, k, i * 128:(i + 1) * 128],
                                       wba_b3[:, k, :], start=(k == 0),
                                       stop=(k == 15))
                    return ins
                w = [gcast.at(), (tp, 256 * 16 * (stile + 1))]
                if gi >= 2:
                    w.append((gev, gi - 1))
                S.step("tensor", gmmf, waits=w, incs=[(gmm, 1)])
                S.step("scalar", lambda e, gi=gi, bank=bank: e.copy(
                    s.ba_tm.ap()[:, gi * 64:(gi + 1) * 64], bank),
                    waits=[gmm.at()], incs=[(gev, 1)])

            for cb in range(96):
                is_qk = cb < 32
                is_v = 32 <= cb < 64
                is_z = cb >= 64
                wsl = wst[:, (cb % 2) * 2048:(cb % 2 + 1) * 2048]
                wbl = wbb[:, (cb % 2) * 2048:(cb % 2 + 1) * 2048]
                wbl3 = wbl.rearrange("p (k n) -> p k n", k=16)
                gcb = stile * 96 + cb
                wldp = wld2[gcb % 2]
                w_wfree = [(wcast, gcb - 1)] if gcb >= 2 else []
                if is_z:
                    srcw = s.w_z.rearrange("(k p) n -> p k n", p=128)[
                        :, :, (cb - 64) * 128:(cb - 63) * 128]
                else:
                    srcw = s.w_qkv.rearrange("(k p) n -> p k n", p=128)[
                        :, :, cb * 128:(cb + 1) * 128]
                S.dma("gpsimd", lambda e, srcw=srcw, wsl=wsl: e.dma_start(
                    wsl.rearrange("p (k n) -> p k n", k=16), srcw),
                    waits=w_wfree, pipe=wldp)
                if not is_z:
                    w_cw = ([(acv, 4 * (n_qkv[0] - 1))] if n_qkv[0] >= 2 else [])
                    S.dma("gpsimd", lambda e, cb=cb: e.dma_start(
                        cw[:, (cb % 2) * 4:(cb % 2) * 4 + 4],
                        s.conv_w[cb * 128:(cb + 1) * 128, :]),
                        waits=w_cw, pipe=wldp)
                    n_qkv[0] += 1
                w_cast = [wldp.at()]
                if gcb >= 2:
                    w_cast.append((amm, 4 * (gcb - 1)))
                S.step("vector", lambda e, wsl=wsl, wbl=wbl: e.tensor_copy(
                    wbl, wsl), waits=w_cast, incs=[(wcast, 1)])

                for tt in range(4):
                    u = n_units[0]
                    n_units[0] += 1
                    gtt = stile * 4 + tt
                    bank = self.psum[[0, 1, 2, 3, 6, 7][u % 6]]
                    bb = u % 3
                    accl = acc[:, bb * 512:(bb + 1) * 512]
                    sill = sil[:, bb * 512:(bb + 1) * 512]
                    tmstl = tmst[:, bb * 512:(bb + 1) * 512]
                    sqnl = sqn
                    ssql = ssq[:, (u % 2) * 8:(u % 2) * 8 + 4]
                    nfml = nfm[:, (u % 3) * 512:(u % 3 + 1) * 512]

                    def mmf(e, tt=tt, bank=bank, wbl3=wbl3):
                        for k in range(16):
                            ins = e.matmul(bank[:], wbl3[:, k, :],
                                           xn_t3[:, k, tt * 512:(tt + 1) * 512],
                                           start=(k == 0), stop=(k == 15))
                        return ins
                    w = [wcast.at(), (tp, 256 * 16 * (stile + 1)), gev.at()]
                    if (u - 6) in bank_free:
                        w.append(bank_free[u - 6])
                    S.step("tensor", mmf, waits=w, incs=[(amm, 1)])

                    if not is_z:
                        def convf(e, cb=cb, bank=bank, accl=accl, cwl=cw[
                                :, (cb % 2) * 4:(cb % 2) * 4 + 4]):
                            tl = tails[:, cb * 4:cb * 4 + 3]
                            e.tensor_scalar(accl, bank[:], cwl[:, 3:4], None,
                                            ALU.mult)
                            e.scalar_tensor_tensor(accl[:, 1:512],
                                                   bank[:, 0:511],
                                                   cwl[:, 2:3], accl[:, 1:512],
                                                   ALU.mult, ALU.add)
                            e.scalar_tensor_tensor(accl[:, 2:512],
                                                   bank[:, 0:510],
                                                   cwl[:, 1:2], accl[:, 2:512],
                                                   ALU.mult, ALU.add)
                            e.scalar_tensor_tensor(accl[:, 3:512],
                                                   bank[:, 0:509],
                                                   cwl[:, 0:1], accl[:, 3:512],
                                                   ALU.mult, ALU.add)
                            e.scalar_tensor_tensor(accl[:, 0:3], tl,
                                                   cwl[:, 0:1], accl[:, 0:3],
                                                   ALU.mult, ALU.add) if False \
                                else None
                            for k in range(3):
                                e.scalar_tensor_tensor(
                                    accl[:, 0:3 - k], tl[:, k:3],
                                    cwl[:, k:k + 1], accl[:, 0:3 - k],
                                    ALU.mult, ALU.add)
                            return e.tensor_copy(tl, bank[:, 509:512])
                        w_cv = [amm.at(), (asl, u - 2) if u >= 3 else (asl, 0)]
                        S.step("vector", convf, waits=w_cv, incs=[(acv, 1)])
                        bank_free[u] = (acv, acv.n)

                        w_sil = [acv.at()]
                        if ("atp", u % 3) in atp_done:
                            w_sil.append(atp_done[("atp", u % 3)])
                        S.step("scalar", lambda e, accl=accl, sill=sill:
                               e.activation(sill, accl, AF.Silu),
                               waits=w_sil, incs=[(asl, 1)])
                    else:
                        w_sil = [amm.at()]
                        if ("atp", u % 3) in atp_done:
                            w_sil.append(atp_done[("atp", u % 3)])
                        S.step("scalar", lambda e, sill=sill, bank=bank:
                               e.activation(sill, bank[:], AF.Silu),
                               waits=w_sil, incs=[(asl, 1)])
                        bank_free[u] = (asl, asl.n)

                    w_tmst = [(asl, asl.n)]
                    if ("tm", u % 3) in tm_guard:
                        w_tmst.append(tm_guard[("tm", u % 3)])
                    if ("ntp", u % 3) in ntp_done:
                        w_tmst.append(ntp_done[("ntp", u % 3)])
                    atq = "sync"
                    S.dma(atq, lambda e, sill=sill, tmstl=tmstl: [
                        e.dma_start_transpose(
                            tmstl[:, ss * 128:(ss + 1) * 128],
                            sill[:, ss * 128:(ss + 1) * 128])
                        for ss in range(4)],
                        waits=w_tmst, pipe=atp2[u % 3], n=4)
                    atp_done[("atp", u % 3)] = atp2[u % 3].at()

                    if is_qk:
                        kh = cb % 16
                        m_qk = qk_units[0]
                        qk_units[0] += 1
                        dst_t = None if cb < 16 else s.s_kt
                        dst_f = (s.s_qf if cb < 16 else s.s_kf)

                        def nrm12(e, tmstl=tmstl, ssql=ssql):
                            e.tensor_tensor(sqnl, tmstl, tmstl, ALU.mult)
                            e.drain()
                            e.tensor_reduce(
                                ssql, sqnl.rearrange("p (s c) -> p s c", s=4),
                                AX.X, ALU.add)
                            return e.tensor_scalar(ssql, ssql, L2_EPS, None,
                                                   ALU.add)
                        S.step("vector", nrm12,
                               waits=[atp_done[("atp", u % 3)], (nr2, m_qk)],
                               incs=[(nrm, 1), (nr2, 1)])

                        def nrm3(e, ssql=ssql):
                            e.activation(ssql, ssql, AF.Ln)
                            return e.activation(ssql, ssql, AF.Exp, scale=-0.5)
                        S.step("scalar", nrm3, waits=[nrm.at()], incs=[(nrm, 1)])

                        def nrm4(e, tmstl=tmstl, ssql=ssql):
                            for ss in range(4):
                                ins = e.tensor_scalar(
                                    tmstl[:, ss * 128:(ss + 1) * 128],
                                    tmstl[:, ss * 128:(ss + 1) * 128],
                                    ssql[:, ss:ss + 1], None, ALU.mult)
                            return ins
                        S.step("vector", nrm4, waits=[nrm.at()],
                               incs=[(nrm, 1)], par=True)
                        if dst_t is not None:
                            S.dma("sync", lambda e, kh=kh, gtt=gtt,
                                  tmstl=tmstl,
                                  dst_t=dst_t: e.dma_start(
                                      dst_t[kh, gtt * 4:(gtt + 1) * 4].transpose(
                                          [1, 0, 2]),
                                      tmstl.rearrange("p (s c) -> p s c", s=4)),
                                waits=[nrm.at()], pipe=sp_tm[u % 3])
                            tm_guard[("tm", u % 3)] = sp_tm[u % 3].at()
                        ntpp = ntp2[u % 3]
                        w_ntp = [nrm.at()]
                        if ("fm", u % 3) in fm_guard:
                            w_ntp.append(fm_guard[("fm", u % 3)])
                        S.dma("sync", lambda e, tmstl=tmstl, nfml=nfml: [
                            e.dma_start_transpose(
                                nfml[:, ss * 128:(ss + 1) * 128],
                                tmstl[:, ss * 128:(ss + 1) * 128])
                            for ss in range(4)],
                            waits=w_ntp, pipe=ntpp, n=4)
                        ntp_done[("ntp", u % 3)] = ntpp.at()
                        S.dma("sync", lambda e, kh=kh, TB=TB, tt=tt,
                              nfml=nfml,
                              dst_f=dst_f: e.dma_start(
                                  dst_f[kh, :, TB + tt * 512:TB + (tt + 1) * 512],
                                  nfml),
                            waits=[ntpp.at()], pipe=sp_fm[u % 3])
                        fm_guard[("fm", u % 3)] = sp_fm[u % 3].at()
                    elif is_v:
                        vh = cb - 32
                        S.dma("sync", lambda e, vh=vh, gtt=gtt, tmstl=tmstl:
                              e.dma_start(
                                  s.s_vt[vh, gtt * 4:(gtt + 1) * 4].transpose(
                                      [1, 0, 2]),
                                  tmstl.rearrange("p (s c) -> p s c", s=4)),
                            waits=[atp_done[("atp", u % 3)]],
                            pipe=sp_tm[u % 3])
                        tm_guard[("tm", u % 3)] = sp_tm[u % 3].at()
                    else:
                        zb = cb - 64
                        S.dma("sync", lambda e, zb=zb, TB=TB, tt=tt,
                              tmstl=tmstl:
                              e.dma_start(
                                  s.s_z[TB + tt * 512:TB + (tt + 1) * 512,
                                        zb * 128:(zb + 1) * 128].rearrange(
                                      "(s p) c -> p s c", p=128),
                                  tmstl.rearrange("p (s c) -> p s c", s=4)),
                            waits=[atp_done[("atp", u % 3)]],
                            pipe=sp_tm[u % 3])
                        tm_guard[("tm", u % 3)] = sp_tm[u % 3].at()
                b0_emit(9)
        b0_emit(10 ** 9)
        self.a1_done_waits = ([amm.at(), acv.at(), asl.at(), b0p.at(),
                               nrm.at()]
                              + [p.at() for p in sp_tm + sp_fm + atp2 + ntp2])

    def ph_a2(self):
        s, S = self, self.S
        off = self.a01_end
        lanes, off = self.carve(off, 4 * HV)
        gl_row, off = self.carve(off, HV)
        gjf, off = self.carve(off, 2 * 128)
        self.a2_end = off
        gjf_sb = gjf
        a2v, a2s, a2g, mm, ev, wr, rd = (S.pipe("a2_v"), S.pipe("a2_s"),
                                         S.pipe("a2_g"), S.pipe("a2_mm"),
                                         S.pipe("a2_ev"), S.pipe("a2_wr"),
                                         S.pipe("a2_rd"))
        self.a2_done = S.pipe("a2_done")
        for j in range(NCH):
            h0 = (j % 2) * 2 * HV
            x1 = lanes[:, h0:h0 + HV]
            t1 = lanes[:, h0 + HV:h0 + 2 * HV]
            bsl = s.ba_tm.ap()[:, j * 64:j * 64 + 32]
            asl = s.ba_tm.ap()[:, j * 64 + 32:j * 64 + 64]
            gj = s.gj_s.ap()[:, j * HV:(j + 1) * HV]
            w_lane = [(mm, 2 * j - 3)] if j >= 2 else []
            S.step("vector", lambda e, x1=x1, asl=asl: e.tensor_tensor(
                x1, asl, s.dtb_bc[:], ALU.add),
                waits=[self.g_ev.at(), self.c_done.at()] + w_lane, incs=[(a2v, 1)])

            def sp_chain(e, x1=x1, t1=t1):
                e.activation(t1, x1, AF.Abs)
                e.activation(t1, t1, AF.Exp, scale=-1.0)
                return e.activation(t1, t1, AF.Ln, bias=1.0)
            S.step("scalar", sp_chain, waits=[a2v.at()], incs=[(a2s, 1)])

            def gfin(e, x1=x1, t1=t1):
                e.tensor_scalar(x1, x1, 0.0, None, ALU.max)
                e.tensor_tensor(t1, x1, t1, ALU.add)
                return e.tensor_tensor(t1, t1, s.negea_bc[:], ALU.mult)
            S.step("vector", gfin, waits=[a2s.at()], incs=[(a2g, 1)])

            bank = self.psum[2 + (j % 2)]

            def cum(e, t1=t1, bank=bank):
                e.matmul(bank[:, 0:HV], s.m_triu, t1, start=True, stop=True)
                return e.matmul(bank[0:1, HV:2 * HV], s.ones_col, t1,
                                start=True, stop=True)
            w_bank = [(ev, j - 1)] if j >= 2 else []
            if j == 0:
                w_bank += [S.pipe("a1_cv").at(), S.pipe("a1_sl").at(),
                           S.pipe("g_mm").at()]
            S.step("tensor", cum, waits=[a2g.at()] + w_bank, incs=[(mm, 1)])

            def evac(e, gj=gj, bank=bank):
                e.tensor_copy(gj, bank[:, 0:HV])
                return e.tensor_copy(gl_row[0:1, :], bank[0:1, HV:2 * HV])
            S.step("vector", evac, waits=[mm.at(), (wr, 32 * max(0, j - 1))],
                   incs=[(ev, 1)])
            tbank = self.psum[6 + (j % 2)]

            def gjtp(e, gj=gj, tbank=tbank):
                return e.transpose(tbank[0:32, 0:128], gj, s.ident_f)
            S.step("tensor", gjtp, waits=[ev.at(), (wr, 32 * max(0, j - 1))],
                   incs=[(mm, 1)])
            S.step("scalar", lambda e, j=j, tbank=tbank: e.copy(
                gjf_sb[0:32, (j % 2) * 128:(j % 2 + 1) * 128], tbank[0:32, 0:128]),
                waits=[mm.at(), (wr, 32 * max(0, j - 1))], incs=[(ev, 1)])
            S.dma("sync", lambda e, j=j: [
                e.dma_start(s.s_gj[j], gjf_sb[0:32, (j % 2) * 128:(j % 2 + 1) * 128]),
                e.dma_start(s.s_gl[j:j + 1, :], gl_row[0:1, :])],
                waits=[ev.at()], pipe=wr, n=2)
            egl = s.egL_s.ap()[:, j * HV:(j + 1) * HV]
            S.dma("sync", lambda e, j=j, egl=egl: e.dma_start(
                egl, s.s_gl[j:j + 1, :].to_broadcast([128, HV])),
                waits=[wr.at()], pipe=rd)

            def deriv_s(e, j=j, gj=gj, bsl=bsl):
                jj = slice(j * HV, (j + 1) * HV)
                e.activation(s.beta_s.ap()[:, jj], bsl, AF.Sigmoid)
                e.activation(s.egjc_s.ap()[:, jj], gj, AF.Exp)
                return e.activation(s.negsc1_s.ap()[:, jj], gj, AF.Exp)
            S.step("scalar", deriv_s, waits=[ev.at(), rd.at()], incs=[(a2s, 1)])

            def deriv_v(e, j=j, gj=gj, egl=egl):
                jj = slice(j * HV, (j + 1) * HV)
                e.tensor_scalar(s.egjc_s.ap()[:, jj], s.egjc_s.ap()[:, jj],
                                C_SCALE, None, ALU.mult)
                e.tensor_tensor(s.egLmg_s.ap()[:, jj], egl, gj, ALU.subtract)
                return e.scalar_tensor_tensor(
                    s.negsc1_s.ap()[:, jj], s.negsc1_s.ap()[:, jj], -1.0,
                    s.beta_s.ap()[:, jj], ALU.mult, ALU.mult)
            S.step("vector", deriv_v, waits=[a2s.at()], incs=[(a2v, 1)])

            def deriv_s2(e, j=j, egl=egl):
                jj = slice(j * HV, (j + 1) * HV)
                e.activation(s.egLmg_s.ap()[:, jj], s.egLmg_s.ap()[:, jj], AF.Exp)
                return e.activation(egl, egl, AF.Exp)
            S.step("scalar", deriv_s2, waits=[a2v.at()], incs=[(self.a2_done, 1)])


    def ph_a3(self):
        s, S = self, self.S
        off = 0
        qf_t, off = self.carve_b(off, 2 * 2048)
        kf_t, off = self.carve_b(off, 2 * 2048)
        kt_t, off = self.carve_b(off, 2 * 2048)
        vt_t, off = self.carve_b(off, 2 * 4096)
        zt_t, off = self.carve_b(off, 2 * 4096)
        gjbc, off = self.carve(off, 2 * 512)
        A0_l, off = self.carve_b(off, 512)
        A0T_l, off = self.carve_b(off, 512)
        Bc_l, off = self.carve_b(off, 512)
        BTc_l, off = self.carve_b(off, 512)
        Bn_l, off = self.carve_b(off, 512)
        BTn_l, off = self.carve_b(off, 512)
        Xc_l, off = self.carve_b(off, 2 * 512)
        vb_l, off = self.carve_b(off, 512)
        kd_l, off = self.carve_b(off, 512)
        iT_l, off = self.carve_b(off, 512)
        X_l, off = self.carve(off, 512)
        em_l, off = self.carve(off, 512)
        em2_l, off = self.carve(off, 512)
        t1_l, off = self.carve(off, 512)
        xg_l, off = self.carve(off, 512)
        rs_l, off = self.carve(off, 16)
        nm_l, off = self.carve_b(off, 512)
        nfm, off = self.carve_b(off, 2 * 4096)
        att, off = self.carve(off, 2 * D)
        sq2, off = self.carve(off, D)
        xr_l, off = self.carve(off, 2 * D)
        wob_l, off = self.carve_b(off, 2 * 1024)
        assert off <= self.ARENA_W, off

        pt, pv, pa = S.pipe("d_pt"), S.pipe("d_pv"), S.pipe("d_pa")
        ld2 = [S.pipe("d_ld0"), S.pipe("d_ld1")]
        gj2 = [S.pipe("d_gj0"), S.pipe("d_gj1")]
        a0t2 = [S.pipe("d_a0t0"), S.pipe("d_a0t1")]
        ntp2 = [S.pipe("d_ntp0"), S.pipe("d_ntp1")]
        wob2 = [S.pipe("d_wob0"), S.pipe("d_wob1")]
        xld2 = [S.pipe("d_xld0"), S.pipe("d_xld1")]
        y1p2 = [S.pipe("d_y10"), S.pipe("d_y11")]
        PS = self.psum

        bar = list(self.a1_done_waits) + [
            p.at() for nm_, p in S._pipes.items()
            if nm_.startswith("a2_") or nm_ == "g_ev"]
        first = {e: True for e in ENGINES}

        def step(eng, fn, waits=(), par=False):
            w = list(waits)
            if first[eng]:
                w = w + bar
                first[eng] = False
            pp = {"tensor": pt, "vector": pv, "scalar": pa}.get(eng)
            return S.step(eng, fn, w, incs=[(pp, 1)] if pp else [], par=par)

        def dma(eng, fn, waits=(), pipe=None, n=1):
            w = list(waits)
            if first[eng]:
                w = w + bar
                first[eng] = False
            return S.dma(eng, fn, w, pipe=pipe, n=n)

        step("vector", lambda e: [e.memset(s.S_f.ap()[:], 0.0),
                                  e.memset(s.S_b.ap()[:], 0.0)])
        sinit_at = pv.at()

        sb_ready = {}
        ld_done = {}
        gj_done = {}
        a0t_prev = {}
        ntp_prev = {}
        wob_prev = {}
        xld_prev = {}
        y1_prev = {}
        ap_free = {}
        lv_free = {}
        gram_free = None
        ps6_free = sinit_at
        lv0_last = None
        out_last = None
        pss_last = None
        a0t_last = None
        ntp_last = None
        wout_last = None
        att_free = None
        nm_done = {}
        NL = 6
        self.y1_done = []

        for j in range(NCH):
            par = j % 2
            qf = qf_t[:, par * 2048:(par + 1) * 2048]
            kf = kf_t[:, par * 2048:(par + 1) * 2048]
            kt = kt_t[:, par * 2048:(par + 1) * 2048]
            vt = vt_t[:, par * 4096:(par + 1) * 4096]
            zt = zt_t[:, par * 4096:(par + 1) * 4096]
            ldp = ld2[par]
            w_ld = ld_done.get(j - 2, [])

            def loads(e, j=j, qf=qf, kf=kf, kt=kt, vt=vt, zt=zt):
                return [
                    e.dma_start(qf.rearrange("p (h c) -> p h c", h=16),
                                s.s_qf[:, :, j * 128:(j + 1) * 128]
                                .transpose([1, 0, 2])),
                    e.dma_start(kf.rearrange("p (h c) -> p h c", h=16),
                                s.s_kf[:, :, j * 128:(j + 1) * 128]
                                .transpose([1, 0, 2])),
                    e.dma_start(kt.rearrange("p (h c) -> p h c", h=16),
                                s.s_kt[:, j].transpose([1, 0, 2])),
                    e.dma_start(vt.rearrange("p (h c) -> p h c", h=32),
                                s.s_vt[:, j].transpose([1, 0, 2])),
                    e.dma_start(zt, s.s_z[j * 128:(j + 1) * 128, :])]
            dma("sync", loads, waits=w_ld, pipe=ldp, n=5)
            ld_at = ldp.at()

            for g in range(8):
                heads = [4 * g + i for i in range(4)]
                pairs = [2 * g, 2 * g + 1]
                gi = j * 8 + g
                gpar = gi % 2
                gjb = gjbc[:, gpar * 512:(gpar + 1) * 512]
                gjp = gj2[gpar]
                w_gj = [gj_done[gi - 2]] if (gi - 2) in gj_done else []
                if (gi - 2) in gj_done:
                    pass

                def gjload(e, j=j, g=g, gjb=gjb):
                    return [e.dma_start(
                        gjb[:, i * 128:(i + 1) * 128],
                        s.s_gj[j, 4 * g + i:4 * g + i + 1, :]
                        .to_broadcast([128, 128]))
                        for i in range(4)]
                dma("gpsimd", gjload, waits=w_gj, pipe=gjp, n=4)
                gj_at = gjp.at()

                def grams(e, kf=kf, qf=qf, pairs=pairs):
                    for i, p in enumerate(pairs):
                        kfp = kf[:, p * 128:(p + 1) * 128]
                        qfp = qf[:, p * 128:(p + 1) * 128]
                        e.matmul(PS[0][:, i * 256:i * 256 + 128], kfp, kfp,
                                 start=True, stop=True)
                        ins = e.matmul(PS[0][:, i * 256 + 128:i * 256 + 256],
                                       kfp, qfp, start=True, stop=True)
                    return ins
                w_gram = [ld_at]
                if gram_free is not None:
                    w_gram.append(gram_free)
                step("tensor", grams, waits=w_gram)
                gram_at = pt.at()

                def prepX(e, j=j, heads=heads, gjb=gjb):
                    for i, h in enumerate(heads):
                        ins = e.tensor_scalar(X_l[:, i * 128:(i + 1) * 128],
                                              gjb[:, i * 128:(i + 1) * 128],
                                              s.pp(s.gj_s, j, h), None,
                                              ALU.subtract)
                    return ins
                S.step("vector", prepX, [gj_at], incs=[(pv, 1)], par=True)

                def prepM(e):
                    for i in range(4):
                        e.tensor_tensor(em_l[:, i * 128:(i + 1) * 128],
                                        X_l[:, i * 128:(i + 1) * 128],
                                        s.m_sl_big, ALU.add)
                        ins = e.tensor_tensor(em2_l[:, i * 128:(i + 1) * 128],
                                              X_l[:, i * 128:(i + 1) * 128],
                                              s.m_u_negbig, ALU.add)
                    return ins
                S.step("vector", prepM, [], incs=[(pv, 1)], par=True)
                prep_at = pv.at()
                gj_done[gi] = (pv, pv.n)

                def exps(e):
                    e.activation(em_l, em_l, AF.Exp, scale=-1.0)
                    return e.activation(em2_l, em2_l, AF.Exp)
                step("scalar", exps, par=True, waits=[prep_at])
                exps_at = pa.at()

                def a0build(e, j=j, heads=heads):
                    for i, h in enumerate(heads):
                        pq = i // 2
                        e.scalar_tensor_tensor(
                            A0_l[:, i * 128:(i + 1) * 128],
                            PS[0][:, pq * 256:pq * 256 + 128],
                            s.pp(s.beta_s, j, h),
                            em_l[:, i * 128:(i + 1) * 128], ALU.mult, ALU.mult)
                        ins = e.tensor_tensor(
                            iT_l[:, i * 128:(i + 1) * 128],
                            PS[0][:, pq * 256 + 128:pq * 256 + 256],
                            em2_l[:, i * 128:(i + 1) * 128], ALU.mult)
                    return ins
                w_a0 = [exps_at, gram_at]
                if a0t_last is not None:
                    w_a0.append(a0t_last)
                if lv0_last is not None:
                    w_a0.append(lv0_last)
                if out_last is not None:
                    w_a0.append(out_last)
                step("vector", a0build, par=True, waits=w_a0)
                a0_at = pv.at()
                gram_free = a0_at

                a0tp = a0t2[gpar]
                w_a0t = [a0_at]
                if lv0_last is not None:
                    w_a0t.append(lv0_last)
                if gpar in a0t_prev:
                    w_a0t.append(a0t_prev[gpar])

                def a0tpf(e):
                    return [e.dma_start_transpose(
                        A0T_l[:, i * 128:(i + 1) * 128],
                        A0_l[:, i * 128:(i + 1) * 128]) for i in range(4)]
                dma("sync", a0tpf, waits=w_a0t, pipe=a0tp, n=4)
                a0t_prev[gpar] = a0tp.at()
                a0t_last = a0tp.at()
                a0t_at = a0tp.at()

                def vbf(e, j=j, heads=heads, vt=vt):
                    for i, h in enumerate(heads):
                        ins = e.tensor_scalar(
                            vb_l[:, i * 128:(i + 1) * 128],
                            vt[:, h * 128:(h + 1) * 128],
                            s.pp(s.beta_s, j, h), None, ALU.mult)
                    return ins
                step("vector", vbf, par=True, waits=[ld_at])
                vb_at = pv.at()

                def psR(e, kf=kf, heads=heads, pairs=pairs):
                    for i, h in enumerate(heads):
                        p = pairs[i // 2]
                        ins = e.matmul(
                            PS[6][:, i * 128:(i + 1) * 128],
                            kf[:, p * 128:(p + 1) * 128],
                            s.S_b.ap()[:, h * 128:(h + 1) * 128],
                            start=True, stop=True)
                    return ins
                w_psr = [ld_at, ps6_free]
                if (j - 1, g) in sb_ready:
                    w_psr.append(sb_ready[(j - 1, g)])
                else:
                    w_psr.append(sinit_at)
                if ap_free.get(1) is not None:
                    w_psr.append(ap_free[1])
                step("tensor", psR, waits=w_psr)
                psr_at = pt.at()

                def xc0(e, j=j, heads=heads):
                    for i, h in enumerate(heads):
                        ins = e.scalar_tensor_tensor(
                            Xc_l[:, i * 128:(i + 1) * 128],
                            PS[6][:, i * 128:(i + 1) * 128],
                            s.pp(s.negsc1_s, j, h),
                            vb_l[:, i * 128:(i + 1) * 128], ALU.mult, ALU.add)
                    return ins
                step("vector", xc0, par=True, waits=[psr_at, vb_at])
                xc_at = pv.at()
                ps6_free = xc_at

                cur_B, cur_BT = A0_l, A0T_l
                cur_ready = a0t_at
                nxt = [(Bn_l, BTn_l), (Bc_l, BTc_l)]
                for k in range(7):
                    apb = PS[5 + (k % 2)]
                    xin = Xc_l[:, (k % 2) * 512:(k % 2) * 512 + 512]
                    xout = Xc_l[:, ((k + 1) % 2) * 512:((k + 1) % 2) * 512 + 512]

                    def apf(e, cur_BT=cur_BT, xin=xin, apb=apb):
                        for i in range(4):
                            ins = e.matmul(apb[:, i * 128:(i + 1) * 128],
                                           cur_BT[:, i * 128:(i + 1) * 128],
                                           xin[:, i * 128:(i + 1) * 128],
                                           start=True, stop=True)
                        return ins
                    w_ap = [cur_ready, xc_at]
                    if ap_free.get(k % 2) is not None:
                        w_ap.append(ap_free[k % 2])
                    if k % 2 == 1:
                        w_ap.append(ps6_free)
                    step("tensor", apf, waits=w_ap)
                    ap_at = pt.at()

                    def apv(e, xin=xin, xout=xout, apb=apb, k=k):
                        return e.tensor_tensor(
                            xout, xin, apb[:],
                            ALU.subtract if k == 0 else ALU.add)
                    step("vector", apv, waits=[ap_at])
                    xc_at = pv.at()
                    ap_free[k % 2] = xc_at
                    if k % 2 == 1:
                        ps6_free = xc_at

                    if k < NL:
                        lvb = (1 + (k % 2) * 2, 2 + (k % 2) * 2)
                        need_B = (k + 1) < NL
                        Bnxt, BTnxt = nxt[k % 2]

                        def lvf(e, cur_B=cur_B, cur_BT=cur_BT, lvb=lvb,
                                need_B=need_B):
                            for i in range(4):
                                if need_B:
                                    e.matmul(
                                        PS[lvb[0]][:, i * 128:(i + 1) * 128],
                                        cur_BT[:, i * 128:(i + 1) * 128],
                                        cur_B[:, i * 128:(i + 1) * 128],
                                        start=True, stop=True)
                                ins = e.matmul(
                                    PS[lvb[1]][:, i * 128:(i + 1) * 128],
                                    cur_B[:, i * 128:(i + 1) * 128],
                                    cur_BT[:, i * 128:(i + 1) * 128],
                                    start=True, stop=True)
                            return ins
                        w_lv = [cur_ready]
                        if lv_free.get(k % 2) is not None:
                            w_lv.append(lv_free[k % 2])
                        step("tensor", lvf, waits=w_lv)
                        lv_at = pt.at()
                        if k == 0:
                            lv0_last = lv_at

                        def lvev(e, lvb=lvb, Bnxt=Bnxt, BTnxt=BTnxt,
                                 need_B=need_B):
                            if need_B:
                                e.copy(Bnxt, PS[lvb[0]][:])
                            return e.copy(BTnxt, PS[lvb[1]][:])
                        step("scalar", lvev, par=True, waits=[lv_at])
                        lvev_at = pa.at()
                        lv_free[k % 2] = lvev_at
                        cur_B, cur_BT = Bnxt, BTnxt
                        cur_ready = lvev_at
                v_new = Xc_l[:, 512:1024]
                vnew_at = xc_at

                def qSf(e, qf=qf, heads=heads, pairs=pairs):
                    for i, h in enumerate(heads):
                        p = pairs[i // 2]
                        ins = e.matmul(
                            PS[6][:, i * 128:(i + 1) * 128],
                            qf[:, p * 128:(p + 1) * 128],
                            s.S_b.ap()[:, h * 128:(h + 1) * 128],
                            start=True, stop=True)
                    return ins
                w_qs = [ps6_free, ap_free[1]]
                if (j - 1, g) in sb_ready:
                    w_qs.append(sb_ready[(j - 1, g)])
                else:
                    w_qs.append(sinit_at)
                step("tensor", qSf, waits=w_qs)
                qs_at = pt.at()

                def t1f(e, j=j, heads=heads):
                    for i, h in enumerate(heads):
                        ins = e.tensor_scalar(
                            t1_l[:, i * 128:(i + 1) * 128],
                            PS[6][:, i * 128:(i + 1) * 128],
                            s.pp(s.egjc_s, j, h), None, ALU.mult)
                    return ins
                step("vector", t1f, par=True, waits=[qs_at])
                t1_at = pv.at()

                def outf(e, v_new=v_new):
                    for i in range(4):
                        ins = e.matmul(PS[6][:, i * 128:(i + 1) * 128],
                                       iT_l[:, i * 128:(i + 1) * 128],
                                       v_new[:, i * 128:(i + 1) * 128],
                                       start=True, stop=True)
                    return ins
                step("tensor", outf, waits=[t1_at, vnew_at, a0_at])
                out_at = pt.at()
                out_last = out_at

                def kdf(e, j=j, heads=heads, kt=kt, pairs=pairs):
                    for i, h in enumerate(heads):
                        p = pairs[i // 2]
                        ins = e.tensor_scalar(
                            kd_l[:, i * 128:(i + 1) * 128],
                            kt[:, p * 128:(p + 1) * 128],
                            s.pp(s.egLmg_s, j, h), None, ALU.mult)
                    return ins
                w_kd = [ld_at]
                if pss_last is not None:
                    w_kd.append(pss_last)
                step("vector", kdf, par=True, waits=w_kd)
                kd_at = pv.at()

                def psSf(e, v_new=v_new):
                    for i in range(4):
                        ins = e.matmul(PS[0][:, i * 128:(i + 1) * 128],
                                       kd_l[:, i * 128:(i + 1) * 128],
                                       v_new[:, i * 128:(i + 1) * 128],
                                       start=True, stop=True)
                    return ins
                step("tensor", psSf, waits=[kd_at, vnew_at, gram_free])
                pss_at = pt.at()
                pss_last = pss_at

                def supd1(e, j=j, heads=heads):
                    for i, h in enumerate(heads):
                        ins = e.scalar_tensor_tensor(
                            s.S_f.ap()[:, h * 128:(h + 1) * 128],
                            s.S_f.ap()[:, h * 128:(h + 1) * 128],
                            s.pp(s.egL_s, j, h),
                            PS[0][:, i * 128:(i + 1) * 128], ALU.mult, ALU.add)
                    return ins
                S.step("vector", supd1, [pss_at], incs=[(pv, 1)], par=True)

                def supd(e, heads=heads):
                    return e.tensor_copy(
                        s.S_b.ap()[:, heads[0] * 128:(heads[0] + 4) * 128],
                        s.S_f.ap()[:, heads[0] * 128:(heads[0] + 4) * 128])
                step("vector", supd, waits=[])
                sb_ready[(j, g)] = pv.at()
                gram_free = pv.at()

                def xgf1(e):
                    for i in range(4):
                        ins = e.scalar_tensor_tensor(
                            t1_l[:, i * 128:(i + 1) * 128],
                            PS[6][:, i * 128:(i + 1) * 128],
                            C_SCALE,
                            t1_l[:, i * 128:(i + 1) * 128], ALU.mult, ALU.add)
                    return ins
                S.step("vector", xgf1, [out_at, t1_at], incs=[(pv, 1)], par=True)

                def xgf(e, g=g, zt=zt):
                    return e.tensor_tensor(
                        xg_l, t1_l, zt[:, g * 512:(g + 1) * 512], ALU.mult)
                step("vector", xgf, waits=[])
                xg_at = pv.at()
                ps6_free = xg_at

                step("scalar", lambda e: e.activation(em_l, xg_l, AF.Square),
                     waits=[xg_at])
                sq_at = pa.at()

                def redf(e):
                    e.tensor_reduce(rs_l[:, 0:4],
                                    em_l.rearrange("p (s c) -> p s c", s=4),
                                    AX.X, ALU.add)
                    return e.tensor_scalar(rs_l[:, 0:4], rs_l[:, 0:4],
                                           RMS_EPS * UV, None, ALU.add)
                step("vector", redf, waits=[sq_at])
                red_at = pv.at()

                def rsqf(e):
                    e.activation(rs_l[:, 4:8], rs_l[:, 0:4], AF.Ln,
                                 scale=1.0 / UV)
                    return e.activation(rs_l[:, 4:8], rs_l[:, 4:8], AF.Exp,
                                        scale=-0.5)
                step("scalar", rsqf, waits=[red_at])
                rsq_at = pa.at()

                def nrmf(e):
                    for i in range(4):
                        ins = e.scalar_tensor_tensor(
                            nm_l[:, i * 128:(i + 1) * 128],
                            xg_l[:, i * 128:(i + 1) * 128],
                            rs_l[:, 4 + i:5 + i],
                            s.normw_bc[:], ALU.mult, ALU.mult)
                    return ins
                w_nrm = [rsq_at]
                if ntp_last is not None:
                    w_nrm.append(ntp_last)
                step("vector", nrmf, par=True, waits=w_nrm)
                nrm_at = pv.at()

                ntpp = ntp2[gpar]
                w_ntp = [nrm_at]
                if gpar in ntp_prev:
                    w_ntp.append(ntp_prev[gpar])
                if wout_last is not None:
                    w_ntp.append(wout_last)
                half = j % 2

                def ntpf(e, g=g, half=half):
                    return [e.dma_start_transpose(
                        nfm[:, half * 4096 + (4 * g + i) * 128:
                            half * 4096 + (4 * g + i + 1) * 128],
                        nm_l[:, i * 128:(i + 1) * 128]) for i in range(4)]
                dma("sync", ntpf, waits=w_ntp, pipe=ntpp, n=4)
                ntp_prev[gpar] = ntpp.at()
                ntp_last = ntpp.at()

            ld_done[j] = [(pt, pt.n), (pv, pv.n)]
            nm_done[j] = (ntp2[0].at(), ntp2[1].at())

            if j % 2 == 1:
                m2 = j // 2
                w_nfm = [nm_done[j - 1][0], nm_done[j - 1][1],
                         nm_done[j][0], nm_done[j][1]]
                ev_at = None
                for ds in range(8):
                    for vq in range(8):
                        wpar = vq % 2
                        wobp = wob2[wpar]
                        wsl = wob_l[:, wpar * 1024:(wpar + 1) * 1024]
                        w_wob = []
                        if (m2 * 64 + ds * 8 + vq) >= 2:
                            w_wob.append(wob_prev[wpar])

                        def wobl(e, ds=ds, vq=vq, wsl=wsl):
                            return e.dma_start(
                                wsl.rearrange("p (v c) -> p v c", v=4),
                                s.s_wob2[vq, :, :, ds * 256:(ds + 1) * 256])
                        dma("sync", wobl, waits=w_wob, pipe=wobp)
                        wob_at = wobp.at()

                        def wmm(e, ds=ds, vq=vq, wsl=wsl):
                            tgb = [PS[7], PS[1]]
                            for i in range(4):
                                for tg in range(2):
                                    ins = e.matmul(
                                        tgb[tg][:, 0:256],
                                        nfm[:, tg * 4096 +
                                            (vq * 4 + i) * 128:
                                            tg * 4096 + (vq * 4 + i + 1) * 128],
                                        wsl[:, i * 256:(i + 1) * 256],
                                        start=(vq == 0 and i == 0),
                                        stop=(vq == 7 and i == 3))
                            return ins
                        w_mm = [wob_at] + w_nfm
                        if vq == 0 and ev_at is not None:
                            w_mm.append(ev_at)
                        if vq == 0 and att_free is not None:
                            w_mm.append(att_free)
                        if ds == 0 and vq == 0:
                            w_mm += [v for v in (lv_free.get(0), lv_free.get(1))
                                     if v is not None]
                        step("tensor", wmm, waits=w_mm)
                        wob_prev[wpar] = (pt, pt.n)
                    wmm_at = pt.at()

                    def wev(e, ds=ds):
                        e.copy(att[:, 0 * 2048 + ds * 256:
                                   0 * 2048 + (ds + 1) * 256],
                               PS[7][:, 0:256])
                        return e.copy(att[:, 1 * 2048 + ds * 256:
                                          1 * 2048 + (ds + 1) * 256],
                                      PS[1][:, 0:256])
                    step("scalar", wev, waits=[wmm_at])
                    ev_at = pa.at()
                wout_last = pt.at()
                lv_free[0] = ev_at

                for tg in range(2):
                    grp = 2 * m2 + tg
                    xpar = grp % 2
                    xlp = xld2[xpar]
                    xrl = xr_l[:, xpar * D:(xpar + 1) * D]
                    y1l = xrl
                    w_x = []
                    if xpar in y1_prev:
                        w_x.append(y1_prev[xpar])

                    def xload(e, grp=grp, xrl=xrl):
                        return e.dma_start(
                            xrl, s.x_in[grp * 128:(grp + 1) * 128, :])
                    dma("sync", xload, waits=w_x, pipe=xlp)
                    x_at = xlp.at()
                    atg = att[:, tg * 2048:(tg + 1) * 2048]

                    def lnsq(e, atg=atg):
                        return e.activation(sq2, atg, AF.Square)
                    step("scalar", lnsq, waits=[ev_at])
                    lsq_at = pa.at()

                    def lnst(e, atg=atg):
                        e.tensor_reduce(rs_l[:, 8:9], atg, AX.X, ALU.add)
                        e.tensor_reduce(rs_l[:, 9:10], sq2, AX.X, ALU.add)
                        e.tensor_scalar(rs_l[:, 10:11], rs_l[:, 8:9], 1.0 / D,
                                        None, ALU.mult)
                        e.tensor_scalar(rs_l[:, 11:12], rs_l[:, 9:10], 1.0 / D,
                                        None, ALU.mult)
                        e.tensor_tensor(rs_l[:, 12:13], rs_l[:, 10:11],
                                        rs_l[:, 10:11], ALU.mult)
                        e.tensor_tensor(rs_l[:, 11:12], rs_l[:, 11:12],
                                        rs_l[:, 12:13], ALU.subtract)
                        e.tensor_scalar(rs_l[:, 11:12], rs_l[:, 11:12], LN_EPS,
                                        None, ALU.add)
                        return e.tensor_scalar(rs_l[:, 10:11], rs_l[:, 10:11],
                                               -1.0, None, ALU.mult)
                    step("vector", lnst, waits=[lsq_at])
                    lst_at = pv.at()

                    def lnrs(e):
                        e.activation(rs_l[:, 11:12], rs_l[:, 11:12], AF.Ln)
                        return e.activation(rs_l[:, 11:12], rs_l[:, 11:12],
                                            AF.Exp, scale=-0.5)
                    step("scalar", lnrs, waits=[lst_at])
                    lrs_at = pa.at()

                    def lnap(e, atg=atg, xrl=xrl):
                        e.tensor_scalar(atg, atg, rs_l[:, 10:11],
                                        rs_l[:, 11:12], ALU.add, ALU.mult)
                        return e.tensor_tensor(xrl, atg, xrl, ALU.add)
                    w_lap = [lrs_at, x_at]
                    if xpar in y1_prev:
                        w_lap.append(y1_prev[xpar])
                    step("vector", lnap, waits=w_lap)
                    lap_at = pv.at()
                    if tg == 1:
                        att_free = lap_at

                    def y1st(e, grp=grp, y1l=y1l):
                        return e.dma_start(
                            s.s_y1[grp * 128:(grp + 1) * 128, :], y1l)
                    dma("sync", y1st, waits=[lap_at], pipe=y1p2[xpar])
                    y1_prev[xpar] = y1p2[xpar].at()
                    self.y1_done.append(y1p2[xpar].at())

    def ph_b(self):
        s, S = self, self.S
        off = 0
        xn2, off = self.carve_b(off, 16 * 1024)
        oacc, off = self.carve_b(off, 16 * 1024)
        pt_off = off
        Pt, off = self.carve_b(off, 16 * 1024)
        sq2 = self.arena.ap()[:, pt_off:pt_off + D]
        y1t, off = self.carve(off, 2 * D)
        x2tm, off = self.carve_b(off, D)
        st3, off = self.carve(off, 32)
        w1c, off = self.carve_b(off, 2 * 2048)
        wgc, off = self.carve_b(off, 2 * 2048)
        w2c, off = self.carve_b(off, 2 * 2048)
        wbc, off = self.carve_b(off, 2 * 1024)
        gw_f, off = self.carve(off, 128)
        gw_b, off = self.carve_b(off, 128)
        gl_t, off = self.carve(off, 10 * 8)
        l9b, off = self.carve_b(off, 8)
        t1b, off = self.carve_b(off, 512)
        gbb, off = self.carve_b(off, 512)
        assert off <= self.ARENA_W, off
        xn2_3 = xn2.rearrange("p (k t) -> p k t", k=16)
        oacc3 = oacc.rearrange("p (k t) -> p k t", k=16)
        Pt3 = Pt.rearrange("p (k t) -> p k t", k=16)

        pt, pv, pa = S.pipe("b_pt"), S.pipe("b_pv"), S.pipe("b_pa")
        yld2 = [S.pipe("b_yld0"), S.pipe("b_yld1")]
        xtp = S.pipe("b_xtp")
        wld2 = [S.pipe("b_wld0"), S.pipe("b_wld1")]
        w2ld2 = [S.pipe("b_w2ld0"), S.pipe("b_w2ld1")]
        wgl = S.pipe("b_wgl")
        wwr = S.pipe("b_wwr")
        wbc2p = [S.pipe("b_wbc0"), S.pipe("b_wbc1")]
        yst2 = [S.pipe("b_yst0"), S.pipe("b_yst1")]
        PS = self.psum

        bar = [p.at() for p in S._pipes.values() if p.n > 0]
        first = {e: True for e in ENGINES}

        def step(eng, fn, waits=(), par=False):
            w = list(waits)
            if first[eng]:
                w = w + bar
                first[eng] = False
            pp = {"tensor": pt, "vector": pv, "scalar": pa}.get(eng)
            return S.step(eng, fn, w, incs=[(pp, 1)] if pp else [], par=par)

        def dma(eng, fn, waits=(), pipe=None, n=1):
            w = list(waits)
            if first[eng]:
                w = w + bar
                first[eng] = False
            return S.dma(eng, fn, w, pipe=pipe, n=n)

        dma("sync", lambda e: e.dma_start(
            gw_f.rearrange("p (k n) -> p k n", k=16),
            s.gate_w.rearrange("(k p) n -> p k n", p=128)), pipe=wgl)
        step("vector", lambda e: e.tensor_copy(gw_b, gw_f), waits=[wgl.at()])
        gwc_at = pv.at()
        gw_b3 = gw_b.rearrange("p (k n) -> p k n", k=16)

        y1_prev = {}
        wld_prev = {}
        w2_prev = {}
        wbc_prev = {}
        yst_prev = {}
        xtp_last = None
        pe_xn_last = None
        oacc_free = None

        for t in range(NFT):
            for gg in range(8):
                grp = t * 8 + gg
                par = grp % 2
                ylp = yld2[par]
                yl = y1t[:, par * D:(par + 1) * D]
                w_y = [y1_prev[par]] if par in y1_prev else []

                def yload(e, grp=grp, yl=yl):
                    return e.dma_start(yl, s.s_y1[grp * 128:(grp + 1) * 128, :])
                dma("sync", yload, waits=w_y, pipe=ylp)
                y_at = ylp.at()

                w_sq = [y_at]
                if pa.n:
                    w_sq.append((pa, pa.n))
                if pe_xn_last is not None:
                    w_sq.append(pe_xn_last)
                step("scalar", lambda e, yl=yl: e.activation(sq2, yl, AF.Square),
                     waits=w_sq)
                sq_at = pa.at()

                def lnst(e, yl=yl):
                    e.tensor_reduce(st3[:, 0:1], yl, AX.X, ALU.add)
                    e.tensor_reduce(st3[:, 1:2], sq2, AX.X, ALU.add)
                    e.tensor_scalar(st3[:, 2:3], st3[:, 0:1], 1.0 / D, None,
                                    ALU.mult)
                    e.tensor_scalar(st3[:, 3:4], st3[:, 1:2], 1.0 / D, None,
                                    ALU.mult)
                    e.tensor_tensor(st3[:, 4:5], st3[:, 2:3], st3[:, 2:3],
                                    ALU.mult)
                    e.tensor_tensor(st3[:, 5:6], st3[:, 3:4], st3[:, 4:5],
                                    ALU.subtract)
                    e.tensor_scalar(st3[:, 5:6], st3[:, 5:6], LN_EPS, None,
                                    ALU.add)
                    return e.tensor_scalar(st3[:, 6:7], st3[:, 2:3], -1.0, None,
                                           ALU.mult)
                step("vector", lnst, waits=[sq_at])
                lst_at = pv.at()

                def lnrs(e):
                    e.activation(st3[:, 5:6], st3[:, 5:6], AF.Ln)
                    return e.activation(st3[:, 5:6], st3[:, 5:6], AF.Exp,
                                        scale=-0.5)
                step("scalar", lnrs, waits=[lst_at])
                lrs_at = pa.at()

                w_ap = [lrs_at]
                if xtp_last is not None:
                    w_ap.append(xtp_last)
                step("vector", lambda e, yl=yl: e.tensor_scalar(
                    x2tm, yl, st3[:, 6:7], st3[:, 5:6], ALU.add, ALU.mult),
                    waits=w_ap)
                ap_at = pv.at()

                w_tp = [ap_at]
                if gg == 0 and pe_xn_last is not None:
                    w_tp.append(pe_xn_last)

                def xtpf(e, gg=gg):
                    return [e.dma_start_transpose(
                        xn2_3[:, k, gg * 128:(gg + 1) * 128],
                        x2tm[:, k * 128:(k + 1) * 128]) for k in range(16)]
                dma("sync", xtpf, waits=w_tp, pipe=xtp, n=16)
                xtp_last = xtp.at()

                def gmm(e, gg=gg):
                    for k in range(16):
                        ins = e.matmul(PS[6][:, 0:8],
                                       xn2_3[:, k, gg * 128:(gg + 1) * 128],
                                       gw_b3[:, k, :], start=(k == 0),
                                       stop=(k == 15))
                    return ins
                w_gmm = [xtp_last, gwc_at]
                if pv.n:
                    w_gmm.append((pv, pv.n))
                step("tensor", gmm, waits=w_gmm)
                gmm_at = pt.at()

                L = lambda i: gl_t[:, i * 8:(i + 1) * 8]

                def gch1(e):
                    e.tensor_tensor(L(0), PS[6][:, 0:8], s.gateb_bc[:], ALU.add)
                    e.tensor_reduce(L(1)[:, 0:1], L(0), AX.X, ALU.max)
                    return e.tensor_scalar(L(1)[:, 1:2], L(1)[:, 0:1], -1.0,
                                           None, ALU.mult)
                step("vector", gch1, waits=[gmm_at, (pa, pa.n)])
                g1_at = pv.at()
                step("scalar", lambda e: e.activation(
                    L(2), L(0), AF.Exp, bias=L(1)[:, 1:2]), waits=[g1_at])
                g2_at = pa.at()

                def gch2(e):
                    e.tensor_reduce(L(1)[:, 2:3], L(2), AX.X, ALU.add)
                    e.reciprocal(L(1)[:, 3:4], L(1)[:, 2:3])
                    e.tensor_scalar(L(3), L(2), L(1)[:, 3:4], None, ALU.mult)
                    e.tensor_tensor(L(4), L(3), s.biases_bc[:], ALU.add)
                    e.tensor_reduce(L(1)[:, 4:5], L(4), AX.X, ALU.max)
                    e.tensor_scalar(L(5), L(4), L(1)[:, 4:5], None, ALU.is_ge)
                    e.scalar_tensor_tensor(L(6), L(5), -BIG, L(4), ALU.mult,
                                           ALU.add)
                    e.tensor_reduce(L(1)[:, 5:6], L(6), AX.X, ALU.max)
                    e.tensor_scalar(L(7), L(6), L(1)[:, 5:6], None, ALU.is_ge)
                    e.scalar_tensor_tensor(L(6), L(7), -BIG, L(6), ALU.mult,
                                           ALU.add)
                    e.tensor_reduce(L(1)[:, 6:7], L(6), AX.X, ALU.max)
                    e.tensor_scalar(L(8), L(4), L(1)[:, 6:7], None, ALU.is_ge)
                    e.tensor_tensor(L(9), L(8), L(3), ALU.mult)
                    e.tensor_reduce(L(1)[:, 7:8], L(9), AX.X, ALU.add)
                    e.reciprocal(L(1)[:, 7:8], L(1)[:, 7:8])
                    e.tensor_scalar(L(9), L(9), L(1)[:, 7:8], None, ALU.mult)
                    return e.tensor_copy(l9b, L(9))
                step("vector", gch2, waits=[g2_at, wwr.at()])
                g3_at = pv.at()

                def wwrite(e, t=t, grp=grp, gg=gg):
                    return [e.dma_start(
                        s.s_wg8[t, ee, gg * 128:(gg + 1) * 128],
                        l9b[:, ee:ee + 1]) for ee in range(8)]
                dma("sync", wwrite, waits=[g3_at], pipe=wwr, n=8)
                y1_prev[par] = wwr.at()
            ln_done = [(pv, pv.n), (pa, pa.n), xtp_last, wwr.at()]

            for e_ in range(8):
                shared = (e_ == 0)
                wpar = e_ % 2
                wsl = wbc[:, wpar * 1024:(wpar + 1) * 1024]
                if not shared:
                    w_wbc = [wwr.at()]
                    if wbc_prev.get(wpar) is not None:
                        w_wbc.append(wbc_prev[wpar])

                    def wbcl(e, t=t, e_=e_, wsl=wsl):
                        return e.dma_start(
                            wsl, s.s_wg8[t, e_ - 1:e_, :].to_broadcast(
                                [128, 1024]))
                    dma("gpsimd", wbcl, waits=w_wbc, pipe=wbc2p[wpar])
                    wbc_at = wbc2p[wpar].at()
                else:
                    wbc_at = None

                slots = ((14, 15, 7) if shared
                         else (e_ - 1, NEXP + e_ - 1, e_ - 1))
                for hb in range(16):
                    u = e_ * 16 + hb
                    lpar = u % 2
                    w1l = w1c[:, lpar * 2048:(lpar + 1) * 2048]
                    wgll = wgc[:, lpar * 2048:(lpar + 1) * 2048]
                    wlp = wld2[lpar]
                    w_wl = [wld_prev[lpar]] if lpar in wld_prev else []

                    def wload(e, slots=slots, hb=hb, w1l=w1l, wgll=wgll):
                        return [
                            e.dma_start(w1l, s.s_fab[slots[0], hb]),
                            e.dma_start(wgll, s.s_fab[slots[1], hb])]
                    dma("sync", wload, waits=w_wl, pipe=wlp, n=2)
                    wl_at = wlp.at()

                    w1l3 = w1l.rearrange("p (k n) -> p k n", k=16)
                    wgl3 = wgll.rearrange("p (k n) -> p k n", k=16)

                    def mmAG(e, w1l3=w1l3, wgl3=wgl3):
                        for half in range(2):
                            for k in range(16):
                                e.matmul(PS[0 + half][:],
                                         w1l3[:, k, :],
                                         xn2_3[:, k, half * 512:(half + 1) * 512],
                                         start=(k == 0), stop=(k == 15))
                        for half in range(2):
                            for k in range(16):
                                ins = e.matmul(
                                    PS[2 + half][:],
                                    wgl3[:, k, :],
                                    xn2_3[:, k, half * 512:(half + 1) * 512],
                                    start=(k == 0), stop=(k == 15))
                        return ins
                    w_mm = [wl_at, ln_done[2]]
                    if u >= 1:
                        w_mm.append(pbuild_at)
                    if u == 0:
                        w_mm += ln_done[:2]
                    step("tensor", mmAG, waits=w_mm)
                    mm_at = pt.at()
                    wld_prev[lpar] = (pt, pt.n)

                    def pbuild(e, hb=hb, wsl=wsl, shared=shared):
                        for half in range(2):
                            e.activation(t1b, PS[0 + half][:], AF.Silu)
                            if shared:
                                e.copy(gbb, PS[2 + half][:])
                            else:
                                pass
                            ins = e.activation(t1b, t1b, AF.Identity)
                        return ins
                    def pact(e):
                        e.activation(t1b, PS[0][:], AF.Silu)
                        return e.activation(t1b[:, 0:0] if False else gbb,
                                            PS[1][:], AF.Silu)
                    def silu0(e):
                        return e.activation(t1b, PS[0][:], AF.Silu)
                    step("scalar", silu0, waits=[mm_at])
                    s0_at = pa.at()

                    def gp0(e, hb=hb, wsl=wsl, shared=shared):
                        if shared:
                            e.tensor_copy(gbb, PS[2][:])
                        else:
                            e.tensor_tensor(gbb, PS[2][:], wsl[:, 0:512],
                                            ALU.mult)
                        return e.tensor_tensor(Pt3[:, hb, 0:512], t1b, gbb,
                                               ALU.mult)
                    w_g0 = [mm_at, s0_at]
                    if wbc_at is not None:
                        w_g0.append(wbc_at)
                    step("vector", gp0, waits=w_g0)

                    def silu1(e):
                        return e.activation(t1b, PS[1][:], AF.Silu)
                    step("scalar", silu1, waits=[(pv, pv.n)])
                    s1_at = pa.at()

                    def gp1(e, hb=hb, wsl=wsl, shared=shared):
                        if shared:
                            e.tensor_copy(gbb, PS[3][:])
                        else:
                            e.tensor_tensor(gbb, PS[3][:], wsl[:, 512:1024],
                                            ALU.mult)
                        return e.tensor_tensor(Pt3[:, hb, 512:1024], t1b, gbb,
                                               ALU.mult)
                    step("vector", gp1, waits=[s1_at])
                    pbuild_at = pv.at()
                P_at = pbuild_at
                wbc_prev[e_ % 2] = (pv, pv.n)

                for db in range(16):
                    u2 = e_ * 16 + db
                    lpar = u2 % 2
                    w2l = w2c[:, lpar * 2048:(lpar + 1) * 2048]
                    w2p = w2ld2[lpar]
                    w_w2 = [w2_prev[lpar]] if lpar in w2_prev else []

                    def w2load(e, slots=slots, db=db, w2l=w2l):
                        return e.dma_start(w2l, s.s_f2b[slots[2], db])
                    dma("sync", w2load, waits=w_w2, pipe=w2p)
                    w2_at = w2p.at()
                    w2l3 = w2l.rearrange("p (k n) -> p k n", k=16)

                    def mmO(e, w2l3=w2l3):
                        for half in range(2):
                            for k in range(16):
                                ins = e.matmul(
                                    PS[4 + half][:], w2l3[:, k, :],
                                    Pt3[:, k, half * 512:(half + 1) * 512],
                                    start=(k == 0), stop=(k == 15))
                        return ins
                    w_mo = [w2_at, P_at]
                    if db >= 1 or e_ >= 1:
                        w_mo.append(oadd_at)
                    if oacc_free is not None and e_ == 0 and db == 0:
                        w_mo.append(oacc_free)
                    step("tensor", mmO, waits=w_mo)
                    mo_at = pt.at()
                    w2_prev[lpar] = (pt, pt.n)

                    def oadd(e, db=db, shared=shared):
                        if shared:
                            e.tensor_copy(oacc3[:, db, 0:512], PS[4][:])
                            return e.tensor_copy(oacc3[:, db, 512:1024],
                                                 PS[5][:])
                        e.tensor_tensor(oacc3[:, db, 0:512],
                                        oacc3[:, db, 0:512], PS[4][:], ALU.add)
                        return e.tensor_tensor(oacc3[:, db, 512:1024],
                                               oacc3[:, db, 512:1024],
                                               PS[5][:], ALU.add)
                    step("vector", oadd, par=True, waits=[mo_at])
                    oadd_at = pv.at()
            pe_xn_last = (pt, pt.n)
            if "dbg_oacc" in self.debug:
                if not hasattr(self, "_dbg_oacc"):
                    self._dbg_oacc = self.dram("dbg_oacc", [NFT, 128, 16 * 1024],
                                               BF16)
                    self._dbg_op = S.pipe("dbg_oacc_p")
                dma("gpsimd", lambda e, t=t: e.dma_start(
                    self._dbg_oacc[t], oacc), waits=[oadd_at],
                    pipe=self._dbg_op)

            for gg in range(8):
                grp = t * 8 + gg
                par = grp % 2
                ylp = yld2[par]
                yl = y1t[:, par * D:(par + 1) * D]

                def yload2(e, grp=grp, yl=yl):
                    return e.dma_start(yl, s.s_y1[grp * 128:(grp + 1) * 128, :])
                dma("sync", yload2, waits=[y1_prev[par], oadd_at], pipe=ylp)
                y_at = ylp.at()

                def otpf(e, gg=gg):
                    return [e.dma_start_transpose(
                        x2tm[:, k * 128:(k + 1) * 128],
                        oacc3[:, k, gg * 128:(gg + 1) * 128]) for k in range(16)]
                w_otp = [oadd_at, (pv, pv.n)]
                if xtp_last is not None:
                    w_otp.append(xtp_last)
                dma("sync", otpf, waits=w_otp, pipe=xtp, n=16)
                xtp_last = xtp.at()

                def cst(e):
                    return e.tensor_copy(sq2, x2tm)
                w_cst = [xtp_last]
                if pa.n:
                    w_cst.append((pa, pa.n))
                step("vector", cst, waits=w_cst)
                cst_at = pv.at()

                def lnsq2(e):
                    return e.activation(st3[:, 8:9] if False else sq2, sq2,
                                        AF.Square, accum_out=None)
                def lnst2a(e):
                    return e.tensor_reduce(st3[:, 8:9], sq2, AX.X, ALU.add)
                step("vector", lnst2a, waits=[cst_at])

                def lnsq2b(e):
                    return e.activation(y1t[:, 0:0] if False else sq2, sq2,
                                        AF.Square)
                step("scalar", lnsq2b, waits=[(pv, pv.n)])
                sqb_at = pa.at()

                def lnst2b(e):
                    e.tensor_reduce(st3[:, 9:10], sq2, AX.X, ALU.add)
                    e.tensor_scalar(st3[:, 10:11], st3[:, 8:9], 1.0 / D, None,
                                    ALU.mult)
                    e.tensor_scalar(st3[:, 11:12], st3[:, 9:10], 1.0 / D, None,
                                    ALU.mult)
                    e.tensor_tensor(st3[:, 12:13], st3[:, 10:11], st3[:, 10:11],
                                    ALU.mult)
                    e.tensor_tensor(st3[:, 11:12], st3[:, 11:12], st3[:, 12:13],
                                    ALU.subtract)
                    e.tensor_scalar(st3[:, 11:12], st3[:, 11:12], LN_EPS, None,
                                    ALU.add)
                    return e.tensor_scalar(st3[:, 10:11], st3[:, 10:11], -1.0,
                                           None, ALU.mult)
                step("vector", lnst2b, waits=[sqb_at])
                l2b_at = pv.at()

                def lnrs2(e):
                    e.activation(st3[:, 11:12], st3[:, 11:12], AF.Ln)
                    return e.activation(st3[:, 11:12], st3[:, 11:12], AF.Exp,
                                        scale=-0.5)
                step("scalar", lnrs2, waits=[l2b_at])
                lrs2_at = pa.at()

                def fin(e, yl=yl):
                    e.tensor_copy(sq2, x2tm)
                    e.tensor_scalar(sq2, sq2, st3[:, 10:11], st3[:, 11:12],
                                    ALU.add, ALU.mult)
                    return e.tensor_tensor(yl, sq2, yl, ALU.add)
                w_fin = [lrs2_at, y_at]
                if par in yst_prev:
                    w_fin.append(yst_prev[par])
                step("vector", fin, waits=w_fin)
                fin_at = pv.at()

                def yst(e, grp=grp, yl=yl):
                    return e.dma_start(s.y_out[grp * 128:(grp + 1) * 128, :],
                                       yl)
                ystp = yst2[par]
                dma("sync", yst, waits=[fin_at], pipe=ystp)
                y1_prev[par] = ystp.at()
                yst_prev[par] = ystp.at()
            oacc_free = (pv, pv.n)
        self.fin_waits = [yst2[0].at(), yst2[1].at()]

    def ph_final(self):
        s, S = self, self.S
        if "dbg_scal" in self.debug:
            dbg = self.dram("dbg_scal", [7, 128, NG * 64], F32)
            dpipe = S.pipe("dbg")
            waits = [p.at() for p in S._pipes.values() if p.n > 0]
            tens = [s.beta_s, s.negsc1_s, s.egjc_s, s.egLmg_s, s.egL_s, s.gj_s,
                    s.ba_tm]
            S.dma("gpsimd", lambda e: [
                e.dma_start(dbg[i, :, :t.ap().shape[1]], t.ap()[:])
                for i, t in enumerate(tens)], waits=waits, pipe=dpipe, n=7)
        waits = [p.at() for p in S._pipes.values() if p.n > 0]
        S.step("sync", lambda e: e.nop(), waits=waits)


def stage_inputs(inputs):
    f32 = np.float32
    sl = np.tril(np.ones((128, 128), f32), -1)
    ui = np.triu(np.ones((128, 128), f32))
    c_masks = np.stack([
        BIG * (1.0 - sl),
        -BIG * (1.0 - ui),
        ui,
        np.eye(128, dtype=f32),
    ]).astype(f32)
    pad_col = np.zeros((D, 1), f32)
    neg = np.full((1,), -1.0e30, f32)
    shared = {
        "w_qkv": np.ascontiguousarray(inputs["L0_attn_W_qkv"]),
        "w_z": np.ascontiguousarray(inputs["L0_attn_W_z"]),
        "w_ba": np.ascontiguousarray(
            np.concatenate([inputs["L0_attn_W_b"], inputs["L0_attn_W_a"]], 1)),
        "conv_w": np.ascontiguousarray(inputs["L0_attn_conv_weight"]),
        "dt_bias_r": np.ascontiguousarray(inputs["L0_attn_dt_bias"][None, :]),
        "a_log_r": np.ascontiguousarray(inputs["L0_attn_A_log"][None, :]),
        "norm_w_r": np.ascontiguousarray(inputs["L0_attn_norm_w"][None, :]),
        "w_out": np.ascontiguousarray(inputs["L0_attn_W_out"]),
        "ffn_w1": np.ascontiguousarray(inputs["L0_ffn_W1"]),
        "ffn_wg": np.ascontiguousarray(inputs["L0_ffn_Wg"]),
        "ffn_w2": np.ascontiguousarray(inputs["L0_ffn_W2"]),
        "ffn_ws1": np.ascontiguousarray(inputs["L0_ffn_Ws1"]),
        "ffn_wsg": np.ascontiguousarray(inputs["L0_ffn_Wsg"]),
        "ffn_ws2": np.ascontiguousarray(inputs["L0_ffn_Ws2"]),
        "gate_w": np.ascontiguousarray(
            np.concatenate([inputs["L0_ffn_gate_w"], pad_col], 1)),
        "gate_b_r": np.ascontiguousarray(
            np.concatenate([inputs["L0_ffn_gate_b"], neg])[None, :]),
        "biases_r": np.ascontiguousarray(
            np.concatenate([inputs["L0_ffn_biases"], neg])[None, :]),
        "c_masks": c_masks,
    }
    shared = {k: np.asarray(v, f32) for k, v in shared.items()}
    hid = np.asarray(inputs["L0_attn_hidden"], f32)
    in_maps = []
    for c in range(8):
        m = dict(shared)
        m["hidden"] = np.ascontiguousarray(hid[c])
        in_maps.append(m)
    return in_maps


_NC_CACHE = {}


def kernel(**inputs):
    key = "full"
    if key not in _NC_CACHE:
        _NC_CACHE[key] = Builder().build()
    nc = _NC_CACHE[key]
    in_maps = stage_inputs(inputs)
    res = run_bass_kernel_spmd(nc, in_maps, core_ids=list(range(8)))
    return np.stack([np.asarray(r["y"], np.float32) for r in res.results])




# version 07
# speedup vs baseline: 1.1055x; 1.0429x over previous
import contextlib
import math
import numpy as np
import concourse.bass as bass
import concourse.mybir as mybir
from concourse.bass_utils import run_bass_kernel_spmd

F32 = mybir.dt.float32
BF16 = mybir.dt.bfloat16
AF = mybir.ActivationFunctionType
ALU = mybir.AluOpType
AX = mybir.AxisListType

T = 4096
D = 2048
HK, HV, AK, UV = 16, 32, 128, 128
KEY_DIM = HK * AK
VALUE_DIM = HV * UV
CONV_DIM = 2 * KEY_DIM + VALUE_DIM
CH = 128
NCH = T // CH
NG = T // 128
BIG = 1.0e6
C_SCALE = AK ** -0.5
LN_EPS = 1e-5
RMS_EPS = 1e-6
L2_EPS = 1e-6
NEXP = 7
FT = 1024
NFT = T // FT
ENGINES = ("sync", "tensor", "vector", "scalar", "gpsimd")


class Pipe:

    def __init__(self, sem, name):
        self.sem = sem
        self.name = name
        self.n = 0

    def at(self):
        return (self, self.n)


_DRAIN_OPS = frozenset([
    "tensor_tensor", "tensor_scalar", "tensor_copy", "tensor_reduce",
    "reduce_sum", "reduce_max", "scalar_tensor_tensor", "reciprocal",
    "memset", "memzero", "select", "bn_stats", "bn_aggr", "activation",
    "copy", "add", "mul", "square", "sqrt", "sign", "tensor_add",
    "tensor_sub", "tensor_mul", "tensor_scalar_add", "tensor_scalar_mul",
    "tensor_scalar_max", "tensor_scalar_min", "tensor_scalar_sub",
    "tensor_relu", "tensor_max", "transpose", "iota", "copy_predicated",
    "max", "max_index", "max_with_indices", "stream_shuffle",
])
_FLUSH_OPS = frozenset(["dma_start", "dma_start_transpose"])


class DrainProxy:

    def __init__(self, e):
        object.__setattr__(self, "_e", e)
        object.__setattr__(self, "_armed", False)
        object.__setattr__(self, "_par", False)
        object.__setattr__(self, "_par_done", False)

    def begin_step(self, par):
        object.__setattr__(self, "_par", par)
        object.__setattr__(self, "_par_done", False)

    def __getattr__(self, name):
        e = object.__getattribute__(self, "_e")
        attr = getattr(e, name)
        if name in _DRAIN_OPS:
            def wrapped(*a, **k):
                par = object.__getattribute__(self, "_par")
                done = object.__getattribute__(self, "_par_done")
                if object.__getattribute__(self, "_armed") and not (par and done):
                    e.drain()
                object.__setattr__(self, "_armed", True)
                object.__setattr__(self, "_par_done", True)
                return attr(*a, **k)
            return wrapped
        if name in _FLUSH_OPS:
            def wrapped2(*a, **k):
                if object.__getattribute__(self, "_armed"):
                    e.drain()
                    object.__setattr__(self, "_armed", False)
                return attr(*a, **k)
            return wrapped2
        return attr


class Sched:

    def __init__(self, nc, exitstack):
        self.nc = nc
        self.exit = exitstack
        self.steps = []
        self._pipes = {}

    def pipe(self, name):
        if name not in self._pipes:
            sem = self.exit.enter_context(self.nc.semaphore(name))
            self._pipes[name] = Pipe(sem, name)
        return self._pipes[name]

    def step(self, engine, fn, waits=(), incs=(), par=False):
        norm = []
        for item in incs:
            if len(item) == 2:
                p_, a_ = item
                norm.append((p_, a_, "last"))
            else:
                norm.append(item)
        self.steps.append((engine, fn, list(waits), norm, par))
        for p_, a_, mode in norm:
            if mode == "each":
                p_.n += a_ * self._count_instrs_hint
            else:
                p_.n += a_
        return tuple(p_.n for p_, _, _ in norm)

    _count_instrs_hint = 1

    def dma(self, engine, fn, waits=(), pipe=None, n=1):
        self._count_instrs_hint = n
        r = self.step(engine, fn, waits,
                      incs=([(pipe, 16, "each")] if pipe else ()))
        self._count_instrs_hint = 1
        return r

    def emit(self, blk):
        by_eng = {e: [] for e in ENGINES}
        for st in self.steps:
            by_eng[st[0]].append(st)
        for eng_name in ENGINES:
            steps = by_eng[eng_name]
            if not steps:
                continue

            def body(e, steps=steps, eng_name=eng_name):
                if eng_name in ("vector", "scalar", "gpsimd"):
                    e = DrainProxy(e)
                watermark = {}
                for _, fn, waits, incs, par_ in steps:
                    if isinstance(e, DrainProxy):
                        e.begin_step(par_)
                    for p_, val in waits:
                        if val > 0 and watermark.get(p_.name, -1) < val:
                            e.wait_ge(p_.sem, val)
                            watermark[p_.name] = val
                    out = fn(e)
                    if incs:
                        instrs = out if isinstance(out, (list, tuple)) else [out]
                        n_on_last = 0
                        for p_, a_, mode in incs:
                            if mode == "each":
                                for ins in instrs:
                                    ins.then_inc(p_.sem, a_)
                            elif n_on_last == 0:
                                instrs[-1].then_inc(p_.sem, a_)
                                n_on_last = 1
                            else:
                                if eng_name in ("vector", "scalar", "gpsimd"):
                                    (e._e if isinstance(e, DrainProxy) else e
                                     ).drain()
                                    if isinstance(e, DrainProxy):
                                        object.__setattr__(e, "_armed", False)
                                e.sem_inc(p_.sem, a_)

            getattr(blk, eng_name)(body)


class Builder:
    def __init__(self, debug=()):
        self.debug = set(debug)
        self.nc = bass.Bass("TRN2", target_bir_lowering=False)
        self.exit = contextlib.ExitStack()

    def dram(self, name, shape, dtype, kind="Internal"):
        k = "ExternalOutput" if name in self.debug else kind
        return self.nc.dram_tensor(name, shape, dtype, kind=k).ap()

    def declare(self):
        nc = self.nc
        dram = self.dram
        inp = lambda n, s: nc.dram_tensor(n, s, F32, kind="ExternalInput").ap()
        self.x_in = inp("hidden", [T, D])
        self.w_qkv = inp("w_qkv", [D, CONV_DIM])
        self.w_z = inp("w_z", [D, VALUE_DIM])
        self.w_ba = inp("w_ba", [D, 64])
        self.conv_w = inp("conv_w", [CONV_DIM, 4])
        self.dt_bias_r = inp("dt_bias_r", [1, HV])
        self.a_log_r = inp("a_log_r", [1, HV])
        self.norm_w_r = inp("norm_w_r", [1, UV])
        self.w_out = inp("w_out", [VALUE_DIM, D])
        self.ffn_w1 = inp("ffn_w1", [NEXP, D, D])
        self.ffn_wg = inp("ffn_wg", [NEXP, D, D])
        self.ffn_w2 = inp("ffn_w2", [NEXP, D, D])
        self.ffn_ws1 = inp("ffn_ws1", [D, D])
        self.ffn_wsg = inp("ffn_wsg", [D, D])
        self.ffn_ws2 = inp("ffn_ws2", [D, D])
        self.gate_w = inp("gate_w", [D, 8])
        self.gate_b_r = inp("gate_b_r", [1, 8])
        self.biases_r = inp("biases_r", [1, 8])
        self.c_masks = inp("c_masks", [4, 128, 128])
        self.y_out = nc.dram_tensor("y", [T, D], F32, kind="ExternalOutput").ap()

        self.s_qf = dram("s_qf", [HK, AK, T], BF16)
        self.s_kf = dram("s_kf", [HK, AK, T], BF16)
        self.s_kt = dram("s_kt", [HK, NG, 128, AK], BF16)
        self.s_vt = dram("s_vt", [HV, NG, 128, UV], BF16)
        self.s_z = dram("s_z", [T, VALUE_DIM], BF16)
        self.s_gj = dram("s_gj", [NCH, HV, CH], F32)
        self.s_gl = dram("s_gl", [NCH, HV], F32)
        self.s_y1 = dram("s_y1", [T, D], F32)
        self.s_wob2 = dram("s_wob2", [8, 128, 4, D], BF16)
        self.s_fab = dram("s_fab", [16, 16, 128, D], BF16)
        self.s_f2b = dram("s_f2b", [8, 16, 128, D], BF16)
        self.s_wg8 = dram("s_wg8", [NFT, 8, FT], BF16)

        A = nc.alloc_sbuf_tensor
        self.masks = A("masks", [128, 4 * 128], F32)
        self.normw_bc = A("normw_bc", [128, UV], F32)
        self.dtb_bc = A("dtb_bc", [128, HV], F32)
        self.negea_bc = A("negea_bc", [128, HV], F32)
        self.gateb_bc = A("gateb_bc", [128, 8], F32)
        self.biases_bc = A("biases_bc", [128, 8], F32)
        self.ba_tm = A("ba_tm", [128, NG * 64], F32)
        self.beta_s = A("beta_s", [128, NCH * HV], F32)
        self.negsc1_s = A("negsc1_s", [128, NCH * HV], F32)
        self.egjc_s = A("egjc_s", [128, NCH * HV], F32)
        self.egLmg_s = A("egLmg_s", [128, NCH * HV], F32)
        self.egL_s = A("egL_s", [128, NCH * HV], F32)
        self.gj_s = A("gj_s", [128, NCH * HV], F32)
        self.S_f = A("S_f", [128, HV * UV], F32)
        self.S_b = A("S_b", [128, HV * UV], BF16)
        self.ARENA_W = 38100
        self.arena = A("arena", [128, self.ARENA_W], F32)
        self.psum = [nc.alloc_psum_tensor(f"pb{i}", [128, 512], F32)
                     for i in range(8)]

        self.m_sl_big = self.masks[:, 0:128]
        self.m_u_negbig = self.masks[:, 128:256]
        self.m_triu = self.masks[:, 256:384]
        self.ones_col = self.masks[:, 383:384]
        self.ident_f = self.masks[:, 384:512]

    def carve(self, off, n):
        assert off + n <= self.ARENA_W, (off, n)
        return self.arena.ap()[:, off:off + n], off + n

    def carve_b(self, off, n_h):
        nw = (n_h + 1) // 2
        assert off + nw <= self.ARENA_W, (off, n_h)
        return self.arena.ap().bitcast(BF16)[:, off * 2: off * 2 + n_h], off + nw

    def pp(self, tens, j, h):
        return tens.ap()[:, j * HV + h: j * HV + h + 1]

    def build(self, phases=("a01", "a2", "a3", "b")):
        self.declare()
        S = Sched(self.nc, self.exit)
        self.S = S
        self.ph_consts()
        if "a01" in phases:
            self.ph_a01()
        if "a2" in phases:
            self.ph_a2()
        if "a3" in phases:
            self.ph_a3()
        if "b" in phases:
            self.ph_b()
        self.ph_final()
        with self.nc.Block() as blk:
            S.emit(blk)
        self.exit.close()
        return self.nc

    def ph_consts(self):
        s, S = self, self.S
        ld = S.pipe("c_ld")
        self.c_done = S.pipe("c_done")
        S.dma("sync", lambda e: [
            e.dma_start(s.masks[:].rearrange("p (k n) -> p k n", k=4),
                        s.c_masks.transpose([1, 0, 2])),
            e.dma_start(s.normw_bc[:], s.norm_w_r.to_broadcast([128, UV])),
            e.dma_start(s.dtb_bc[:], s.dt_bias_r.to_broadcast([128, HV])),
            e.dma_start(s.negea_bc[:], s.a_log_r.to_broadcast([128, HV])),
            e.dma_start(s.gateb_bc[:], s.gate_b_r.to_broadcast([128, 8])),
            e.dma_start(s.biases_bc[:], s.biases_r.to_broadcast([128, 8])),
        ], pipe=ld, n=6)
        c_act = S.pipe("c_act")
        S.step("scalar", lambda e: e.activation(s.negea_bc[:], s.negea_bc[:], AF.Exp),
               waits=[ld.at()], incs=[(c_act, 1)])
        S.step("vector",
               lambda e: e.tensor_scalar(s.negea_bc[:], s.negea_bc[:], -1.0, None,
                                         ALU.mult),
               waits=[c_act.at()], incs=[(self.c_done, 1)])

    def ph_a01(self):
        s, S = self, self.S
        off = 0
        xn_t, off = self.carve_b(off, 16 * 2048)
        xh, off = self.carve(off, 2 * D)
        sq, off = self.carve(off, D)
        xn_tm, off = self.carve_b(off, D)
        st_, off = self.carve(off, 16)
        wba_f, off = self.carve(off, 16 * 64)
        wba_b, off = self.carve_b(off, 16 * 64)
        wst, off = self.carve(off, 2 * 2048)
        wbb, off = self.carve_b(off, 2 * 2048)
        cw, off = self.carve(off, 2 * 4)
        acc, off = self.carve(off, 3 * 512)
        sil, off = self.carve_b(off, 3 * 512)
        tmst, off = self.carve_b(off, 3 * 512)
        sqn, off = self.carve(off, 512)
        ssq, off = self.carve(off, 2 * 8)
        nfm, off = self.carve_b(off, 3 * 512)
        tails, off = self.carve(off, 64 * 4)
        b0f, off = self.carve(off, 2 * 512)
        b0b, off = self.carve_b(off, 2 * 512)
        self.a01_end = off
        xn_t3 = xn_t.rearrange("p (k t) -> p k t", k=16)
        wbb3 = wbb.rearrange("p (k n) -> p k n", k=16)
        wba_b3 = wba_b.rearrange("p (k n) -> p k n", k=16)

        ld2 = [S.pipe("a0_ld0"), S.pipe("a0_ld1")]
        sqp, v1, v2, tp = (S.pipe("a0_sq"), S.pipe("a0_v1"),
                           S.pipe("a0_v2"), S.pipe("a0_tp"))
        gld, gcast, gmm, gev = (S.pipe("g_ld"), S.pipe("g_cast"), S.pipe("g_mm"),
                                S.pipe("g_ev"))
        wld2 = [S.pipe("a1_wld0"), S.pipe("a1_wld1")]
        wcast, amm, acv, asl = (
            S.pipe("a1_wcast"), S.pipe("a1_mm"),
            S.pipe("a1_cv"), S.pipe("a1_sl"))
        atp2 = [S.pipe("a1_tp0"), S.pipe("a1_tp1"), S.pipe("a1_tp2")]
        atp_done = {}
        ntp2 = [S.pipe("a1_ntp0"), S.pipe("a1_ntp1"), S.pipe("a1_ntp2")]
        sp_tm = [S.pipe("a1_sptm0"), S.pipe("a1_sptm1"), S.pipe("a1_sptm2")]
        sp_fm = [S.pipe("a1_spfm0"), S.pipe("a1_spfm1"), S.pipe("a1_spfm2")]
        n_qkv = [0]
        pending_tail = []
        tm_guard = {}
        fm_guard = {}
        ntp_done = {}
        bank_free = {}
        nrm, ntp, nr2 = S.pipe("a1_nrm"), S.pipe("a1_ntp"), S.pipe("a1_nr2")
        qk_units = [0]
        b0p, b0c = S.pipe("b0_dma"), S.pipe("b0_cast")
        self.g_ev = gev

        S.step("vector", lambda e: e.memset(tails, 0.0))

        b0_jobs = []
        amats = ([(s.ffn_w1[e_], e_) for e_ in range(NEXP)]
                 + [(s.ffn_wg[e_], NEXP + e_) for e_ in range(NEXP)]
                 + [(s.ffn_ws1, 14), (s.ffn_wsg, 15)])
        for src_m, slot in amats:
            for r in range(16):
                for hh in range(4):
                    dst = s.s_fab[slot, hh * 4:(hh + 1) * 4, :,
                                  r * 128:(r + 1) * 128].transpose([1, 0, 2])
                    b0_jobs.append((src_m[r * 128:(r + 1) * 128,
                                          hh * 512:(hh + 1) * 512]
                                    .rearrange("p (h c) -> p h c", h=4), dst))
        mats2 = [(s.ffn_w2[e_], e_) for e_ in range(NEXP)] + [(s.ffn_ws2, 7)]
        for src_m, slot in mats2:
            for r in range(16):
                for hh in range(4):
                    dst = s.s_f2b[slot, hh * 4:(hh + 1) * 4, :,
                                  r * 128:(r + 1) * 128].transpose([1, 0, 2])
                    b0_jobs.append((src_m[r * 128:(r + 1) * 128,
                                          hh * 512:(hh + 1) * 512]
                                    .rearrange("p (h c) -> p h c", h=4), dst))
        for r in range(32):
            for hh in range(4):
                b0_jobs.append((s.w_out[r * 128:(r + 1) * 128,
                                        hh * 512:(hh + 1) * 512],
                                s.s_wob2[r // 4, :, r % 4,
                                         hh * 512:(hh + 1) * 512]))
        b0_i = [0]
        import os
        b0_disable = os.environ.get("NO_B0", "0") == "1"

        def b0_emit(n):
            if b0_disable:
                return
            for _ in range(n):
                if b0_i[0] >= len(b0_jobs):
                    return
                i = b0_i[0]
                b0_i[0] += 1
                srcap, dstap = b0_jobs[i]
                bb = (i % 2)
                fsl = b0f[:, bb * 512:(bb + 1) * 512]
                bsl = b0b[:, bb * 512:(bb + 1) * 512]
                w_ = [(b0c, i - 1)] if i >= 2 else []
                S.dma("gpsimd", lambda e, srcap=srcap, fsl=fsl: e.dma_start(
                    fsl if srcap.ndim == 2 else
                    fsl.rearrange("p (h c) -> p h c", h=4), srcap),
                    waits=w_, pipe=b0p)
                eng = "vector" if i % 2 == 0 else "scalar"
                fn = ((lambda e, fsl=fsl, bsl=bsl: e.tensor_copy(bsl, fsl))
                      if eng == "vector" else
                      (lambda e, fsl=fsl, bsl=bsl: e.copy(bsl, fsl)))
                S.step(eng, fn, waits=[b0p.at()], incs=[(b0c, 1)])
                S.dma("gpsimd", lambda e, dstap=dstap, bsl=bsl: e.dma_start(
                    dstap, bsl if dstap.ndim == 2 else
                    bsl.rearrange("p (h c) -> p h c", h=4)),
                    waits=[b0c.at()], pipe=b0p)

        S.dma("sync", lambda e: e.dma_start(
            wba_f.rearrange("p (k n) -> p k n", k=16),
            s.w_ba.rearrange("(k p) n -> p k n", p=128)), pipe=gld)
        S.step("vector", lambda e: e.tensor_copy(wba_b, wba_f),
               waits=[gld.at()], incs=[(gcast, 1)])

        n_units = [0]
        for stile in range(2):
            TB = stile * 2048
            for i in range(16):
                gi = stile * 16 + i
                b = (i % 2) * D
                xhb = xh[:, b:b + D]
                w_free = [(v2, gi - 1)] if gi >= 2 else []
                ldp = ld2[gi % 2]
                S.dma("sync", lambda e, gi=gi, xhb=xhb: e.dma_start(
                    xhb, s.x_in[gi * 128:(gi + 1) * 128, :]),
                    waits=w_free, pipe=ldp)
                S.step("scalar", lambda e, xhb=xhb: e.activation(
                    sq, xhb, AF.Square),
                    waits=[ldp.at(), (v1, gi)], incs=[(sqp, 1)])

                def vstats(e, xhb=xhb):
                    e.tensor_reduce(st_[:, 0:1], xhb, AX.X, ALU.add)
                    e.tensor_reduce(st_[:, 1:2], sq, AX.X, ALU.add)
                    e.tensor_scalar(st_[:, 2:3], st_[:, 0:1], 1.0 / D, None, ALU.mult)
                    e.tensor_scalar(st_[:, 3:4], st_[:, 1:2], 1.0 / D, None, ALU.mult)
                    e.tensor_tensor(st_[:, 4:5], st_[:, 2:3], st_[:, 2:3], ALU.mult)
                    e.tensor_tensor(st_[:, 5:6], st_[:, 3:4], st_[:, 4:5],
                                    ALU.subtract)
                    e.tensor_scalar(st_[:, 5:6], st_[:, 5:6], LN_EPS, None,
                                    ALU.add)
                    return e.tensor_scalar(st_[:, 7:8], st_[:, 2:3], -1.0, None,
                                           ALU.mult)
                S.step("vector", vstats, waits=[ldp.at(), sqp.at()],
                       incs=[(v1, 1)])

                def rstd(e):
                    e.activation(st_[:, 6:7], st_[:, 5:6], AF.Ln)
                    return e.activation(st_[:, 6:7], st_[:, 6:7], AF.Exp,
                                        scale=-0.5)
                S.step("scalar", rstd, waits=[v1.at()], incs=[(sqp, 1)])
                w_tp = [(tp, 256 * gi)] if gi >= 1 else []
                w_stile = [(amm, 384)] if (stile == 1 and i == 0) else []
                S.step("vector", lambda e, xhb=xhb: e.tensor_scalar(
                    xn_tm, xhb, st_[:, 7:8], st_[:, 6:7], ALU.add, ALU.mult),
                    waits=[sqp.at()] + w_tp + w_stile, incs=[(v2, 1)])
                S.dma("sync", lambda e, i=i: [
                    e.dma_start_transpose(xn_t3[:, k, i * 128:(i + 1) * 128],
                                          xn_tm[:, k * 128:(k + 1) * 128])
                    for k in range(16)], waits=[(v2, gi + 1)], pipe=tp, n=16)

            for i in range(16):
                gi = stile * 16 + i
                bank = self.psum[4 + (gi % 2)][:, 0:64]

                def gmmf(e, i=i, bank=bank):
                    for k in range(16):
                        ins = e.matmul(bank, xn_t3[:, k, i * 128:(i + 1) * 128],
                                       wba_b3[:, k, :], start=(k == 0),
                                       stop=(k == 15))
                    return ins
                w = [gcast.at(), (tp, 256 * 16 * (stile + 1))]
                if gi >= 2:
                    w.append((gev, gi - 1))
                S.step("tensor", gmmf, waits=w, incs=[(gmm, 1)])
                S.step("scalar", lambda e, gi=gi, bank=bank: e.copy(
                    s.ba_tm.ap()[:, gi * 64:(gi + 1) * 64], bank),
                    waits=[gmm.at()], incs=[(gev, 1)])

            for cb in range(96):
                is_qk = cb < 32
                is_v = 32 <= cb < 64
                is_z = cb >= 64
                wsl = wst[:, (cb % 2) * 2048:(cb % 2 + 1) * 2048]
                wbl = wbb[:, (cb % 2) * 2048:(cb % 2 + 1) * 2048]
                wbl3 = wbl.rearrange("p (k n) -> p k n", k=16)
                gcb = stile * 96 + cb
                wldp = wld2[gcb % 2]
                w_wfree = [(wcast, gcb - 1)] if gcb >= 2 else []
                if is_z:
                    srcw = s.w_z.rearrange("(k p) n -> p k n", p=128)[
                        :, :, (cb - 64) * 128:(cb - 63) * 128]
                else:
                    srcw = s.w_qkv.rearrange("(k p) n -> p k n", p=128)[
                        :, :, cb * 128:(cb + 1) * 128]
                S.dma("gpsimd", lambda e, srcw=srcw, wsl=wsl: e.dma_start(
                    wsl.rearrange("p (k n) -> p k n", k=16), srcw),
                    waits=w_wfree, pipe=wldp)
                if not is_z:
                    w_cw = ([(acv, 4 * (n_qkv[0] - 1))] if n_qkv[0] >= 2 else [])
                    S.dma("gpsimd", lambda e, cb=cb: e.dma_start(
                        cw[:, (cb % 2) * 4:(cb % 2) * 4 + 4],
                        s.conv_w[cb * 128:(cb + 1) * 128, :]),
                        waits=w_cw, pipe=wldp)
                    n_qkv[0] += 1
                w_cast = [wldp.at()]
                if gcb >= 2:
                    w_cast.append((amm, 4 * (gcb - 1)))
                S.step("vector", lambda e, wsl=wsl, wbl=wbl: e.tensor_copy(
                    wbl, wsl), waits=w_cast, incs=[(wcast, 1)])

                for tt in range(4):
                    u = n_units[0]
                    n_units[0] += 1
                    gtt = stile * 4 + tt
                    bank = self.psum[[0, 1, 2, 3, 6, 7][u % 6]]
                    bb = u % 3
                    accl = acc[:, bb * 512:(bb + 1) * 512]
                    sill = sil[:, bb * 512:(bb + 1) * 512]
                    tmstl = tmst[:, bb * 512:(bb + 1) * 512]
                    sqnl = sqn
                    ssql = ssq[:, (u % 2) * 8:(u % 2) * 8 + 4]
                    nfml = nfm[:, (u % 3) * 512:(u % 3 + 1) * 512]

                    def mmf(e, tt=tt, bank=bank, wbl3=wbl3):
                        for k in range(16):
                            ins = e.matmul(bank[:], wbl3[:, k, :],
                                           xn_t3[:, k, tt * 512:(tt + 1) * 512],
                                           start=(k == 0), stop=(k == 15))
                        return ins
                    w = [wcast.at(), (tp, 256 * 16 * (stile + 1)), gev.at()]
                    if (u - 6) in bank_free:
                        w.append(bank_free[u - 6])
                    S.step("tensor", mmf, waits=w, incs=[(amm, 1)])

                    if not is_z:
                        def convf(e, cb=cb, bank=bank, accl=accl, cwl=cw[
                                :, (cb % 2) * 4:(cb % 2) * 4 + 4]):
                            tl = tails[:, cb * 4:cb * 4 + 3]
                            e.tensor_scalar(accl, bank[:], cwl[:, 3:4], None,
                                            ALU.mult)
                            e.scalar_tensor_tensor(accl[:, 1:512],
                                                   bank[:, 0:511],
                                                   cwl[:, 2:3], accl[:, 1:512],
                                                   ALU.mult, ALU.add)
                            e.scalar_tensor_tensor(accl[:, 2:512],
                                                   bank[:, 0:510],
                                                   cwl[:, 1:2], accl[:, 2:512],
                                                   ALU.mult, ALU.add)
                            e.scalar_tensor_tensor(accl[:, 3:512],
                                                   bank[:, 0:509],
                                                   cwl[:, 0:1], accl[:, 3:512],
                                                   ALU.mult, ALU.add)
                            e.scalar_tensor_tensor(accl[:, 0:3], tl,
                                                   cwl[:, 0:1], accl[:, 0:3],
                                                   ALU.mult, ALU.add) if False \
                                else None
                            for k in range(3):
                                e.scalar_tensor_tensor(
                                    accl[:, 0:3 - k], tl[:, k:3],
                                    cwl[:, k:k + 1], accl[:, 0:3 - k],
                                    ALU.mult, ALU.add)
                            return e.tensor_copy(tl, bank[:, 509:512])
                        w_cv = [amm.at(), (asl, u - 2) if u >= 3 else (asl, 0)]
                        S.step("vector", convf, waits=w_cv, incs=[(acv, 1)])
                        bank_free[u] = (acv, acv.n)

                        w_sil = [acv.at()]
                        if ("atp", u % 3) in atp_done:
                            w_sil.append(atp_done[("atp", u % 3)])
                        S.step("scalar", lambda e, accl=accl, sill=sill:
                               e.activation(sill, accl, AF.Silu),
                               waits=w_sil, incs=[(asl, 1)])
                    else:
                        w_sil = [amm.at()]
                        if ("atp", u % 3) in atp_done:
                            w_sil.append(atp_done[("atp", u % 3)])
                        S.step("scalar", lambda e, sill=sill, bank=bank:
                               e.activation(sill, bank[:], AF.Silu),
                               waits=w_sil, incs=[(asl, 1)])
                        bank_free[u] = (asl, asl.n)

                    w_tmst = [(asl, asl.n)]
                    if ("tm", u % 3) in tm_guard:
                        w_tmst.append(tm_guard[("tm", u % 3)])
                    if ("ntp", u % 3) in ntp_done:
                        w_tmst.append(ntp_done[("ntp", u % 3)])
                    atq = "sync"
                    S.dma(atq, lambda e, sill=sill, tmstl=tmstl: [
                        e.dma_start_transpose(
                            tmstl[:, ss * 128:(ss + 1) * 128],
                            sill[:, ss * 128:(ss + 1) * 128])
                        for ss in range(4)],
                        waits=w_tmst, pipe=atp2[u % 3], n=4)
                    atp_done[("atp", u % 3)] = atp2[u % 3].at()

                    def tail_thunk(u=u, cb=cb, gtt=gtt, TB=TB,
                                   tt=tt, tmstl=tmstl, sqnl=sqnl,
                                   ssql=ssql, nfml=nfml,
                                   is_qk=is_qk, is_v=is_v):
                            if is_qk:
                                kh = cb % 16
                                m_qk = qk_units[0]
                                qk_units[0] += 1
                                dst_t = None if cb < 16 else s.s_kt
                                dst_f = (s.s_qf if cb < 16 else s.s_kf)

                                def nrm12(e, tmstl=tmstl, ssql=ssql):
                                    e.tensor_tensor(sqnl, tmstl, tmstl, ALU.mult)
                                    e.drain()
                                    e.tensor_reduce(
                                        ssql, sqnl.rearrange("p (s c) -> p s c", s=4),
                                        AX.X, ALU.add)
                                    return e.tensor_scalar(ssql, ssql, L2_EPS, None,
                                                           ALU.add)
                                S.step("vector", nrm12,
                                       waits=[atp_done[("atp", u % 3)], (nr2, m_qk)],
                                       incs=[(nrm, 1), (nr2, 1)])

                                def nrm3(e, ssql=ssql):
                                    e.activation(ssql, ssql, AF.Ln)
                                    return e.activation(ssql, ssql, AF.Exp, scale=-0.5)
                                S.step("scalar", nrm3, waits=[nrm.at()], incs=[(nrm, 1)])

                                def nrm4(e, tmstl=tmstl, ssql=ssql):
                                    for ss in range(4):
                                        ins = e.tensor_scalar(
                                            tmstl[:, ss * 128:(ss + 1) * 128],
                                            tmstl[:, ss * 128:(ss + 1) * 128],
                                            ssql[:, ss:ss + 1], None, ALU.mult)
                                    return ins
                                S.step("vector", nrm4, waits=[nrm.at()],
                                       incs=[(nrm, 1)], par=True)
                                if dst_t is not None:
                                    S.dma("sync", lambda e, kh=kh, gtt=gtt,
                                          tmstl=tmstl,
                                          dst_t=dst_t: e.dma_start(
                                              dst_t[kh, gtt * 4:(gtt + 1) * 4].transpose(
                                                  [1, 0, 2]),
                                              tmstl.rearrange("p (s c) -> p s c", s=4)),
                                        waits=[nrm.at()], pipe=sp_tm[u % 3])
                                    tm_guard[("tm", u % 3)] = sp_tm[u % 3].at()
                                ntpp = ntp2[u % 3]
                                w_ntp = [nrm.at()]
                                if ("fm", u % 3) in fm_guard:
                                    w_ntp.append(fm_guard[("fm", u % 3)])
                                S.dma("sync", lambda e, tmstl=tmstl, nfml=nfml: [
                                    e.dma_start_transpose(
                                        nfml[:, ss * 128:(ss + 1) * 128],
                                        tmstl[:, ss * 128:(ss + 1) * 128])
                                    for ss in range(4)],
                                    waits=w_ntp, pipe=ntpp, n=4)
                                ntp_done[("ntp", u % 3)] = ntpp.at()
                                S.dma("sync", lambda e, kh=kh, TB=TB, tt=tt,
                                      nfml=nfml,
                                      dst_f=dst_f: e.dma_start(
                                          dst_f[kh, :, TB + tt * 512:TB + (tt + 1) * 512],
                                          nfml),
                                    waits=[ntpp.at()], pipe=sp_fm[u % 3])
                                fm_guard[("fm", u % 3)] = sp_fm[u % 3].at()
                            elif is_v:
                                vh = cb - 32
                                S.dma("sync", lambda e, vh=vh, gtt=gtt, tmstl=tmstl:
                                      e.dma_start(
                                          s.s_vt[vh, gtt * 4:(gtt + 1) * 4].transpose(
                                              [1, 0, 2]),
                                          tmstl.rearrange("p (s c) -> p s c", s=4)),
                                    waits=[atp_done[("atp", u % 3)]],
                                    pipe=sp_tm[u % 3])
                                tm_guard[("tm", u % 3)] = sp_tm[u % 3].at()
                            else:
                                zb = cb - 64
                                S.dma("sync", lambda e, zb=zb, TB=TB, tt=tt,
                                      tmstl=tmstl:
                                      e.dma_start(
                                          s.s_z[TB + tt * 512:TB + (tt + 1) * 512,
                                                zb * 128:(zb + 1) * 128].rearrange(
                                              "(s p) c -> p s c", p=128),
                                          tmstl.rearrange("p (s c) -> p s c", s=4)),
                                    waits=[atp_done[("atp", u % 3)]],
                                    pipe=sp_tm[u % 3])
                                tm_guard[("tm", u % 3)] = sp_tm[u % 3].at()

                    while pending_tail:
                        pending_tail.pop(0)()
                    pending_tail.append(tail_thunk)
                b0_emit(9)
            while pending_tail:
                pending_tail.pop(0)()
        b0_emit(10 ** 9)
        self.a1_done_waits = ([amm.at(), acv.at(), asl.at(), b0p.at(),
                               nrm.at()]
                              + [p.at() for p in sp_tm + sp_fm + atp2 + ntp2])

    def ph_a2(self):
        s, S = self, self.S
        off = self.a01_end
        lanes, off = self.carve(off, 4 * HV)
        gl_row, off = self.carve(off, HV)
        gjf, off = self.carve(off, 2 * 128)
        self.a2_end = off
        gjf_sb = gjf
        a2v, a2s, a2g, mm, ev, wr, rd = (S.pipe("a2_v"), S.pipe("a2_s"),
                                         S.pipe("a2_g"), S.pipe("a2_mm"),
                                         S.pipe("a2_ev"), S.pipe("a2_wr"),
                                         S.pipe("a2_rd"))
        self.a2_done = S.pipe("a2_done")
        for j in range(NCH):
            h0 = (j % 2) * 2 * HV
            x1 = lanes[:, h0:h0 + HV]
            t1 = lanes[:, h0 + HV:h0 + 2 * HV]
            bsl = s.ba_tm.ap()[:, j * 64:j * 64 + 32]
            asl = s.ba_tm.ap()[:, j * 64 + 32:j * 64 + 64]
            gj = s.gj_s.ap()[:, j * HV:(j + 1) * HV]
            w_lane = [(mm, 2 * j - 3)] if j >= 2 else []
            S.step("vector", lambda e, x1=x1, asl=asl: e.tensor_tensor(
                x1, asl, s.dtb_bc[:], ALU.add),
                waits=[self.g_ev.at(), self.c_done.at()] + w_lane, incs=[(a2v, 1)])

            def sp_chain(e, x1=x1, t1=t1):
                e.activation(t1, x1, AF.Abs)
                e.activation(t1, t1, AF.Exp, scale=-1.0)
                return e.activation(t1, t1, AF.Ln, bias=1.0)
            S.step("scalar", sp_chain, waits=[a2v.at()], incs=[(a2s, 1)])

            def gfin(e, x1=x1, t1=t1):
                e.tensor_scalar(x1, x1, 0.0, None, ALU.max)
                e.tensor_tensor(t1, x1, t1, ALU.add)
                return e.tensor_tensor(t1, t1, s.negea_bc[:], ALU.mult)
            S.step("vector", gfin, waits=[a2s.at()], incs=[(a2g, 1)])

            bank = self.psum[2 + (j % 2)]

            def cum(e, t1=t1, bank=bank):
                e.matmul(bank[:, 0:HV], s.m_triu, t1, start=True, stop=True)
                return e.matmul(bank[0:1, HV:2 * HV], s.ones_col, t1,
                                start=True, stop=True)
            w_bank = [(ev, j - 1)] if j >= 2 else []
            if j == 0:
                w_bank += [S.pipe("a1_cv").at(), S.pipe("a1_sl").at(),
                           S.pipe("g_mm").at()]
            S.step("tensor", cum, waits=[a2g.at()] + w_bank, incs=[(mm, 1)])

            def evac(e, gj=gj, bank=bank):
                e.tensor_copy(gj, bank[:, 0:HV])
                return e.tensor_copy(gl_row[0:1, :], bank[0:1, HV:2 * HV])
            S.step("vector", evac, waits=[mm.at(), (wr, 32 * max(0, j - 1))],
                   incs=[(ev, 1)])
            tbank = self.psum[6 + (j % 2)]

            def gjtp(e, gj=gj, tbank=tbank):
                return e.transpose(tbank[0:32, 0:128], gj, s.ident_f)
            S.step("tensor", gjtp, waits=[ev.at(), (wr, 32 * max(0, j - 1))],
                   incs=[(mm, 1)])
            S.step("scalar", lambda e, j=j, tbank=tbank: e.copy(
                gjf_sb[0:32, (j % 2) * 128:(j % 2 + 1) * 128], tbank[0:32, 0:128]),
                waits=[mm.at(), (wr, 32 * max(0, j - 1))], incs=[(ev, 1)])
            S.dma("sync", lambda e, j=j: [
                e.dma_start(s.s_gj[j], gjf_sb[0:32, (j % 2) * 128:(j % 2 + 1) * 128]),
                e.dma_start(s.s_gl[j:j + 1, :], gl_row[0:1, :])],
                waits=[ev.at()], pipe=wr, n=2)
            egl = s.egL_s.ap()[:, j * HV:(j + 1) * HV]
            S.dma("sync", lambda e, j=j, egl=egl: e.dma_start(
                egl, s.s_gl[j:j + 1, :].to_broadcast([128, HV])),
                waits=[wr.at()], pipe=rd)

            def deriv_s(e, j=j, gj=gj, bsl=bsl):
                jj = slice(j * HV, (j + 1) * HV)
                e.activation(s.beta_s.ap()[:, jj], bsl, AF.Sigmoid)
                e.activation(s.egjc_s.ap()[:, jj], gj, AF.Exp)
                return e.activation(s.negsc1_s.ap()[:, jj], gj, AF.Exp)
            S.step("scalar", deriv_s, waits=[ev.at(), rd.at()], incs=[(a2s, 1)])

            def deriv_v(e, j=j, gj=gj, egl=egl):
                jj = slice(j * HV, (j + 1) * HV)
                e.tensor_scalar(s.egjc_s.ap()[:, jj], s.egjc_s.ap()[:, jj],
                                C_SCALE, None, ALU.mult)
                e.tensor_tensor(s.egLmg_s.ap()[:, jj], egl, gj, ALU.subtract)
                return e.scalar_tensor_tensor(
                    s.negsc1_s.ap()[:, jj], s.negsc1_s.ap()[:, jj], -1.0,
                    s.beta_s.ap()[:, jj], ALU.mult, ALU.mult)
            S.step("vector", deriv_v, waits=[a2s.at()], incs=[(a2v, 1)])

            def deriv_s2(e, j=j, egl=egl):
                jj = slice(j * HV, (j + 1) * HV)
                e.activation(s.egLmg_s.ap()[:, jj], s.egLmg_s.ap()[:, jj], AF.Exp)
                return e.activation(egl, egl, AF.Exp)
            S.step("scalar", deriv_s2, waits=[a2v.at()], incs=[(self.a2_done, 1)])


    def ph_a3(self):
        s, S = self, self.S
        off = 0
        qf_t, off = self.carve_b(off, 2 * 2048)
        kf_t, off = self.carve_b(off, 2 * 2048)
        kt_t, off = self.carve_b(off, 2 * 2048)
        vt_t, off = self.carve_b(off, 2 * 4096)
        zt_t, off = self.carve_b(off, 2 * 4096)
        gjbc, off = self.carve(off, 2 * 512)
        A0_l, off = self.carve_b(off, 512)
        A0T_l, off = self.carve_b(off, 512)
        Bc_l, off = self.carve_b(off, 512)
        BTc_l, off = self.carve_b(off, 512)
        Bn_l, off = self.carve_b(off, 512)
        BTn_l, off = self.carve_b(off, 512)
        Xc_l, off = self.carve_b(off, 2 * 512)
        vb_l, off = self.carve_b(off, 512)
        kd_l, off = self.carve_b(off, 512)
        iT_l, off = self.carve_b(off, 512)
        X_l, off = self.carve(off, 512)
        em_l, off = self.carve(off, 512)
        em2_l, off = self.carve(off, 512)
        t1_l, off = self.carve(off, 512)
        xg_l, off = self.carve(off, 512)
        rs_l, off = self.carve(off, 16)
        nm_l, off = self.carve_b(off, 512)
        nfm, off = self.carve_b(off, 2 * 4096)
        att, off = self.carve(off, 2 * D)
        sq2, off = self.carve(off, D)
        xr_l, off = self.carve(off, 2 * D)
        wob_l, off = self.carve_b(off, 2 * 1024)
        assert off <= self.ARENA_W, off

        pt, pv, pa = S.pipe("d_pt"), S.pipe("d_pv"), S.pipe("d_pa")
        ld2 = [S.pipe("d_ld0"), S.pipe("d_ld1")]
        gj2 = [S.pipe("d_gj0"), S.pipe("d_gj1")]
        a0t2 = [S.pipe("d_a0t0"), S.pipe("d_a0t1")]
        ntp2 = [S.pipe("d_ntp0"), S.pipe("d_ntp1")]
        wob2 = [S.pipe("d_wob0"), S.pipe("d_wob1")]
        xld2 = [S.pipe("d_xld0"), S.pipe("d_xld1")]
        y1p2 = [S.pipe("d_y10"), S.pipe("d_y11")]
        PS = self.psum

        bar = list(self.a1_done_waits) + [
            p.at() for nm_, p in S._pipes.items()
            if nm_.startswith("a2_") or nm_ == "g_ev"]
        first = {e: True for e in ENGINES}

        def step(eng, fn, waits=(), par=False):
            w = list(waits)
            if first[eng]:
                w = w + bar
                first[eng] = False
            pp = {"tensor": pt, "vector": pv, "scalar": pa}.get(eng)
            return S.step(eng, fn, w, incs=[(pp, 1)] if pp else [], par=par)

        def dma(eng, fn, waits=(), pipe=None, n=1):
            w = list(waits)
            if first[eng]:
                w = w + bar
                first[eng] = False
            return S.dma(eng, fn, w, pipe=pipe, n=n)

        step("vector", lambda e: [e.memset(s.S_f.ap()[:], 0.0),
                                  e.memset(s.S_b.ap()[:], 0.0)])
        sinit_at = pv.at()

        sb_ready = {}
        ld_done = {}
        gj_done = {}
        a0t_prev = {}
        ntp_prev = {}
        wob_prev = {}
        xld_prev = {}
        y1_prev = {}
        ap_free = {}
        lv_free = {}
        gram_free = None
        ps6_free = sinit_at
        lv0_last = None
        out_last = None
        pss_last = None
        a0t_last = None
        ntp_last = None
        wout_last = None
        att_free = None
        nm_done = {}
        NL = 6
        self.y1_done = []

        for j in range(NCH):
            par = j % 2
            qf = qf_t[:, par * 2048:(par + 1) * 2048]
            kf = kf_t[:, par * 2048:(par + 1) * 2048]
            kt = kt_t[:, par * 2048:(par + 1) * 2048]
            vt = vt_t[:, par * 4096:(par + 1) * 4096]
            zt = zt_t[:, par * 4096:(par + 1) * 4096]
            ldp = ld2[par]
            w_ld = ld_done.get(j - 2, [])

            def loads(e, j=j, qf=qf, kf=kf, kt=kt, vt=vt, zt=zt):
                return [
                    e.dma_start(qf.rearrange("p (h c) -> p h c", h=16),
                                s.s_qf[:, :, j * 128:(j + 1) * 128]
                                .transpose([1, 0, 2])),
                    e.dma_start(kf.rearrange("p (h c) -> p h c", h=16),
                                s.s_kf[:, :, j * 128:(j + 1) * 128]
                                .transpose([1, 0, 2])),
                    e.dma_start(kt.rearrange("p (h c) -> p h c", h=16),
                                s.s_kt[:, j].transpose([1, 0, 2])),
                    e.dma_start(vt.rearrange("p (h c) -> p h c", h=32),
                                s.s_vt[:, j].transpose([1, 0, 2])),
                    e.dma_start(zt, s.s_z[j * 128:(j + 1) * 128, :])]
            dma("sync", loads, waits=w_ld, pipe=ldp, n=5)
            ld_at = ldp.at()

            for g in range(8):
                heads = [4 * g + i for i in range(4)]
                pairs = [2 * g, 2 * g + 1]
                gi = j * 8 + g
                gpar = gi % 2
                gjb = gjbc[:, gpar * 512:(gpar + 1) * 512]
                gjp = gj2[gpar]
                w_gj = [gj_done[gi - 2]] if (gi - 2) in gj_done else []
                if (gi - 2) in gj_done:
                    pass

                def gjload(e, j=j, g=g, gjb=gjb):
                    return [e.dma_start(
                        gjb[:, i * 128:(i + 1) * 128],
                        s.s_gj[j, 4 * g + i:4 * g + i + 1, :]
                        .to_broadcast([128, 128]))
                        for i in range(4)]
                dma("gpsimd", gjload, waits=w_gj, pipe=gjp, n=4)
                gj_at = gjp.at()

                def grams(e, kf=kf, qf=qf, pairs=pairs):
                    for i, p in enumerate(pairs):
                        kfp = kf[:, p * 128:(p + 1) * 128]
                        qfp = qf[:, p * 128:(p + 1) * 128]
                        e.matmul(PS[0][:, i * 256:i * 256 + 128], kfp, kfp,
                                 start=True, stop=True)
                        ins = e.matmul(PS[0][:, i * 256 + 128:i * 256 + 256],
                                       kfp, qfp, start=True, stop=True)
                    return ins
                w_gram = [ld_at]
                if gram_free is not None:
                    w_gram.append(gram_free)
                step("tensor", grams, waits=w_gram)
                gram_at = pt.at()

                def prepX(e, j=j, heads=heads, gjb=gjb):
                    for i, h in enumerate(heads):
                        ins = e.tensor_scalar(X_l[:, i * 128:(i + 1) * 128],
                                              gjb[:, i * 128:(i + 1) * 128],
                                              s.pp(s.gj_s, j, h), None,
                                              ALU.subtract)
                    return ins
                S.step("vector", prepX, [gj_at], incs=[(pv, 1)], par=True)

                def prepM(e):
                    for i in range(4):
                        e.tensor_tensor(em_l[:, i * 128:(i + 1) * 128],
                                        X_l[:, i * 128:(i + 1) * 128],
                                        s.m_sl_big, ALU.add)
                        ins = e.tensor_tensor(em2_l[:, i * 128:(i + 1) * 128],
                                              X_l[:, i * 128:(i + 1) * 128],
                                              s.m_u_negbig, ALU.add)
                    return ins
                S.step("vector", prepM, [], incs=[(pv, 1)], par=True)
                prep_at = pv.at()
                gj_done[gi] = (pv, pv.n)

                def exps(e):
                    e.activation(em_l, em_l, AF.Exp, scale=-1.0)
                    return e.activation(em2_l, em2_l, AF.Exp)
                step("scalar", exps, par=True, waits=[prep_at])
                exps_at = pa.at()

                def a0build(e, j=j, heads=heads):
                    for i, h in enumerate(heads):
                        pq = i // 2
                        e.scalar_tensor_tensor(
                            A0_l[:, i * 128:(i + 1) * 128],
                            PS[0][:, pq * 256:pq * 256 + 128],
                            s.pp(s.beta_s, j, h),
                            em_l[:, i * 128:(i + 1) * 128], ALU.mult, ALU.mult)
                        ins = e.tensor_tensor(
                            iT_l[:, i * 128:(i + 1) * 128],
                            PS[0][:, pq * 256 + 128:pq * 256 + 256],
                            em2_l[:, i * 128:(i + 1) * 128], ALU.mult)
                    return ins
                w_a0 = [exps_at, gram_at]
                if a0t_last is not None:
                    w_a0.append(a0t_last)
                if lv0_last is not None:
                    w_a0.append(lv0_last)
                if out_last is not None:
                    w_a0.append(out_last)
                step("vector", a0build, par=True, waits=w_a0)
                a0_at = pv.at()
                gram_free = a0_at

                a0tp = a0t2[gpar]
                w_a0t = [a0_at]
                if lv0_last is not None:
                    w_a0t.append(lv0_last)
                if gpar in a0t_prev:
                    w_a0t.append(a0t_prev[gpar])

                def a0tpf(e):
                    return [e.dma_start_transpose(
                        A0T_l[:, i * 128:(i + 1) * 128],
                        A0_l[:, i * 128:(i + 1) * 128]) for i in range(4)]
                dma("sync", a0tpf, waits=w_a0t, pipe=a0tp, n=4)
                a0t_prev[gpar] = a0tp.at()
                a0t_last = a0tp.at()
                a0t_at = a0tp.at()

                def vbf(e, j=j, heads=heads, vt=vt):
                    for i, h in enumerate(heads):
                        ins = e.tensor_scalar(
                            vb_l[:, i * 128:(i + 1) * 128],
                            vt[:, h * 128:(h + 1) * 128],
                            s.pp(s.beta_s, j, h), None, ALU.mult)
                    return ins
                step("vector", vbf, par=True, waits=[ld_at])
                vb_at = pv.at()

                def psR(e, kf=kf, heads=heads, pairs=pairs):
                    for i, h in enumerate(heads):
                        p = pairs[i // 2]
                        ins = e.matmul(
                            PS[6][:, i * 128:(i + 1) * 128],
                            kf[:, p * 128:(p + 1) * 128],
                            s.S_b.ap()[:, h * 128:(h + 1) * 128],
                            start=True, stop=True)
                    return ins
                w_psr = [ld_at, ps6_free]
                if (j - 1, g) in sb_ready:
                    w_psr.append(sb_ready[(j - 1, g)])
                else:
                    w_psr.append(sinit_at)
                if ap_free.get(1) is not None:
                    w_psr.append(ap_free[1])
                step("tensor", psR, waits=w_psr)
                psr_at = pt.at()

                def xc0(e, j=j, heads=heads):
                    for i, h in enumerate(heads):
                        ins = e.scalar_tensor_tensor(
                            Xc_l[:, i * 128:(i + 1) * 128],
                            PS[6][:, i * 128:(i + 1) * 128],
                            s.pp(s.negsc1_s, j, h),
                            vb_l[:, i * 128:(i + 1) * 128], ALU.mult, ALU.add)
                    return ins
                step("vector", xc0, par=True, waits=[psr_at, vb_at])
                xc_at = pv.at()
                ps6_free = xc_at

                cur_B, cur_BT = A0_l, A0T_l
                cur_ready = a0t_at
                nxt = [(Bn_l, BTn_l), (Bc_l, BTc_l)]
                for k in range(7):
                    apb = PS[5 + (k % 2)]
                    xin = Xc_l[:, (k % 2) * 512:(k % 2) * 512 + 512]
                    xout = Xc_l[:, ((k + 1) % 2) * 512:((k + 1) % 2) * 512 + 512]

                    def apf(e, cur_BT=cur_BT, xin=xin, apb=apb):
                        for i in range(4):
                            ins = e.matmul(apb[:, i * 128:(i + 1) * 128],
                                           cur_BT[:, i * 128:(i + 1) * 128],
                                           xin[:, i * 128:(i + 1) * 128],
                                           start=True, stop=True)
                        return ins
                    w_ap = [cur_ready, xc_at]
                    if ap_free.get(k % 2) is not None:
                        w_ap.append(ap_free[k % 2])
                    if k % 2 == 1:
                        w_ap.append(ps6_free)
                    step("tensor", apf, waits=w_ap)
                    ap_at = pt.at()

                    def apv(e, xin=xin, xout=xout, apb=apb, k=k):
                        return e.tensor_tensor(
                            xout, xin, apb[:],
                            ALU.subtract if k == 0 else ALU.add)
                    step("vector", apv, waits=[ap_at])
                    xc_at = pv.at()
                    ap_free[k % 2] = xc_at
                    if k % 2 == 1:
                        ps6_free = xc_at

                    if k < NL:
                        lvb = (1 + (k % 2) * 2, 2 + (k % 2) * 2)
                        need_B = (k + 1) < NL
                        Bnxt, BTnxt = nxt[k % 2]

                        def lvf(e, cur_B=cur_B, cur_BT=cur_BT, lvb=lvb,
                                need_B=need_B):
                            for i in range(4):
                                if need_B:
                                    e.matmul(
                                        PS[lvb[0]][:, i * 128:(i + 1) * 128],
                                        cur_BT[:, i * 128:(i + 1) * 128],
                                        cur_B[:, i * 128:(i + 1) * 128],
                                        start=True, stop=True)
                                ins = e.matmul(
                                    PS[lvb[1]][:, i * 128:(i + 1) * 128],
                                    cur_B[:, i * 128:(i + 1) * 128],
                                    cur_BT[:, i * 128:(i + 1) * 128],
                                    start=True, stop=True)
                            return ins
                        w_lv = [cur_ready]
                        if lv_free.get(k % 2) is not None:
                            w_lv.append(lv_free[k % 2])
                        step("tensor", lvf, waits=w_lv)
                        lv_at = pt.at()
                        if k == 0:
                            lv0_last = lv_at

                        def lvev(e, lvb=lvb, Bnxt=Bnxt, BTnxt=BTnxt,
                                 need_B=need_B):
                            if need_B:
                                e.copy(Bnxt, PS[lvb[0]][:])
                            return e.copy(BTnxt, PS[lvb[1]][:])
                        step("scalar", lvev, par=True, waits=[lv_at])
                        lvev_at = pa.at()
                        lv_free[k % 2] = lvev_at
                        cur_B, cur_BT = Bnxt, BTnxt
                        cur_ready = lvev_at
                v_new = Xc_l[:, 512:1024]
                vnew_at = xc_at

                def qSf(e, qf=qf, heads=heads, pairs=pairs):
                    for i, h in enumerate(heads):
                        p = pairs[i // 2]
                        ins = e.matmul(
                            PS[6][:, i * 128:(i + 1) * 128],
                            qf[:, p * 128:(p + 1) * 128],
                            s.S_b.ap()[:, h * 128:(h + 1) * 128],
                            start=True, stop=True)
                    return ins
                w_qs = [ps6_free, ap_free[1]]
                if (j - 1, g) in sb_ready:
                    w_qs.append(sb_ready[(j - 1, g)])
                else:
                    w_qs.append(sinit_at)
                step("tensor", qSf, waits=w_qs)
                qs_at = pt.at()

                def t1f(e, j=j, heads=heads):
                    for i, h in enumerate(heads):
                        ins = e.tensor_scalar(
                            t1_l[:, i * 128:(i + 1) * 128],
                            PS[6][:, i * 128:(i + 1) * 128],
                            s.pp(s.egjc_s, j, h), None, ALU.mult)
                    return ins
                step("vector", t1f, par=True, waits=[qs_at])
                t1_at = pv.at()

                def outf(e, v_new=v_new):
                    for i in range(4):
                        ins = e.matmul(PS[6][:, i * 128:(i + 1) * 128],
                                       iT_l[:, i * 128:(i + 1) * 128],
                                       v_new[:, i * 128:(i + 1) * 128],
                                       start=True, stop=True)
                    return ins
                step("tensor", outf, waits=[t1_at, vnew_at, a0_at])
                out_at = pt.at()
                out_last = out_at

                def kdf(e, j=j, heads=heads, kt=kt, pairs=pairs):
                    for i, h in enumerate(heads):
                        p = pairs[i // 2]
                        ins = e.tensor_scalar(
                            kd_l[:, i * 128:(i + 1) * 128],
                            kt[:, p * 128:(p + 1) * 128],
                            s.pp(s.egLmg_s, j, h), None, ALU.mult)
                    return ins
                w_kd = [ld_at]
                if pss_last is not None:
                    w_kd.append(pss_last)
                step("vector", kdf, par=True, waits=w_kd)
                kd_at = pv.at()

                def psSf(e, v_new=v_new):
                    for i in range(4):
                        ins = e.matmul(PS[0][:, i * 128:(i + 1) * 128],
                                       kd_l[:, i * 128:(i + 1) * 128],
                                       v_new[:, i * 128:(i + 1) * 128],
                                       start=True, stop=True)
                    return ins
                step("tensor", psSf, waits=[kd_at, vnew_at, gram_free])
                pss_at = pt.at()
                pss_last = pss_at

                def supd1(e, j=j, heads=heads):
                    for i, h in enumerate(heads):
                        ins = e.scalar_tensor_tensor(
                            s.S_f.ap()[:, h * 128:(h + 1) * 128],
                            s.S_f.ap()[:, h * 128:(h + 1) * 128],
                            s.pp(s.egL_s, j, h),
                            PS[0][:, i * 128:(i + 1) * 128], ALU.mult, ALU.add)
                    return ins
                S.step("vector", supd1, [pss_at], incs=[(pv, 1)], par=True)

                def supd(e, heads=heads):
                    return e.tensor_copy(
                        s.S_b.ap()[:, heads[0] * 128:(heads[0] + 4) * 128],
                        s.S_f.ap()[:, heads[0] * 128:(heads[0] + 4) * 128])
                step("vector", supd, waits=[])
                sb_ready[(j, g)] = pv.at()
                gram_free = pv.at()

                def xgf1(e):
                    for i in range(4):
                        ins = e.scalar_tensor_tensor(
                            t1_l[:, i * 128:(i + 1) * 128],
                            PS[6][:, i * 128:(i + 1) * 128],
                            C_SCALE,
                            t1_l[:, i * 128:(i + 1) * 128], ALU.mult, ALU.add)
                    return ins
                S.step("vector", xgf1, [out_at, t1_at], incs=[(pv, 1)], par=True)

                def xgf(e, g=g, zt=zt):
                    return e.tensor_tensor(
                        xg_l, t1_l, zt[:, g * 512:(g + 1) * 512], ALU.mult)
                step("vector", xgf, waits=[])
                xg_at = pv.at()
                ps6_free = xg_at

                step("scalar", lambda e: e.activation(em_l, xg_l, AF.Square),
                     waits=[xg_at])
                sq_at = pa.at()

                def redf(e):
                    e.tensor_reduce(rs_l[:, 0:4],
                                    em_l.rearrange("p (s c) -> p s c", s=4),
                                    AX.X, ALU.add)
                    return e.tensor_scalar(rs_l[:, 0:4], rs_l[:, 0:4],
                                           RMS_EPS * UV, None, ALU.add)
                step("vector", redf, waits=[sq_at])
                red_at = pv.at()

                def rsqf(e):
                    e.activation(rs_l[:, 4:8], rs_l[:, 0:4], AF.Ln,
                                 scale=1.0 / UV)
                    return e.activation(rs_l[:, 4:8], rs_l[:, 4:8], AF.Exp,
                                        scale=-0.5)
                step("scalar", rsqf, waits=[red_at])
                rsq_at = pa.at()

                def nrmf(e):
                    for i in range(4):
                        ins = e.scalar_tensor_tensor(
                            nm_l[:, i * 128:(i + 1) * 128],
                            xg_l[:, i * 128:(i + 1) * 128],
                            rs_l[:, 4 + i:5 + i],
                            s.normw_bc[:], ALU.mult, ALU.mult)
                    return ins
                w_nrm = [rsq_at]
                if ntp_last is not None:
                    w_nrm.append(ntp_last)
                step("vector", nrmf, par=True, waits=w_nrm)
                nrm_at = pv.at()

                ntpp = ntp2[gpar]
                w_ntp = [nrm_at]
                if gpar in ntp_prev:
                    w_ntp.append(ntp_prev[gpar])
                if wout_last is not None:
                    w_ntp.append(wout_last)
                half = j % 2

                def ntpf(e, g=g, half=half):
                    return [e.dma_start_transpose(
                        nfm[:, half * 4096 + (4 * g + i) * 128:
                            half * 4096 + (4 * g + i + 1) * 128],
                        nm_l[:, i * 128:(i + 1) * 128]) for i in range(4)]
                dma("sync", ntpf, waits=w_ntp, pipe=ntpp, n=4)
                ntp_prev[gpar] = ntpp.at()
                ntp_last = ntpp.at()

            ld_done[j] = [(pt, pt.n), (pv, pv.n)]
            nm_done[j] = (ntp2[0].at(), ntp2[1].at())

            if j % 2 == 1:
                m2 = j // 2
                w_nfm = [nm_done[j - 1][0], nm_done[j - 1][1],
                         nm_done[j][0], nm_done[j][1]]
                ev_at = None
                for ds in range(8):
                    for vq in range(8):
                        wpar = vq % 2
                        wobp = wob2[wpar]
                        wsl = wob_l[:, wpar * 1024:(wpar + 1) * 1024]
                        w_wob = []
                        if (m2 * 64 + ds * 8 + vq) >= 2:
                            w_wob.append(wob_prev[wpar])

                        def wobl(e, ds=ds, vq=vq, wsl=wsl):
                            return e.dma_start(
                                wsl.rearrange("p (v c) -> p v c", v=4),
                                s.s_wob2[vq, :, :, ds * 256:(ds + 1) * 256])
                        dma("sync", wobl, waits=w_wob, pipe=wobp)
                        wob_at = wobp.at()

                        def wmm(e, ds=ds, vq=vq, wsl=wsl):
                            tgb = [PS[7], PS[1]]
                            for i in range(4):
                                for tg in range(2):
                                    ins = e.matmul(
                                        tgb[tg][:, 0:256],
                                        nfm[:, tg * 4096 +
                                            (vq * 4 + i) * 128:
                                            tg * 4096 + (vq * 4 + i + 1) * 128],
                                        wsl[:, i * 256:(i + 1) * 256],
                                        start=(vq == 0 and i == 0),
                                        stop=(vq == 7 and i == 3))
                            return ins
                        w_mm = [wob_at] + w_nfm
                        if vq == 0 and ev_at is not None:
                            w_mm.append(ev_at)
                        if vq == 0 and att_free is not None:
                            w_mm.append(att_free)
                        if ds == 0 and vq == 0:
                            w_mm += [v for v in (lv_free.get(0), lv_free.get(1))
                                     if v is not None]
                        step("tensor", wmm, waits=w_mm)
                        wob_prev[wpar] = (pt, pt.n)
                    wmm_at = pt.at()

                    def wev(e, ds=ds):
                        e.copy(att[:, 0 * 2048 + ds * 256:
                                   0 * 2048 + (ds + 1) * 256],
                               PS[7][:, 0:256])
                        return e.copy(att[:, 1 * 2048 + ds * 256:
                                          1 * 2048 + (ds + 1) * 256],
                                      PS[1][:, 0:256])
                    step("scalar", wev, waits=[wmm_at])
                    ev_at = pa.at()
                wout_last = pt.at()
                lv_free[0] = ev_at

                for tg in range(2):
                    grp = 2 * m2 + tg
                    xpar = grp % 2
                    xlp = xld2[xpar]
                    xrl = xr_l[:, xpar * D:(xpar + 1) * D]
                    y1l = xrl
                    w_x = []
                    if xpar in y1_prev:
                        w_x.append(y1_prev[xpar])

                    def xload(e, grp=grp, xrl=xrl):
                        return e.dma_start(
                            xrl, s.x_in[grp * 128:(grp + 1) * 128, :])
                    dma("sync", xload, waits=w_x, pipe=xlp)
                    x_at = xlp.at()
                    atg = att[:, tg * 2048:(tg + 1) * 2048]

                    def lnsq(e, atg=atg):
                        return e.activation(sq2, atg, AF.Square)
                    step("scalar", lnsq, waits=[ev_at])
                    lsq_at = pa.at()

                    def lnst(e, atg=atg):
                        e.tensor_reduce(rs_l[:, 8:9], atg, AX.X, ALU.add)
                        e.tensor_reduce(rs_l[:, 9:10], sq2, AX.X, ALU.add)
                        e.tensor_scalar(rs_l[:, 10:11], rs_l[:, 8:9], 1.0 / D,
                                        None, ALU.mult)
                        e.tensor_scalar(rs_l[:, 11:12], rs_l[:, 9:10], 1.0 / D,
                                        None, ALU.mult)
                        e.tensor_tensor(rs_l[:, 12:13], rs_l[:, 10:11],
                                        rs_l[:, 10:11], ALU.mult)
                        e.tensor_tensor(rs_l[:, 11:12], rs_l[:, 11:12],
                                        rs_l[:, 12:13], ALU.subtract)
                        e.tensor_scalar(rs_l[:, 11:12], rs_l[:, 11:12], LN_EPS,
                                        None, ALU.add)
                        return e.tensor_scalar(rs_l[:, 10:11], rs_l[:, 10:11],
                                               -1.0, None, ALU.mult)
                    step("vector", lnst, waits=[lsq_at])
                    lst_at = pv.at()

                    def lnrs(e):
                        e.activation(rs_l[:, 11:12], rs_l[:, 11:12], AF.Ln)
                        return e.activation(rs_l[:, 11:12], rs_l[:, 11:12],
                                            AF.Exp, scale=-0.5)
                    step("scalar", lnrs, waits=[lst_at])
                    lrs_at = pa.at()

                    def lnap(e, atg=atg, xrl=xrl):
                        e.tensor_scalar(atg, atg, rs_l[:, 10:11],
                                        rs_l[:, 11:12], ALU.add, ALU.mult)
                        return e.tensor_tensor(xrl, atg, xrl, ALU.add)
                    w_lap = [lrs_at, x_at]
                    if xpar in y1_prev:
                        w_lap.append(y1_prev[xpar])
                    step("vector", lnap, waits=w_lap)
                    lap_at = pv.at()
                    if tg == 1:
                        att_free = lap_at

                    def y1st(e, grp=grp, y1l=y1l):
                        return e.dma_start(
                            s.s_y1[grp * 128:(grp + 1) * 128, :], y1l)
                    dma("sync", y1st, waits=[lap_at], pipe=y1p2[xpar])
                    y1_prev[xpar] = y1p2[xpar].at()
                    self.y1_done.append(y1p2[xpar].at())

    def ph_b(self):
        s, S = self, self.S
        off = 0
        xn2, off = self.carve_b(off, 16 * 1024)
        oacc, off = self.carve_b(off, 16 * 1024)
        pt_off = off
        Pt, off = self.carve_b(off, 16 * 1024)
        sq2 = self.arena.ap()[:, pt_off:pt_off + D]
        y1t, off = self.carve(off, 2 * D)
        x2tm, off = self.carve_b(off, D)
        st3, off = self.carve(off, 32)
        w1c, off = self.carve_b(off, 2 * 2048)
        wgc, off = self.carve_b(off, 2 * 2048)
        w2c, off = self.carve_b(off, 2 * 2048)
        wbc, off = self.carve_b(off, 2 * 1024)
        gw_f, off = self.carve(off, 128)
        gw_b, off = self.carve_b(off, 128)
        gl_t, off = self.carve(off, 10 * 8)
        l9b, off = self.carve_b(off, 8)
        t1b, off = self.carve_b(off, 512)
        gbb, off = self.carve_b(off, 512)
        assert off <= self.ARENA_W, off
        xn2_3 = xn2.rearrange("p (k t) -> p k t", k=16)
        oacc3 = oacc.rearrange("p (k t) -> p k t", k=16)
        Pt3 = Pt.rearrange("p (k t) -> p k t", k=16)

        pt, pv, pa = S.pipe("b_pt"), S.pipe("b_pv"), S.pipe("b_pa")
        yld2 = [S.pipe("b_yld0"), S.pipe("b_yld1")]
        xtp = S.pipe("b_xtp")
        wld2 = [S.pipe("b_wld0"), S.pipe("b_wld1")]
        w2ld2 = [S.pipe("b_w2ld0"), S.pipe("b_w2ld1")]
        wgl = S.pipe("b_wgl")
        wwr = S.pipe("b_wwr")
        wbc2p = [S.pipe("b_wbc0"), S.pipe("b_wbc1")]
        yst2 = [S.pipe("b_yst0"), S.pipe("b_yst1")]
        PS = self.psum

        bar = [p.at() for p in S._pipes.values() if p.n > 0]
        first = {e: True for e in ENGINES}

        def step(eng, fn, waits=(), par=False):
            w = list(waits)
            if first[eng]:
                w = w + bar
                first[eng] = False
            pp = {"tensor": pt, "vector": pv, "scalar": pa}.get(eng)
            return S.step(eng, fn, w, incs=[(pp, 1)] if pp else [], par=par)

        def dma(eng, fn, waits=(), pipe=None, n=1):
            w = list(waits)
            if first[eng]:
                w = w + bar
                first[eng] = False
            return S.dma(eng, fn, w, pipe=pipe, n=n)

        dma("sync", lambda e: e.dma_start(
            gw_f.rearrange("p (k n) -> p k n", k=16),
            s.gate_w.rearrange("(k p) n -> p k n", p=128)), pipe=wgl)
        step("vector", lambda e: e.tensor_copy(gw_b, gw_f), waits=[wgl.at()])
        gwc_at = pv.at()
        gw_b3 = gw_b.rearrange("p (k n) -> p k n", k=16)

        y1_prev = {}
        wld_prev = {}
        w2_prev = {}
        wbc_prev = {}
        yst_prev = {}
        xtp_last = None
        pe_xn_last = None
        oacc_free = None

        for t in range(NFT):
            for gg in range(8):
                grp = t * 8 + gg
                par = grp % 2
                ylp = yld2[par]
                yl = y1t[:, par * D:(par + 1) * D]
                w_y = [y1_prev[par]] if par in y1_prev else []

                def yload(e, grp=grp, yl=yl):
                    return e.dma_start(yl, s.s_y1[grp * 128:(grp + 1) * 128, :])
                dma("sync", yload, waits=w_y, pipe=ylp)
                y_at = ylp.at()

                w_sq = [y_at]
                if pa.n:
                    w_sq.append((pa, pa.n))
                if pe_xn_last is not None:
                    w_sq.append(pe_xn_last)
                step("scalar", lambda e, yl=yl: e.activation(sq2, yl, AF.Square),
                     waits=w_sq)
                sq_at = pa.at()

                def lnst(e, yl=yl):
                    e.tensor_reduce(st3[:, 0:1], yl, AX.X, ALU.add)
                    e.tensor_reduce(st3[:, 1:2], sq2, AX.X, ALU.add)
                    e.tensor_scalar(st3[:, 2:3], st3[:, 0:1], 1.0 / D, None,
                                    ALU.mult)
                    e.tensor_scalar(st3[:, 3:4], st3[:, 1:2], 1.0 / D, None,
                                    ALU.mult)
                    e.tensor_tensor(st3[:, 4:5], st3[:, 2:3], st3[:, 2:3],
                                    ALU.mult)
                    e.tensor_tensor(st3[:, 5:6], st3[:, 3:4], st3[:, 4:5],
                                    ALU.subtract)
                    e.tensor_scalar(st3[:, 5:6], st3[:, 5:6], LN_EPS, None,
                                    ALU.add)
                    return e.tensor_scalar(st3[:, 6:7], st3[:, 2:3], -1.0, None,
                                           ALU.mult)
                step("vector", lnst, waits=[sq_at])
                lst_at = pv.at()

                def lnrs(e):
                    e.activation(st3[:, 5:6], st3[:, 5:6], AF.Ln)
                    return e.activation(st3[:, 5:6], st3[:, 5:6], AF.Exp,
                                        scale=-0.5)
                step("scalar", lnrs, waits=[lst_at])
                lrs_at = pa.at()

                w_ap = [lrs_at]
                if xtp_last is not None:
                    w_ap.append(xtp_last)
                step("vector", lambda e, yl=yl: e.tensor_scalar(
                    x2tm, yl, st3[:, 6:7], st3[:, 5:6], ALU.add, ALU.mult),
                    waits=w_ap)
                ap_at = pv.at()

                w_tp = [ap_at]
                if gg == 0 and pe_xn_last is not None:
                    w_tp.append(pe_xn_last)

                def xtpf(e, gg=gg):
                    return [e.dma_start_transpose(
                        xn2_3[:, k, gg * 128:(gg + 1) * 128],
                        x2tm[:, k * 128:(k + 1) * 128]) for k in range(16)]
                dma("sync", xtpf, waits=w_tp, pipe=xtp, n=16)
                xtp_last = xtp.at()

                def gmm(e, gg=gg):
                    for k in range(16):
                        ins = e.matmul(PS[6][:, 0:8],
                                       xn2_3[:, k, gg * 128:(gg + 1) * 128],
                                       gw_b3[:, k, :], start=(k == 0),
                                       stop=(k == 15))
                    return ins
                w_gmm = [xtp_last, gwc_at]
                if pv.n:
                    w_gmm.append((pv, pv.n))
                step("tensor", gmm, waits=w_gmm)
                gmm_at = pt.at()

                L = lambda i: gl_t[:, i * 8:(i + 1) * 8]

                def gch1(e):
                    e.tensor_tensor(L(0), PS[6][:, 0:8], s.gateb_bc[:], ALU.add)
                    e.tensor_reduce(L(1)[:, 0:1], L(0), AX.X, ALU.max)
                    return e.tensor_scalar(L(1)[:, 1:2], L(1)[:, 0:1], -1.0,
                                           None, ALU.mult)
                step("vector", gch1, waits=[gmm_at, (pa, pa.n)])
                g1_at = pv.at()
                step("scalar", lambda e: e.activation(
                    L(2), L(0), AF.Exp, bias=L(1)[:, 1:2]), waits=[g1_at])
                g2_at = pa.at()

                def gch2(e):
                    e.tensor_reduce(L(1)[:, 2:3], L(2), AX.X, ALU.add)
                    e.reciprocal(L(1)[:, 3:4], L(1)[:, 2:3])
                    e.tensor_scalar(L(3), L(2), L(1)[:, 3:4], None, ALU.mult)
                    e.tensor_tensor(L(4), L(3), s.biases_bc[:], ALU.add)
                    e.tensor_reduce(L(1)[:, 4:5], L(4), AX.X, ALU.max)
                    e.tensor_scalar(L(5), L(4), L(1)[:, 4:5], None, ALU.is_ge)
                    e.scalar_tensor_tensor(L(6), L(5), -BIG, L(4), ALU.mult,
                                           ALU.add)
                    e.tensor_reduce(L(1)[:, 5:6], L(6), AX.X, ALU.max)
                    e.tensor_scalar(L(7), L(6), L(1)[:, 5:6], None, ALU.is_ge)
                    e.scalar_tensor_tensor(L(6), L(7), -BIG, L(6), ALU.mult,
                                           ALU.add)
                    e.tensor_reduce(L(1)[:, 6:7], L(6), AX.X, ALU.max)
                    e.tensor_scalar(L(8), L(4), L(1)[:, 6:7], None, ALU.is_ge)
                    e.tensor_tensor(L(9), L(8), L(3), ALU.mult)
                    e.tensor_reduce(L(1)[:, 7:8], L(9), AX.X, ALU.add)
                    e.reciprocal(L(1)[:, 7:8], L(1)[:, 7:8])
                    e.tensor_scalar(L(9), L(9), L(1)[:, 7:8], None, ALU.mult)
                    return e.tensor_copy(l9b, L(9))
                step("vector", gch2, waits=[g2_at, wwr.at()])
                g3_at = pv.at()

                def wwrite(e, t=t, grp=grp, gg=gg):
                    return [e.dma_start(
                        s.s_wg8[t, ee, gg * 128:(gg + 1) * 128],
                        l9b[:, ee:ee + 1]) for ee in range(8)]
                dma("sync", wwrite, waits=[g3_at], pipe=wwr, n=8)
                y1_prev[par] = wwr.at()
            ln_done = [(pv, pv.n), (pa, pa.n), xtp_last, wwr.at()]

            for e_ in range(8):
                shared = (e_ == 0)
                wpar = e_ % 2
                wsl = wbc[:, wpar * 1024:(wpar + 1) * 1024]
                if not shared:
                    w_wbc = [wwr.at()]
                    if wbc_prev.get(wpar) is not None:
                        w_wbc.append(wbc_prev[wpar])

                    def wbcl(e, t=t, e_=e_, wsl=wsl):
                        return e.dma_start(
                            wsl, s.s_wg8[t, e_ - 1:e_, :].to_broadcast(
                                [128, 1024]))
                    dma("gpsimd", wbcl, waits=w_wbc, pipe=wbc2p[wpar])
                    wbc_at = wbc2p[wpar].at()
                else:
                    wbc_at = None

                slots = ((14, 15, 7) if shared
                         else (e_ - 1, NEXP + e_ - 1, e_ - 1))
                for hb in range(16):
                    u = e_ * 16 + hb
                    lpar = u % 2
                    w1l = w1c[:, lpar * 2048:(lpar + 1) * 2048]
                    wgll = wgc[:, lpar * 2048:(lpar + 1) * 2048]
                    wlp = wld2[lpar]
                    w_wl = [wld_prev[lpar]] if lpar in wld_prev else []

                    def wload(e, slots=slots, hb=hb, w1l=w1l, wgll=wgll):
                        return [
                            e.dma_start(w1l, s.s_fab[slots[0], hb]),
                            e.dma_start(wgll, s.s_fab[slots[1], hb])]
                    dma("sync", wload, waits=w_wl, pipe=wlp, n=2)
                    wl_at = wlp.at()

                    w1l3 = w1l.rearrange("p (k n) -> p k n", k=16)
                    wgl3 = wgll.rearrange("p (k n) -> p k n", k=16)

                    def mmAG(e, w1l3=w1l3, wgl3=wgl3):
                        for k in range(16):
                            for half in range(2):
                                e.matmul(PS[0 + half][:],
                                         w1l3[:, k, :],
                                         xn2_3[:, k, half * 512:(half + 1) * 512],
                                         start=(k == 0), stop=(k == 15))
                        for k in range(16):
                            for half in range(2):
                                ins = e.matmul(
                                    PS[2 + half][:],
                                    wgl3[:, k, :],
                                    xn2_3[:, k, half * 512:(half + 1) * 512],
                                    start=(k == 0), stop=(k == 15))
                        return ins
                    w_mm = [wl_at, ln_done[2]]
                    if u >= 1:
                        w_mm.append(pbuild_at)
                    if u == 0:
                        w_mm += ln_done[:2]
                    step("tensor", mmAG, waits=w_mm)
                    mm_at = pt.at()
                    wld_prev[lpar] = (pt, pt.n)

                    def pbuild(e, hb=hb, wsl=wsl, shared=shared):
                        for half in range(2):
                            e.activation(t1b, PS[0 + half][:], AF.Silu)
                            if shared:
                                e.copy(gbb, PS[2 + half][:])
                            else:
                                pass
                            ins = e.activation(t1b, t1b, AF.Identity)
                        return ins
                    def pact(e):
                        e.activation(t1b, PS[0][:], AF.Silu)
                        return e.activation(t1b[:, 0:0] if False else gbb,
                                            PS[1][:], AF.Silu)
                    def silu0(e):
                        return e.activation(t1b, PS[0][:], AF.Silu)
                    step("scalar", silu0, waits=[mm_at])
                    s0_at = pa.at()

                    def gp0(e, hb=hb, wsl=wsl, shared=shared):
                        if shared:
                            e.tensor_copy(gbb, PS[2][:])
                        else:
                            e.tensor_tensor(gbb, PS[2][:], wsl[:, 0:512],
                                            ALU.mult)
                        return e.tensor_tensor(Pt3[:, hb, 0:512], t1b, gbb,
                                               ALU.mult)
                    w_g0 = [mm_at, s0_at]
                    if wbc_at is not None:
                        w_g0.append(wbc_at)
                    step("vector", gp0, waits=w_g0)

                    def silu1(e):
                        return e.activation(t1b, PS[1][:], AF.Silu)
                    step("scalar", silu1, waits=[(pv, pv.n)])
                    s1_at = pa.at()

                    def gp1(e, hb=hb, wsl=wsl, shared=shared):
                        if shared:
                            e.tensor_copy(gbb, PS[3][:])
                        else:
                            e.tensor_tensor(gbb, PS[3][:], wsl[:, 512:1024],
                                            ALU.mult)
                        return e.tensor_tensor(Pt3[:, hb, 512:1024], t1b, gbb,
                                               ALU.mult)
                    step("vector", gp1, waits=[s1_at])
                    pbuild_at = pv.at()
                P_at = pbuild_at
                wbc_prev[e_ % 2] = (pv, pv.n)

                for db in range(16):
                    u2 = e_ * 16 + db
                    lpar = u2 % 2
                    w2l = w2c[:, lpar * 2048:(lpar + 1) * 2048]
                    w2p = w2ld2[lpar]
                    w_w2 = [w2_prev[lpar]] if lpar in w2_prev else []

                    def w2load(e, slots=slots, db=db, w2l=w2l):
                        return e.dma_start(w2l, s.s_f2b[slots[2], db])
                    dma("sync", w2load, waits=w_w2, pipe=w2p)
                    w2_at = w2p.at()
                    w2l3 = w2l.rearrange("p (k n) -> p k n", k=16)

                    def mmO(e, w2l3=w2l3):
                        for k in range(16):
                            for half in range(2):
                                ins = e.matmul(
                                    PS[4 + half][:], w2l3[:, k, :],
                                    Pt3[:, k, half * 512:(half + 1) * 512],
                                    start=(k == 0), stop=(k == 15))
                        return ins
                    w_mo = [w2_at, P_at]
                    if db >= 1 or e_ >= 1:
                        w_mo.append(oadd_at)
                    if oacc_free is not None and e_ == 0 and db == 0:
                        w_mo.append(oacc_free)
                    step("tensor", mmO, waits=w_mo)
                    mo_at = pt.at()
                    w2_prev[lpar] = (pt, pt.n)

                    def oadd(e, db=db, shared=shared):
                        if shared:
                            e.tensor_copy(oacc3[:, db, 0:512], PS[4][:])
                            return e.tensor_copy(oacc3[:, db, 512:1024],
                                                 PS[5][:])
                        e.tensor_tensor(oacc3[:, db, 0:512],
                                        oacc3[:, db, 0:512], PS[4][:], ALU.add)
                        return e.tensor_tensor(oacc3[:, db, 512:1024],
                                               oacc3[:, db, 512:1024],
                                               PS[5][:], ALU.add)
                    step("vector", oadd, par=True, waits=[mo_at])
                    oadd_at = pv.at()
            pe_xn_last = (pt, pt.n)
            if "dbg_oacc" in self.debug:
                if not hasattr(self, "_dbg_oacc"):
                    self._dbg_oacc = self.dram("dbg_oacc", [NFT, 128, 16 * 1024],
                                               BF16)
                    self._dbg_op = S.pipe("dbg_oacc_p")
                dma("gpsimd", lambda e, t=t: e.dma_start(
                    self._dbg_oacc[t], oacc), waits=[oadd_at],
                    pipe=self._dbg_op)

            for gg in range(8):
                grp = t * 8 + gg
                par = grp % 2
                ylp = yld2[par]
                yl = y1t[:, par * D:(par + 1) * D]

                def yload2(e, grp=grp, yl=yl):
                    return e.dma_start(yl, s.s_y1[grp * 128:(grp + 1) * 128, :])
                dma("sync", yload2, waits=[y1_prev[par], oadd_at], pipe=ylp)
                y_at = ylp.at()

                def otpf(e, gg=gg):
                    return [e.dma_start_transpose(
                        x2tm[:, k * 128:(k + 1) * 128],
                        oacc3[:, k, gg * 128:(gg + 1) * 128]) for k in range(16)]
                w_otp = [oadd_at, (pv, pv.n)]
                if xtp_last is not None:
                    w_otp.append(xtp_last)
                dma("sync", otpf, waits=w_otp, pipe=xtp, n=16)
                xtp_last = xtp.at()

                def cst(e):
                    return e.tensor_copy(sq2, x2tm)
                w_cst = [xtp_last]
                if pa.n:
                    w_cst.append((pa, pa.n))
                step("vector", cst, waits=w_cst)
                cst_at = pv.at()

                def lnsq2(e):
                    return e.activation(st3[:, 8:9] if False else sq2, sq2,
                                        AF.Square, accum_out=None)
                def lnst2a(e):
                    return e.tensor_reduce(st3[:, 8:9], sq2, AX.X, ALU.add)
                step("vector", lnst2a, waits=[cst_at])

                def lnsq2b(e):
                    return e.activation(y1t[:, 0:0] if False else sq2, sq2,
                                        AF.Square)
                step("scalar", lnsq2b, waits=[(pv, pv.n)])
                sqb_at = pa.at()

                def lnst2b(e):
                    e.tensor_reduce(st3[:, 9:10], sq2, AX.X, ALU.add)
                    e.tensor_scalar(st3[:, 10:11], st3[:, 8:9], 1.0 / D, None,
                                    ALU.mult)
                    e.tensor_scalar(st3[:, 11:12], st3[:, 9:10], 1.0 / D, None,
                                    ALU.mult)
                    e.tensor_tensor(st3[:, 12:13], st3[:, 10:11], st3[:, 10:11],
                                    ALU.mult)
                    e.tensor_tensor(st3[:, 11:12], st3[:, 11:12], st3[:, 12:13],
                                    ALU.subtract)
                    e.tensor_scalar(st3[:, 11:12], st3[:, 11:12], LN_EPS, None,
                                    ALU.add)
                    return e.tensor_scalar(st3[:, 10:11], st3[:, 10:11], -1.0,
                                           None, ALU.mult)
                step("vector", lnst2b, waits=[sqb_at])
                l2b_at = pv.at()

                def lnrs2(e):
                    e.activation(st3[:, 11:12], st3[:, 11:12], AF.Ln)
                    return e.activation(st3[:, 11:12], st3[:, 11:12], AF.Exp,
                                        scale=-0.5)
                step("scalar", lnrs2, waits=[l2b_at])
                lrs2_at = pa.at()

                def fin(e, yl=yl):
                    e.tensor_copy(sq2, x2tm)
                    e.tensor_scalar(sq2, sq2, st3[:, 10:11], st3[:, 11:12],
                                    ALU.add, ALU.mult)
                    return e.tensor_tensor(yl, sq2, yl, ALU.add)
                w_fin = [lrs2_at, y_at]
                if par in yst_prev:
                    w_fin.append(yst_prev[par])
                step("vector", fin, waits=w_fin)
                fin_at = pv.at()

                def yst(e, grp=grp, yl=yl):
                    return e.dma_start(s.y_out[grp * 128:(grp + 1) * 128, :],
                                       yl)
                ystp = yst2[par]
                dma("sync", yst, waits=[fin_at], pipe=ystp)
                y1_prev[par] = ystp.at()
                yst_prev[par] = ystp.at()
            oacc_free = (pv, pv.n)
        self.fin_waits = [yst2[0].at(), yst2[1].at()]

    def ph_final(self):
        s, S = self, self.S
        if "dbg_scal" in self.debug:
            dbg = self.dram("dbg_scal", [7, 128, NG * 64], F32)
            dpipe = S.pipe("dbg")
            waits = [p.at() for p in S._pipes.values() if p.n > 0]
            tens = [s.beta_s, s.negsc1_s, s.egjc_s, s.egLmg_s, s.egL_s, s.gj_s,
                    s.ba_tm]
            S.dma("gpsimd", lambda e: [
                e.dma_start(dbg[i, :, :t.ap().shape[1]], t.ap()[:])
                for i, t in enumerate(tens)], waits=waits, pipe=dpipe, n=7)
        waits = [p.at() for p in S._pipes.values() if p.n > 0]
        S.step("sync", lambda e: e.nop(), waits=waits)


def stage_inputs(inputs):
    f32 = np.float32
    sl = np.tril(np.ones((128, 128), f32), -1)
    ui = np.triu(np.ones((128, 128), f32))
    c_masks = np.stack([
        BIG * (1.0 - sl),
        -BIG * (1.0 - ui),
        ui,
        np.eye(128, dtype=f32),
    ]).astype(f32)
    pad_col = np.zeros((D, 1), f32)
    neg = np.full((1,), -1.0e30, f32)
    shared = {
        "w_qkv": np.ascontiguousarray(inputs["L0_attn_W_qkv"]),
        "w_z": np.ascontiguousarray(inputs["L0_attn_W_z"]),
        "w_ba": np.ascontiguousarray(
            np.concatenate([inputs["L0_attn_W_b"], inputs["L0_attn_W_a"]], 1)),
        "conv_w": np.ascontiguousarray(inputs["L0_attn_conv_weight"]),
        "dt_bias_r": np.ascontiguousarray(inputs["L0_attn_dt_bias"][None, :]),
        "a_log_r": np.ascontiguousarray(inputs["L0_attn_A_log"][None, :]),
        "norm_w_r": np.ascontiguousarray(inputs["L0_attn_norm_w"][None, :]),
        "w_out": np.ascontiguousarray(inputs["L0_attn_W_out"]),
        "ffn_w1": np.ascontiguousarray(inputs["L0_ffn_W1"]),
        "ffn_wg": np.ascontiguousarray(inputs["L0_ffn_Wg"]),
        "ffn_w2": np.ascontiguousarray(inputs["L0_ffn_W2"]),
        "ffn_ws1": np.ascontiguousarray(inputs["L0_ffn_Ws1"]),
        "ffn_wsg": np.ascontiguousarray(inputs["L0_ffn_Wsg"]),
        "ffn_ws2": np.ascontiguousarray(inputs["L0_ffn_Ws2"]),
        "gate_w": np.ascontiguousarray(
            np.concatenate([inputs["L0_ffn_gate_w"], pad_col], 1)),
        "gate_b_r": np.ascontiguousarray(
            np.concatenate([inputs["L0_ffn_gate_b"], neg])[None, :]),
        "biases_r": np.ascontiguousarray(
            np.concatenate([inputs["L0_ffn_biases"], neg])[None, :]),
        "c_masks": c_masks,
    }
    shared = {k: np.asarray(v, f32) for k, v in shared.items()}
    hid = np.asarray(inputs["L0_attn_hidden"], f32)
    in_maps = []
    for c in range(8):
        m = dict(shared)
        m["hidden"] = np.ascontiguousarray(hid[c])
        in_maps.append(m)
    return in_maps


_NC_CACHE = {}


def kernel(**inputs):
    key = "full"
    if key not in _NC_CACHE:
        _NC_CACHE[key] = Builder().build()
    nc = _NC_CACHE[key]
    in_maps = stage_inputs(inputs)
    res = run_bass_kernel_spmd(nc, in_maps, core_ids=list(range(8)))
    return np.stack([np.asarray(r["y"], np.float32) for r in res.results])




# version 10
# speedup vs baseline: 1.2104x; 1.0949x over previous
import contextlib
import math
import numpy as np
import concourse.bass as bass
import concourse.mybir as mybir
from concourse.bass_utils import run_bass_kernel_spmd

F32 = mybir.dt.float32
BF16 = mybir.dt.bfloat16
AF = mybir.ActivationFunctionType
ALU = mybir.AluOpType
AX = mybir.AxisListType

T = 4096
D = 2048
HK, HV, AK, UV = 16, 32, 128, 128
KEY_DIM = HK * AK
VALUE_DIM = HV * UV
CONV_DIM = 2 * KEY_DIM + VALUE_DIM
CH = 128
NCH = T // CH
NG = T // 128
BIG = 1.0e6
C_SCALE = AK ** -0.5
LN_EPS = 1e-5
RMS_EPS = 1e-6
L2_EPS = 1e-6
NEXP = 7
FT = 1024
NFT = T // FT
ENGINES = ("sync", "tensor", "vector", "scalar", "gpsimd")


class Pipe:

    def __init__(self, sem, name):
        self.sem = sem
        self.name = name
        self.n = 0

    def at(self):
        return (self, self.n)


_DRAIN_OPS = frozenset([
    "tensor_tensor", "tensor_scalar", "tensor_copy", "tensor_reduce",
    "reduce_sum", "reduce_max", "scalar_tensor_tensor", "reciprocal",
    "memset", "memzero", "select", "bn_stats", "bn_aggr", "activation",
    "copy", "add", "mul", "square", "sqrt", "sign", "tensor_add",
    "tensor_sub", "tensor_mul", "tensor_scalar_add", "tensor_scalar_mul",
    "tensor_scalar_max", "tensor_scalar_min", "tensor_scalar_sub",
    "tensor_relu", "tensor_max", "transpose", "iota", "copy_predicated",
    "max", "max_index", "max_with_indices", "stream_shuffle",
])
_FLUSH_OPS = frozenset(["dma_start", "dma_start_transpose"])


class DrainProxy:

    def __init__(self, e):
        object.__setattr__(self, "_e", e)
        object.__setattr__(self, "_armed", False)
        object.__setattr__(self, "_par", False)
        object.__setattr__(self, "_par_done", False)

    def begin_step(self, par):
        object.__setattr__(self, "_par", par)
        object.__setattr__(self, "_par_done", False)

    def __getattr__(self, name):
        e = object.__getattribute__(self, "_e")
        attr = getattr(e, name)
        if name in _DRAIN_OPS:
            def wrapped(*a, **k):
                par = object.__getattribute__(self, "_par")
                done = object.__getattribute__(self, "_par_done")
                if object.__getattribute__(self, "_armed") and not (par and done):
                    e.drain()
                object.__setattr__(self, "_armed", True)
                object.__setattr__(self, "_par_done", True)
                return attr(*a, **k)
            return wrapped
        if name in _FLUSH_OPS:
            def wrapped2(*a, **k):
                if object.__getattribute__(self, "_armed"):
                    e.drain()
                    object.__setattr__(self, "_armed", False)
                return attr(*a, **k)
            return wrapped2
        return attr


class Sched:

    def __init__(self, nc, exitstack):
        self.nc = nc
        self.exit = exitstack
        self.steps = []
        self._pipes = {}

    def pipe(self, name):
        if name not in self._pipes:
            sem = self.exit.enter_context(self.nc.semaphore(name))
            self._pipes[name] = Pipe(sem, name)
        return self._pipes[name]

    def step(self, engine, fn, waits=(), incs=(), par=False):
        norm = []
        for item in incs:
            if len(item) == 2:
                p_, a_ = item
                norm.append((p_, a_, "last"))
            else:
                norm.append(item)
        self.steps.append((engine, fn, list(waits), norm, par))
        for p_, a_, mode in norm:
            if mode == "each":
                p_.n += a_ * self._count_instrs_hint
            else:
                p_.n += a_
        return tuple(p_.n for p_, _, _ in norm)

    _count_instrs_hint = 1

    def dma(self, engine, fn, waits=(), pipe=None, n=1):
        self._count_instrs_hint = n
        r = self.step(engine, fn, waits,
                      incs=([(pipe, 16, "each")] if pipe else ()))
        self._count_instrs_hint = 1
        return r

    def emit(self, blk):
        by_eng = {e: [] for e in ENGINES}
        for st in self.steps:
            by_eng[st[0]].append(st)
        for eng_name in ENGINES:
            steps = by_eng[eng_name]
            if not steps:
                continue

            def body(e, steps=steps, eng_name=eng_name):
                if eng_name in ("vector", "scalar", "gpsimd"):
                    e = DrainProxy(e)
                watermark = {}
                for _, fn, waits, incs, par_ in steps:
                    if isinstance(e, DrainProxy):
                        e.begin_step(par_)
                    for p_, val in waits:
                        if val > 0 and watermark.get(p_.name, -1) < val:
                            e.wait_ge(p_.sem, val)
                            watermark[p_.name] = val
                    out = fn(e)
                    if incs:
                        instrs = out if isinstance(out, (list, tuple)) else [out]
                        n_on_last = 0
                        for p_, a_, mode in incs:
                            if mode == "each":
                                for ins in instrs:
                                    ins.then_inc(p_.sem, a_)
                            elif n_on_last == 0:
                                instrs[-1].then_inc(p_.sem, a_)
                                n_on_last = 1
                            else:
                                if eng_name in ("vector", "scalar", "gpsimd"):
                                    (e._e if isinstance(e, DrainProxy) else e
                                     ).drain()
                                    if isinstance(e, DrainProxy):
                                        object.__setattr__(e, "_armed", False)
                                e.sem_inc(p_.sem, a_)

            getattr(blk, eng_name)(body)


class Builder:
    def __init__(self, debug=()):
        self.debug = set(debug)
        self.nc = bass.Bass("TRN2", target_bir_lowering=False)
        self.exit = contextlib.ExitStack()

    def dram(self, name, shape, dtype, kind="Internal"):
        k = "ExternalOutput" if name in self.debug else kind
        return self.nc.dram_tensor(name, shape, dtype, kind=k).ap()

    def declare(self):
        nc = self.nc
        dram = self.dram
        inp = lambda n, s: nc.dram_tensor(n, s, F32, kind="ExternalInput").ap()
        self.x_in = inp("hidden", [T, D])
        self.w_qkv = inp("w_qkv", [D, CONV_DIM])
        self.w_z = inp("w_z", [D, VALUE_DIM])
        self.w_ba = inp("w_ba", [D, 64])
        self.conv_w = inp("conv_w", [CONV_DIM, 4])
        self.dt_bias_r = inp("dt_bias_r", [1, HV])
        self.a_log_r = inp("a_log_r", [1, HV])
        self.norm_w_r = inp("norm_w_r", [1, UV])
        self.w_out = inp("w_out", [VALUE_DIM, D])
        self.ffn_w1 = inp("ffn_w1", [NEXP, D, D])
        self.ffn_wg = inp("ffn_wg", [NEXP, D, D])
        self.ffn_w2 = inp("ffn_w2", [NEXP, D, D])
        self.ffn_ws1 = inp("ffn_ws1", [D, D])
        self.ffn_wsg = inp("ffn_wsg", [D, D])
        self.ffn_ws2 = inp("ffn_ws2", [D, D])
        self.gate_w = inp("gate_w", [D, 8])
        self.gate_b_r = inp("gate_b_r", [1, 8])
        self.biases_r = inp("biases_r", [1, 8])
        self.c_masks = inp("c_masks", [4, 128, 128])
        self.y_out = nc.dram_tensor("y", [T, D], F32, kind="ExternalOutput").ap()

        self.s_qf = dram("s_qf", [HK, AK, T], BF16)
        self.s_kf = dram("s_kf", [HK, AK, T], BF16)
        self.s_kt = dram("s_kt", [HK, NG, 128, AK], BF16)
        self.s_vt = dram("s_vt", [HV, NG, 128, UV], BF16)
        self.s_z = dram("s_z", [T, VALUE_DIM], BF16)
        self.s_gj = dram("s_gj", [NCH, HV, CH], F32)
        self.s_gl = dram("s_gl", [NCH, HV], F32)
        self.s_y1 = dram("s_y1", [T, D], F32)
        self.s_wob2 = dram("s_wob2", [8, 128, 4, D], BF16)
        self.s_fab = dram("s_fab", [16, 16, 128, D], BF16)
        self.s_f2b = dram("s_f2b", [8, 16, 128, D], BF16)
        self.s_wg8 = dram("s_wg8", [NFT, 8, FT], BF16)

        A = nc.alloc_sbuf_tensor
        self.masks = A("masks", [128, 4 * 128], F32)
        self.normw_bc = A("normw_bc", [128, UV], F32)
        self.dtb_bc = A("dtb_bc", [128, HV], F32)
        self.negea_bc = A("negea_bc", [128, HV], F32)
        self.gateb_bc = A("gateb_bc", [128, 8], F32)
        self.biases_bc = A("biases_bc", [128, 8], F32)
        self.ba_tm = A("ba_tm", [128, NG * 64], F32)
        self.beta_s = A("beta_s", [128, NCH * HV], F32)
        self.negsc1_s = A("negsc1_s", [128, NCH * HV], F32)
        self.egjc_s = A("egjc_s", [128, NCH * HV], F32)
        self.egLmg_s = A("egLmg_s", [128, NCH * HV], F32)
        self.egL_s = A("egL_s", [128, NCH * HV], F32)
        self.gj_s = A("gj_s", [128, NCH * HV], F32)
        self.S_f = A("S_f", [128, HV * UV], F32)
        self.S_b = A("S_b", [128, HV * UV], BF16)
        self.ARENA_W = 38100
        self.arena = A("arena", [128, self.ARENA_W], F32)
        self.psum = [nc.alloc_psum_tensor(f"pb{i}", [128, 512], F32)
                     for i in range(8)]

        self.m_sl_big = self.masks[:, 0:128]
        self.m_u_negbig = self.masks[:, 128:256]
        self.m_triu = self.masks[:, 256:384]
        self.ones_col = self.masks[:, 383:384]
        self.ident_f = self.masks[:, 384:512]

    def carve(self, off, n):
        assert off + n <= self.ARENA_W, (off, n)
        return self.arena.ap()[:, off:off + n], off + n

    def carve_b(self, off, n_h):
        nw = (n_h + 1) // 2
        assert off + nw <= self.ARENA_W, (off, n_h)
        return self.arena.ap().bitcast(BF16)[:, off * 2: off * 2 + n_h], off + nw

    def pp(self, tens, j, h):
        return tens.ap()[:, j * HV + h: j * HV + h + 1]

    def build(self, phases=("a01", "a2", "a3", "b")):
        self.declare()
        S = Sched(self.nc, self.exit)
        self.S = S
        self.ph_consts()
        if "a01" in phases:
            self.ph_a01()
        if "a2" in phases:
            self.ph_a2()
        if "a3" in phases:
            self.ph_a3()
        if "b" in phases:
            self.ph_b()
        self.ph_final()
        with self.nc.Block() as blk:
            S.emit(blk)
        self.exit.close()
        return self.nc

    def ph_consts(self):
        s, S = self, self.S
        ld = S.pipe("c_ld")
        self.c_done = S.pipe("c_done")
        S.dma("sync", lambda e: [
            e.dma_start(s.masks[:].rearrange("p (k n) -> p k n", k=4),
                        s.c_masks.transpose([1, 0, 2])),
            e.dma_start(s.normw_bc[:], s.norm_w_r.to_broadcast([128, UV])),
            e.dma_start(s.dtb_bc[:], s.dt_bias_r.to_broadcast([128, HV])),
            e.dma_start(s.negea_bc[:], s.a_log_r.to_broadcast([128, HV])),
            e.dma_start(s.gateb_bc[:], s.gate_b_r.to_broadcast([128, 8])),
            e.dma_start(s.biases_bc[:], s.biases_r.to_broadcast([128, 8])),
        ], pipe=ld, n=6)
        c_act = S.pipe("c_act")
        S.step("scalar", lambda e: e.activation(s.negea_bc[:], s.negea_bc[:], AF.Exp),
               waits=[ld.at()], incs=[(c_act, 1)])
        S.step("vector",
               lambda e: e.tensor_scalar(s.negea_bc[:], s.negea_bc[:], -1.0, None,
                                         ALU.mult),
               waits=[c_act.at()], incs=[(self.c_done, 1)])

    def ph_a01(self):
        s, S = self, self.S
        off = 0
        xn_t, off = self.carve_b(off, 16 * 2048)
        xh, off = self.carve(off, 2 * D)
        sq, off = self.carve(off, D)
        xn_tm, off = self.carve_b(off, D)
        st_, off = self.carve(off, 16)
        wba_f, off = self.carve(off, 16 * 64)
        wba_b, off = self.carve_b(off, 16 * 64)
        wst, off = self.carve(off, 2 * 2048)
        wbb, off = self.carve_b(off, 2 * 2048)
        cw, off = self.carve(off, 2 * 4)
        acc, off = self.carve(off, 3 * 512)
        sil, off = self.carve_b(off, 3 * 512)
        tmst, off = self.carve_b(off, 3 * 512)
        sqn, off = self.carve(off, 512)
        ssq, off = self.carve(off, 2 * 8)
        nfm, off = self.carve_b(off, 3 * 512)
        tails, off = self.carve(off, 64 * 4)
        b0f, off = self.carve(off, 2 * 512)
        b0b, off = self.carve_b(off, 2 * 512)
        self.a01_end = off
        xn_t3 = xn_t.rearrange("p (k t) -> p k t", k=16)
        wbb3 = wbb.rearrange("p (k n) -> p k n", k=16)
        wba_b3 = wba_b.rearrange("p (k n) -> p k n", k=16)

        ld2 = [S.pipe("a0_ld0"), S.pipe("a0_ld1")]
        sqp, v1, v2, tp = (S.pipe("a0_sq"), S.pipe("a0_v1"),
                           S.pipe("a0_v2"), S.pipe("a0_tp"))
        gld, gcast, gmm, gev = (S.pipe("g_ld"), S.pipe("g_cast"), S.pipe("g_mm"),
                                S.pipe("g_ev"))
        wld2 = [S.pipe("a1_wld0"), S.pipe("a1_wld1")]
        wcast, amm, acv, asl = (
            S.pipe("a1_wcast"), S.pipe("a1_mm"),
            S.pipe("a1_cv"), S.pipe("a1_sl"))
        atp2 = [S.pipe("a1_tp0"), S.pipe("a1_tp1"), S.pipe("a1_tp2")]
        atp_done = {}
        ntp2 = [S.pipe("a1_ntp0"), S.pipe("a1_ntp1"), S.pipe("a1_ntp2")]
        sp_tm = [S.pipe("a1_sptm0"), S.pipe("a1_sptm1"), S.pipe("a1_sptm2")]
        sp_fm = [S.pipe("a1_spfm0"), S.pipe("a1_spfm1"), S.pipe("a1_spfm2")]
        n_qkv = [0]
        pending_tail = []
        tm_guard = {}
        fm_guard = {}
        ntp_done = {}
        bank_free = {}
        nrm, ntp, nr2 = S.pipe("a1_nrm"), S.pipe("a1_ntp"), S.pipe("a1_nr2")
        qk_units = [0]
        b0p, b0c = S.pipe("b0_dma"), S.pipe("b0_cast")
        self.g_ev = gev

        S.step("vector", lambda e: e.memset(tails, 0.0))

        b0_jobs = []
        amats = ([(s.ffn_w1[e_], e_) for e_ in range(NEXP)]
                 + [(s.ffn_wg[e_], NEXP + e_) for e_ in range(NEXP)]
                 + [(s.ffn_ws1, 14), (s.ffn_wsg, 15)])
        for src_m, slot in amats:
            for r in range(16):
                for hh in range(4):
                    dst = s.s_fab[slot, hh * 4:(hh + 1) * 4, :,
                                  r * 128:(r + 1) * 128].transpose([1, 0, 2])
                    b0_jobs.append((src_m[r * 128:(r + 1) * 128,
                                          hh * 512:(hh + 1) * 512]
                                    .rearrange("p (h c) -> p h c", h=4), dst))
        mats2 = [(s.ffn_w2[e_], e_) for e_ in range(NEXP)] + [(s.ffn_ws2, 7)]
        for src_m, slot in mats2:
            for r in range(16):
                for hh in range(4):
                    dst = s.s_f2b[slot, hh * 4:(hh + 1) * 4, :,
                                  r * 128:(r + 1) * 128].transpose([1, 0, 2])
                    b0_jobs.append((src_m[r * 128:(r + 1) * 128,
                                          hh * 512:(hh + 1) * 512]
                                    .rearrange("p (h c) -> p h c", h=4), dst))
        for r in range(32):
            for hh in range(4):
                b0_jobs.append((s.w_out[r * 128:(r + 1) * 128,
                                        hh * 512:(hh + 1) * 512],
                                s.s_wob2[r // 4, :, r % 4,
                                         hh * 512:(hh + 1) * 512]))
        b0_i = [0]
        import os
        b0_disable = os.environ.get("NO_B0", "0") == "1"

        def b0_emit(n):
            if b0_disable:
                return
            for _ in range(n):
                if b0_i[0] >= len(b0_jobs):
                    return
                i = b0_i[0]
                b0_i[0] += 1
                srcap, dstap = b0_jobs[i]
                bb = (i % 2)
                fsl = b0f[:, bb * 512:(bb + 1) * 512]
                bsl = b0b[:, bb * 512:(bb + 1) * 512]
                w_ = [(b0c, i - 1)] if i >= 2 else []
                S.dma("gpsimd", lambda e, srcap=srcap, fsl=fsl: e.dma_start(
                    fsl if srcap.ndim == 2 else
                    fsl.rearrange("p (h c) -> p h c", h=4), srcap),
                    waits=w_, pipe=b0p)
                eng = "vector" if i % 2 == 0 else "scalar"
                fn = ((lambda e, fsl=fsl, bsl=bsl: e.tensor_copy(bsl, fsl))
                      if eng == "vector" else
                      (lambda e, fsl=fsl, bsl=bsl: e.copy(bsl, fsl)))
                S.step(eng, fn, waits=[b0p.at()], incs=[(b0c, 1)])
                S.dma("gpsimd", lambda e, dstap=dstap, bsl=bsl: e.dma_start(
                    dstap, bsl if dstap.ndim == 2 else
                    bsl.rearrange("p (h c) -> p h c", h=4)),
                    waits=[b0c.at()], pipe=b0p)

        S.dma("sync", lambda e: e.dma_start(
            wba_f.rearrange("p (k n) -> p k n", k=16),
            s.w_ba.rearrange("(k p) n -> p k n", p=128)), pipe=gld)
        S.step("vector", lambda e: e.tensor_copy(wba_b, wba_f),
               waits=[gld.at()], incs=[(gcast, 1)])

        n_units = [0]
        for stile in range(2):
            TB = stile * 2048
            for i in range(16):
                gi = stile * 16 + i
                b = (i % 2) * D
                xhb = xh[:, b:b + D]
                w_free = [(v2, gi - 1)] if gi >= 2 else []
                ldp = ld2[gi % 2]
                S.dma("sync", lambda e, gi=gi, xhb=xhb: e.dma_start(
                    xhb, s.x_in[gi * 128:(gi + 1) * 128, :]),
                    waits=w_free, pipe=ldp)
                S.step("scalar", lambda e, xhb=xhb: e.activation(
                    sq, xhb, AF.Square),
                    waits=[ldp.at(), (v1, gi)], incs=[(sqp, 1)])

                def vstats(e, xhb=xhb):
                    e.tensor_reduce(st_[:, 0:1], xhb, AX.X, ALU.add)
                    e.tensor_reduce(st_[:, 1:2], sq, AX.X, ALU.add)
                    e.tensor_scalar(st_[:, 2:3], st_[:, 0:1], 1.0 / D, None, ALU.mult)
                    e.tensor_scalar(st_[:, 3:4], st_[:, 1:2], 1.0 / D, None, ALU.mult)
                    e.tensor_tensor(st_[:, 4:5], st_[:, 2:3], st_[:, 2:3], ALU.mult)
                    e.tensor_tensor(st_[:, 5:6], st_[:, 3:4], st_[:, 4:5],
                                    ALU.subtract)
                    e.tensor_scalar(st_[:, 5:6], st_[:, 5:6], LN_EPS, None,
                                    ALU.add)
                    return e.tensor_scalar(st_[:, 7:8], st_[:, 2:3], -1.0, None,
                                           ALU.mult)
                S.step("vector", vstats, waits=[ldp.at(), sqp.at()],
                       incs=[(v1, 1)])

                def rstd(e):
                    e.activation(st_[:, 6:7], st_[:, 5:6], AF.Ln)
                    return e.activation(st_[:, 6:7], st_[:, 6:7], AF.Exp,
                                        scale=-0.5)
                S.step("scalar", rstd, waits=[v1.at()], incs=[(sqp, 1)])
                w_tp = [(tp, 256 * gi)] if gi >= 1 else []
                w_stile = [(amm, 384)] if (stile == 1 and i == 0) else []
                S.step("vector", lambda e, xhb=xhb: e.tensor_scalar(
                    xn_tm, xhb, st_[:, 7:8], st_[:, 6:7], ALU.add, ALU.mult),
                    waits=[sqp.at()] + w_tp + w_stile, incs=[(v2, 1)])
                S.dma("sync", lambda e, i=i: [
                    e.dma_start_transpose(xn_t3[:, k, i * 128:(i + 1) * 128],
                                          xn_tm[:, k * 128:(k + 1) * 128])
                    for k in range(16)], waits=[(v2, gi + 1)], pipe=tp, n=16)

            for i in range(16):
                gi = stile * 16 + i
                bank = self.psum[4 + (gi % 2)][:, 0:64]

                def gmmf(e, i=i, bank=bank):
                    for k in range(16):
                        ins = e.matmul(bank, xn_t3[:, k, i * 128:(i + 1) * 128],
                                       wba_b3[:, k, :], start=(k == 0),
                                       stop=(k == 15))
                    return ins
                w = [gcast.at(), (tp, 256 * 16 * (stile + 1))]
                if gi >= 2:
                    w.append((gev, gi - 1))
                S.step("tensor", gmmf, waits=w, incs=[(gmm, 1)])
                S.step("scalar", lambda e, gi=gi, bank=bank: e.copy(
                    s.ba_tm.ap()[:, gi * 64:(gi + 1) * 64], bank),
                    waits=[gmm.at()], incs=[(gev, 1)])

            for cb in range(96):
                is_qk = cb < 32
                is_v = 32 <= cb < 64
                is_z = cb >= 64
                wsl = wst[:, (cb % 2) * 2048:(cb % 2 + 1) * 2048]
                wbl = wbb[:, (cb % 2) * 2048:(cb % 2 + 1) * 2048]
                wbl3 = wbl.rearrange("p (k n) -> p k n", k=16)
                gcb = stile * 96 + cb
                wldp = wld2[gcb % 2]
                w_wfree = [(wcast, gcb - 1)] if gcb >= 2 else []
                if is_z:
                    srcw = s.w_z.rearrange("(k p) n -> p k n", p=128)[
                        :, :, (cb - 64) * 128:(cb - 63) * 128]
                else:
                    srcw = s.w_qkv.rearrange("(k p) n -> p k n", p=128)[
                        :, :, cb * 128:(cb + 1) * 128]
                S.dma("gpsimd", lambda e, srcw=srcw, wsl=wsl: e.dma_start(
                    wsl.rearrange("p (k n) -> p k n", k=16), srcw),
                    waits=w_wfree, pipe=wldp)
                if not is_z:
                    w_cw = ([(acv, 4 * (n_qkv[0] - 1))] if n_qkv[0] >= 2 else [])
                    S.dma("gpsimd", lambda e, cb=cb: e.dma_start(
                        cw[:, (cb % 2) * 4:(cb % 2) * 4 + 4],
                        s.conv_w[cb * 128:(cb + 1) * 128, :]),
                        waits=w_cw, pipe=wldp)
                    n_qkv[0] += 1
                w_cast = [wldp.at()]
                if gcb >= 2:
                    w_cast.append((amm, 4 * (gcb - 1)))
                S.step("vector", lambda e, wsl=wsl, wbl=wbl: e.tensor_copy(
                    wbl, wsl), waits=w_cast, incs=[(wcast, 1)])

                for tt in range(4):
                    u = n_units[0]
                    n_units[0] += 1
                    gtt = stile * 4 + tt
                    bank = self.psum[[0, 1, 2, 3, 6, 7][u % 6]]
                    bb = u % 3
                    accl = acc[:, bb * 512:(bb + 1) * 512]
                    sill = sil[:, bb * 512:(bb + 1) * 512]
                    tmstl = tmst[:, bb * 512:(bb + 1) * 512]
                    sqnl = sqn
                    ssql = ssq[:, (u % 2) * 8:(u % 2) * 8 + 4]
                    nfml = nfm[:, (u % 3) * 512:(u % 3 + 1) * 512]

                    def mmf(e, tt=tt, bank=bank, wbl3=wbl3):
                        for k in range(16):
                            ins = e.matmul(bank[:], wbl3[:, k, :],
                                           xn_t3[:, k, tt * 512:(tt + 1) * 512],
                                           start=(k == 0), stop=(k == 15))
                        return ins
                    w = [wcast.at(), (tp, 256 * 16 * (stile + 1)), gev.at()]
                    if (u - 6) in bank_free:
                        w.append(bank_free[u - 6])
                    S.step("tensor", mmf, waits=w, incs=[(amm, 1)])

                    if not is_z:
                        def convf(e, cb=cb, bank=bank, accl=accl, cwl=cw[
                                :, (cb % 2) * 4:(cb % 2) * 4 + 4]):
                            tl = tails[:, cb * 4:cb * 4 + 3]
                            e.tensor_scalar(accl, bank[:], cwl[:, 3:4], None,
                                            ALU.mult)
                            e.scalar_tensor_tensor(accl[:, 1:512],
                                                   bank[:, 0:511],
                                                   cwl[:, 2:3], accl[:, 1:512],
                                                   ALU.mult, ALU.add)
                            e.scalar_tensor_tensor(accl[:, 2:512],
                                                   bank[:, 0:510],
                                                   cwl[:, 1:2], accl[:, 2:512],
                                                   ALU.mult, ALU.add)
                            e.scalar_tensor_tensor(accl[:, 3:512],
                                                   bank[:, 0:509],
                                                   cwl[:, 0:1], accl[:, 3:512],
                                                   ALU.mult, ALU.add)
                            e.scalar_tensor_tensor(accl[:, 0:3], tl,
                                                   cwl[:, 0:1], accl[:, 0:3],
                                                   ALU.mult, ALU.add) if False \
                                else None
                            for k in range(3):
                                e.scalar_tensor_tensor(
                                    accl[:, 0:3 - k], tl[:, k:3],
                                    cwl[:, k:k + 1], accl[:, 0:3 - k],
                                    ALU.mult, ALU.add)
                            return e.tensor_copy(tl, bank[:, 509:512])
                        w_cv = [amm.at(), (asl, u - 2) if u >= 3 else (asl, 0)]
                        S.step("vector", convf, waits=w_cv, incs=[(acv, 1)])
                        bank_free[u] = (acv, acv.n)

                        w_sil = [acv.at()]
                        if ("atp", u % 3) in atp_done:
                            w_sil.append(atp_done[("atp", u % 3)])
                        S.step("scalar", lambda e, accl=accl, sill=sill:
                               e.activation(sill, accl, AF.Silu),
                               waits=w_sil, incs=[(asl, 1)])
                    else:
                        w_sil = [amm.at()]
                        if ("atp", u % 3) in atp_done:
                            w_sil.append(atp_done[("atp", u % 3)])
                        S.step("scalar", lambda e, sill=sill, bank=bank:
                               e.activation(sill, bank[:], AF.Silu),
                               waits=w_sil, incs=[(asl, 1)])
                        bank_free[u] = (asl, asl.n)

                    w_tmst = [(asl, asl.n)]
                    if ("tm", u % 3) in tm_guard:
                        w_tmst.append(tm_guard[("tm", u % 3)])
                    if ("ntp", u % 3) in ntp_done:
                        w_tmst.append(ntp_done[("ntp", u % 3)])
                    atq = "sync"
                    S.dma(atq, lambda e, sill=sill, tmstl=tmstl: [
                        e.dma_start_transpose(
                            tmstl[:, ss * 128:(ss + 1) * 128],
                            sill[:, ss * 128:(ss + 1) * 128])
                        for ss in range(4)],
                        waits=w_tmst, pipe=atp2[u % 3], n=4)
                    atp_done[("atp", u % 3)] = atp2[u % 3].at()

                    def tail_thunk(u=u, cb=cb, gtt=gtt, TB=TB,
                                   tt=tt, tmstl=tmstl, sqnl=sqnl,
                                   ssql=ssql, nfml=nfml,
                                   is_qk=is_qk, is_v=is_v):
                            if is_qk:
                                kh = cb % 16
                                m_qk = qk_units[0]
                                qk_units[0] += 1
                                dst_t = None if cb < 16 else s.s_kt
                                dst_f = (s.s_qf if cb < 16 else s.s_kf)

                                def nrm12(e, tmstl=tmstl, ssql=ssql):
                                    e.tensor_tensor(sqnl, tmstl, tmstl, ALU.mult)
                                    e.drain()
                                    e.tensor_reduce(
                                        ssql, sqnl.rearrange("p (s c) -> p s c", s=4),
                                        AX.X, ALU.add)
                                    return e.tensor_scalar(ssql, ssql, L2_EPS, None,
                                                           ALU.add)
                                S.step("vector", nrm12,
                                       waits=[atp_done[("atp", u % 3)], (nr2, m_qk)],
                                       incs=[(nrm, 1), (nr2, 1)])

                                def nrm3(e, ssql=ssql):
                                    e.activation(ssql, ssql, AF.Ln)
                                    return e.activation(ssql, ssql, AF.Exp, scale=-0.5)
                                S.step("scalar", nrm3, waits=[nrm.at()], incs=[(nrm, 1)])

                                def nrm4(e, tmstl=tmstl, ssql=ssql):
                                    for ss in range(4):
                                        ins = e.tensor_scalar(
                                            tmstl[:, ss * 128:(ss + 1) * 128],
                                            tmstl[:, ss * 128:(ss + 1) * 128],
                                            ssql[:, ss:ss + 1], None, ALU.mult)
                                    return ins
                                S.step("vector", nrm4, waits=[nrm.at()],
                                       incs=[(nrm, 1)], par=True)
                                if dst_t is not None:
                                    S.dma("sync", lambda e, kh=kh, gtt=gtt,
                                          tmstl=tmstl,
                                          dst_t=dst_t: e.dma_start(
                                              dst_t[kh, gtt * 4:(gtt + 1) * 4].transpose(
                                                  [1, 0, 2]),
                                              tmstl.rearrange("p (s c) -> p s c", s=4)),
                                        waits=[nrm.at()], pipe=sp_tm[u % 3])
                                    tm_guard[("tm", u % 3)] = sp_tm[u % 3].at()
                                ntpp = ntp2[u % 3]
                                w_ntp = [nrm.at()]
                                if ("fm", u % 3) in fm_guard:
                                    w_ntp.append(fm_guard[("fm", u % 3)])
                                S.dma("sync", lambda e, tmstl=tmstl, nfml=nfml: [
                                    e.dma_start_transpose(
                                        nfml[:, ss * 128:(ss + 1) * 128],
                                        tmstl[:, ss * 128:(ss + 1) * 128])
                                    for ss in range(4)],
                                    waits=w_ntp, pipe=ntpp, n=4)
                                ntp_done[("ntp", u % 3)] = ntpp.at()
                                S.dma("sync", lambda e, kh=kh, TB=TB, tt=tt,
                                      nfml=nfml,
                                      dst_f=dst_f: e.dma_start(
                                          dst_f[kh, :, TB + tt * 512:TB + (tt + 1) * 512],
                                          nfml),
                                    waits=[ntpp.at()], pipe=sp_fm[u % 3])
                                fm_guard[("fm", u % 3)] = sp_fm[u % 3].at()
                            elif is_v:
                                vh = cb - 32
                                S.dma("sync", lambda e, vh=vh, gtt=gtt, tmstl=tmstl:
                                      e.dma_start(
                                          s.s_vt[vh, gtt * 4:(gtt + 1) * 4].transpose(
                                              [1, 0, 2]),
                                          tmstl.rearrange("p (s c) -> p s c", s=4)),
                                    waits=[atp_done[("atp", u % 3)]],
                                    pipe=sp_tm[u % 3])
                                tm_guard[("tm", u % 3)] = sp_tm[u % 3].at()
                            else:
                                zb = cb - 64
                                S.dma("sync", lambda e, zb=zb, TB=TB, tt=tt,
                                      tmstl=tmstl:
                                      e.dma_start(
                                          s.s_z[TB + tt * 512:TB + (tt + 1) * 512,
                                                zb * 128:(zb + 1) * 128].rearrange(
                                              "(s p) c -> p s c", p=128),
                                          tmstl.rearrange("p (s c) -> p s c", s=4)),
                                    waits=[atp_done[("atp", u % 3)]],
                                    pipe=sp_tm[u % 3])
                                tm_guard[("tm", u % 3)] = sp_tm[u % 3].at()

                    while pending_tail:
                        pending_tail.pop(0)()
                    pending_tail.append(tail_thunk)
                b0_emit(9)
            while pending_tail:
                pending_tail.pop(0)()
        b0_emit(10 ** 9)
        self.a1_done_waits = ([amm.at(), acv.at(), asl.at(), b0p.at(),
                               nrm.at()]
                              + [p.at() for p in sp_tm + sp_fm + atp2 + ntp2])

    def ph_a2(self):
        s, S = self, self.S
        off = self.a01_end
        lanes, off = self.carve(off, 4 * HV)
        gl_row, off = self.carve(off, HV)
        gjf, off = self.carve(off, 2 * 128)
        self.a2_end = off
        gjf_sb = gjf
        a2v, a2s, a2g, mm, ev, wr, rd = (S.pipe("a2_v"), S.pipe("a2_s"),
                                         S.pipe("a2_g"), S.pipe("a2_mm"),
                                         S.pipe("a2_ev"), S.pipe("a2_wr"),
                                         S.pipe("a2_rd"))
        self.a2_done = S.pipe("a2_done")
        for j in range(NCH):
            h0 = (j % 2) * 2 * HV
            x1 = lanes[:, h0:h0 + HV]
            t1 = lanes[:, h0 + HV:h0 + 2 * HV]
            bsl = s.ba_tm.ap()[:, j * 64:j * 64 + 32]
            asl = s.ba_tm.ap()[:, j * 64 + 32:j * 64 + 64]
            gj = s.gj_s.ap()[:, j * HV:(j + 1) * HV]
            w_lane = [(mm, 2 * j - 3)] if j >= 2 else []
            S.step("vector", lambda e, x1=x1, asl=asl: e.tensor_tensor(
                x1, asl, s.dtb_bc[:], ALU.add),
                waits=[self.g_ev.at(), self.c_done.at()] + w_lane, incs=[(a2v, 1)])

            def sp_chain(e, x1=x1, t1=t1):
                e.activation(t1, x1, AF.Abs)
                e.activation(t1, t1, AF.Exp, scale=-1.0)
                return e.activation(t1, t1, AF.Ln, bias=1.0)
            S.step("scalar", sp_chain, waits=[a2v.at()], incs=[(a2s, 1)])

            def gfin(e, x1=x1, t1=t1):
                e.tensor_scalar(x1, x1, 0.0, None, ALU.max)
                e.tensor_tensor(t1, x1, t1, ALU.add)
                return e.tensor_tensor(t1, t1, s.negea_bc[:], ALU.mult)
            S.step("vector", gfin, waits=[a2s.at()], incs=[(a2g, 1)])

            bank = self.psum[2 + (j % 2)]

            def cum(e, t1=t1, bank=bank):
                e.matmul(bank[:, 0:HV], s.m_triu, t1, start=True, stop=True)
                return e.matmul(bank[0:1, HV:2 * HV], s.ones_col, t1,
                                start=True, stop=True)
            w_bank = [(ev, j - 1)] if j >= 2 else []
            if j == 0:
                w_bank += [S.pipe("a1_cv").at(), S.pipe("a1_sl").at(),
                           S.pipe("g_mm").at()]
            S.step("tensor", cum, waits=[a2g.at()] + w_bank, incs=[(mm, 1)])

            def evac(e, gj=gj, bank=bank):
                e.tensor_copy(gj, bank[:, 0:HV])
                return e.tensor_copy(gl_row[0:1, :], bank[0:1, HV:2 * HV])
            S.step("vector", evac, waits=[mm.at(), (wr, 32 * max(0, j - 1))],
                   incs=[(ev, 1)])
            tbank = self.psum[6 + (j % 2)]

            def gjtp(e, gj=gj, tbank=tbank):
                return e.transpose(tbank[0:32, 0:128], gj, s.ident_f)
            S.step("tensor", gjtp, waits=[ev.at(), (wr, 32 * max(0, j - 1))],
                   incs=[(mm, 1)])
            S.step("scalar", lambda e, j=j, tbank=tbank: e.copy(
                gjf_sb[0:32, (j % 2) * 128:(j % 2 + 1) * 128], tbank[0:32, 0:128]),
                waits=[mm.at(), (wr, 32 * max(0, j - 1))], incs=[(ev, 1)])
            S.dma("sync", lambda e, j=j: [
                e.dma_start(s.s_gj[j], gjf_sb[0:32, (j % 2) * 128:(j % 2 + 1) * 128]),
                e.dma_start(s.s_gl[j:j + 1, :], gl_row[0:1, :])],
                waits=[ev.at()], pipe=wr, n=2)
            egl = s.egL_s.ap()[:, j * HV:(j + 1) * HV]
            S.dma("sync", lambda e, j=j, egl=egl: e.dma_start(
                egl, s.s_gl[j:j + 1, :].to_broadcast([128, HV])),
                waits=[wr.at()], pipe=rd)

            def deriv_s(e, j=j, gj=gj, bsl=bsl):
                jj = slice(j * HV, (j + 1) * HV)
                e.activation(s.beta_s.ap()[:, jj], bsl, AF.Sigmoid)
                e.activation(s.egjc_s.ap()[:, jj], gj, AF.Exp)
                return e.activation(s.negsc1_s.ap()[:, jj], gj, AF.Exp)
            S.step("scalar", deriv_s, waits=[ev.at(), rd.at()], incs=[(a2s, 1)])

            def deriv_v(e, j=j, gj=gj, egl=egl):
                jj = slice(j * HV, (j + 1) * HV)
                e.tensor_scalar(s.egjc_s.ap()[:, jj], s.egjc_s.ap()[:, jj],
                                C_SCALE, None, ALU.mult)
                e.tensor_tensor(s.egLmg_s.ap()[:, jj], egl, gj, ALU.subtract)
                return e.scalar_tensor_tensor(
                    s.negsc1_s.ap()[:, jj], s.negsc1_s.ap()[:, jj], -1.0,
                    s.beta_s.ap()[:, jj], ALU.mult, ALU.mult)
            S.step("vector", deriv_v, waits=[a2s.at()], incs=[(a2v, 1)])

            def deriv_s2(e, j=j, egl=egl):
                jj = slice(j * HV, (j + 1) * HV)
                e.activation(s.egLmg_s.ap()[:, jj], s.egLmg_s.ap()[:, jj], AF.Exp)
                return e.activation(egl, egl, AF.Exp)
            S.step("scalar", deriv_s2, waits=[a2v.at()], incs=[(self.a2_done, 1)])


    def ph_a3(self):
        s, S = self, self.S
        off = 0
        qf_t, off = self.carve_b(off, 2 * 2048)
        kf_t, off = self.carve_b(off, 2 * 2048)
        kt_t, off = self.carve_b(off, 2 * 2048)
        vt_t, off = self.carve_b(off, 2 * 4096)
        zt_t, off = self.carve_b(off, 2 * 4096)
        gjbc, off = self.carve(off, 2 * 512)
        A0_2, off = self.carve_b(off, 2 * 512)
        A0T_2, off = self.carve_b(off, 2 * 512)
        Bc_l, off = self.carve_b(off, 512)
        BTc_l, off = self.carve_b(off, 512)
        Bn_l, off = self.carve_b(off, 512)
        BTn_l, off = self.carve_b(off, 512)
        Xc_l, off = self.carve_b(off, 2 * 512)
        vb_2, off = self.carve_b(off, 2 * 512)
        kd_2, off = self.carve_b(off, 2 * 512)
        iT_2, off = self.carve_b(off, 2 * 512)
        X_l, off = self.carve(off, 512)
        em_l, off = self.carve(off, 512)
        em2_l, off = self.carve(off, 512)
        t1_l, off = self.carve(off, 512)
        xg_l, off = self.carve(off, 512)
        rs_l, off = self.carve(off, 16)
        nm_l, off = self.carve_b(off, 512)
        nfm, off = self.carve_b(off, 2 * 4096)
        att, off = self.carve(off, 2 * D)
        sq2, off = self.carve(off, D)
        xr_l, off = self.carve(off, 2 * D)
        wob_l, off = self.carve_b(off, 2 * 1024)
        assert off <= self.ARENA_W, off

        pt, pv, pa = S.pipe("d_pt"), S.pipe("d_pv"), S.pipe("d_pa")
        ld2 = [S.pipe("d_ld0"), S.pipe("d_ld1")]
        gj2 = [S.pipe("d_gj0"), S.pipe("d_gj1")]
        a0t2 = [S.pipe("d_a0t0"), S.pipe("d_a0t1")]
        ntp2 = [S.pipe("d_ntp0"), S.pipe("d_ntp1")]
        wob2 = [S.pipe("d_wob0"), S.pipe("d_wob1")]
        xld2 = [S.pipe("d_xld0"), S.pipe("d_xld1")]
        y1p2 = [S.pipe("d_y10"), S.pipe("d_y11")]
        PS = self.psum

        bar = list(self.a1_done_waits) + [
            p.at() for nm_, p in S._pipes.items()
            if nm_.startswith("a2_") or nm_ == "g_ev"]
        first = {e: True for e in ENGINES}

        def step(eng, fn, waits=(), par=False):
            w = list(waits)
            if first[eng]:
                w = w + bar
                first[eng] = False
            pp = {"tensor": pt, "vector": pv, "scalar": pa}.get(eng)
            return S.step(eng, fn, w, incs=[(pp, 1)] if pp else [], par=par)

        def dma(eng, fn, waits=(), pipe=None, n=1):
            w = list(waits)
            if first[eng]:
                w = w + bar
                first[eng] = False
            return S.dma(eng, fn, w, pipe=pipe, n=n)

        step("vector", lambda e: [e.memset(s.S_f.ap()[:], 0.0),
                                  e.memset(s.S_b.ap()[:], 0.0)])
        sinit_at = pv.at()

        sb_ready = {}
        ld_done = {}
        gj_done = {}
        a0t_prev = {}
        ntp_prev = {}
        wob_prev = {}
        xld_prev = {}
        y1_prev = {}
        ap_free = {}
        lv_free = {}
        gram_free = None
        state_free = None
        a3_pending = []
        ps6_free = sinit_at
        lv0_last = None
        out_last = None
        pss_last = None
        a0t_last = None
        ntp_last = None
        wout_last = None
        att_free = None
        nm_done = {}
        NL = 6
        self.y1_done = []

        for j in range(NCH):
            par = j % 2
            qf = qf_t[:, par * 2048:(par + 1) * 2048]
            kf = kf_t[:, par * 2048:(par + 1) * 2048]
            kt = kt_t[:, par * 2048:(par + 1) * 2048]
            vt = vt_t[:, par * 4096:(par + 1) * 4096]
            zt = zt_t[:, par * 4096:(par + 1) * 4096]
            ldp = ld2[par]
            w_ld = ld_done.get(j - 2, [])

            def loads(e, j=j, qf=qf, kf=kf, kt=kt, vt=vt, zt=zt):
                return [
                    e.dma_start(qf.rearrange("p (h c) -> p h c", h=16),
                                s.s_qf[:, :, j * 128:(j + 1) * 128]
                                .transpose([1, 0, 2])),
                    e.dma_start(kf.rearrange("p (h c) -> p h c", h=16),
                                s.s_kf[:, :, j * 128:(j + 1) * 128]
                                .transpose([1, 0, 2])),
                    e.dma_start(kt.rearrange("p (h c) -> p h c", h=16),
                                s.s_kt[:, j].transpose([1, 0, 2])),
                    e.dma_start(vt.rearrange("p (h c) -> p h c", h=32),
                                s.s_vt[:, j].transpose([1, 0, 2])),
                    e.dma_start(zt, s.s_z[j * 128:(j + 1) * 128, :])]
            dma("sync", loads, waits=w_ld, pipe=ldp, n=5)
            ld_at = ldp.at()

            for g in range(8):
                heads = [4 * g + i for i in range(4)]
                pairs = [2 * g, 2 * g + 1]
                gi = j * 8 + g
                gpar = gi % 2
                A0_l = A0_2[:, gpar * 512:(gpar + 1) * 512]
                A0T_l = A0T_2[:, gpar * 512:(gpar + 1) * 512]
                vb_l = vb_2[:, gpar * 512:(gpar + 1) * 512]
                kd_l = kd_2[:, gpar * 512:(gpar + 1) * 512]
                iT_l = iT_2[:, gpar * 512:(gpar + 1) * 512]
                gjb = gjbc[:, gpar * 512:(gpar + 1) * 512]
                gjp = gj2[gpar]
                w_gj = [gj_done[gi - 2]] if (gi - 2) in gj_done else []
                if (gi - 2) in gj_done:
                    pass

                def gjload(e, j=j, g=g, gjb=gjb):
                    return [e.dma_start(
                        gjb[:, i * 128:(i + 1) * 128],
                        s.s_gj[j, 4 * g + i:4 * g + i + 1, :]
                        .to_broadcast([128, 128]))
                        for i in range(4)]
                dma("gpsimd", gjload, waits=w_gj, pipe=gjp, n=4)
                gj_at = gjp.at()

                def grams(e, kf=kf, qf=qf, pairs=pairs):
                    for i, p in enumerate(pairs):
                        kfp = kf[:, p * 128:(p + 1) * 128]
                        qfp = qf[:, p * 128:(p + 1) * 128]
                        e.matmul(PS[0][:, i * 256:i * 256 + 128], kfp, kfp,
                                 start=True, stop=True)
                        ins = e.matmul(PS[0][:, i * 256 + 128:i * 256 + 256],
                                       kfp, qfp, start=True, stop=True)
                    return ins
                w_gram = [ld_at]
                if gram_free is not None:
                    w_gram.append(gram_free)
                step("tensor", grams, waits=w_gram)
                gram_at = pt.at()

                def prepX(e, j=j, heads=heads, gjb=gjb):
                    for i, h in enumerate(heads):
                        ins = e.tensor_scalar(X_l[:, i * 128:(i + 1) * 128],
                                              gjb[:, i * 128:(i + 1) * 128],
                                              s.pp(s.gj_s, j, h), None,
                                              ALU.subtract)
                    return ins
                S.step("vector", prepX, [gj_at], incs=[(pv, 1)], par=True)

                def prepM(e):
                    for i in range(4):
                        e.tensor_tensor(em_l[:, i * 128:(i + 1) * 128],
                                        X_l[:, i * 128:(i + 1) * 128],
                                        s.m_sl_big, ALU.add)
                        ins = e.tensor_tensor(em2_l[:, i * 128:(i + 1) * 128],
                                              X_l[:, i * 128:(i + 1) * 128],
                                              s.m_u_negbig, ALU.add)
                    return ins
                S.step("vector", prepM, [], incs=[(pv, 1)], par=True)
                prep_at = pv.at()
                gj_done[gi] = (pv, pv.n)

                def exps(e):
                    e.activation(em_l, em_l, AF.Exp, scale=-1.0)
                    return e.activation(em2_l, em2_l, AF.Exp)
                step("scalar", exps, par=True, waits=[prep_at])
                exps_at = pa.at()

                def a0build(e, j=j, heads=heads, A0_l=A0_l, iT_l=iT_l):
                    for i, h in enumerate(heads):
                        pq = i // 2
                        e.scalar_tensor_tensor(
                            A0_l[:, i * 128:(i + 1) * 128],
                            PS[0][:, pq * 256:pq * 256 + 128],
                            s.pp(s.beta_s, j, h),
                            em_l[:, i * 128:(i + 1) * 128], ALU.mult, ALU.mult)
                        ins = e.tensor_tensor(
                            iT_l[:, i * 128:(i + 1) * 128],
                            PS[0][:, pq * 256 + 128:pq * 256 + 256],
                            em2_l[:, i * 128:(i + 1) * 128], ALU.mult)
                    return ins
                w_a0 = [exps_at, gram_at]
                if a0t_last is not None:
                    w_a0.append(a0t_last)
                if lv0_last is not None:
                    w_a0.append(lv0_last)
                if out_last is not None:
                    w_a0.append(out_last)
                step("vector", a0build, par=True, waits=w_a0)
                a0_at = pv.at()
                gram_free = a0_at

                a0tp = a0t2[gpar]
                w_a0t = [a0_at]
                if lv0_last is not None:
                    w_a0t.append(lv0_last)
                if gpar in a0t_prev:
                    w_a0t.append(a0t_prev[gpar])

                def a0tpf(e, A0_l=A0_l, A0T_l=A0T_l):
                    return [e.dma_start_transpose(
                        A0T_l[:, i * 128:(i + 1) * 128],
                        A0_l[:, i * 128:(i + 1) * 128]) for i in range(4)]
                dma("sync", a0tpf, waits=w_a0t, pipe=a0tp, n=4)
                a0t_prev[gpar] = a0tp.at()
                a0t_last = a0tp.at()
                a0t_at = a0tp.at()

                def vbf(e, j=j, heads=heads, vt=vt, vb_l=vb_l):
                    for i, h in enumerate(heads):
                        ins = e.tensor_scalar(
                            vb_l[:, i * 128:(i + 1) * 128],
                            vt[:, h * 128:(h + 1) * 128],
                            s.pp(s.beta_s, j, h), None, ALU.mult)
                    return ins
                step("vector", vbf, par=True, waits=[ld_at])
                vb_at = pv.at()

                def a3_tail(j=j, g=g, gi=gi, gpar=gpar, heads=heads,
                            pairs=pairs, qf=qf, kf=kf, kt=kt, vt=vt,
                            zt=zt, A0_l=A0_l, A0T_l=A0T_l, vb_l=vb_l,
                            kd_l=kd_l, iT_l=iT_l, ld_at=ld_at,
                            vb_at=vb_at, a0_at=a0_at, a0t_at=a0t_at):
                    nonlocal ps6_free, state_free, ntp_last, pss_last
                    nonlocal out_last, lv0_last
                    def psR(e, kf=kf, heads=heads, pairs=pairs):
                        for i, h in enumerate(heads):
                            p = pairs[i // 2]
                            ins = e.matmul(
                                PS[6][:, i * 128:(i + 1) * 128],
                                kf[:, p * 128:(p + 1) * 128],
                                s.S_b.ap()[:, h * 128:(h + 1) * 128],
                                start=True, stop=True)
                        return ins
                    w_psr = [ld_at, ps6_free]
                    if (j - 1, g) in sb_ready:
                        w_psr.append(sb_ready[(j - 1, g)])
                    else:
                        w_psr.append(sinit_at)
                    if ap_free.get(1) is not None:
                        w_psr.append(ap_free[1])
                    step("tensor", psR, waits=w_psr)
                    psr_at = pt.at()

                    def xc0(e, j=j, heads=heads, vb_l=vb_l):
                        for i, h in enumerate(heads):
                            ins = e.scalar_tensor_tensor(
                                Xc_l[:, i * 128:(i + 1) * 128],
                                PS[6][:, i * 128:(i + 1) * 128],
                                s.pp(s.negsc1_s, j, h),
                                vb_l[:, i * 128:(i + 1) * 128], ALU.mult, ALU.add)
                        return ins
                    step("vector", xc0, par=True, waits=[psr_at, vb_at])
                    xc_at = pv.at()
                    ps6_free = xc_at

                    cur_B, cur_BT = A0_l, A0T_l
                    cur_ready = a0t_at
                    nxt = [(Bn_l, BTn_l), (Bc_l, BTc_l)]
                    for k in range(7):
                        apb = PS[5 + (k % 2)]
                        xin = Xc_l[:, (k % 2) * 512:(k % 2) * 512 + 512]
                        xout = Xc_l[:, ((k + 1) % 2) * 512:((k + 1) % 2) * 512 + 512]

                        def apf(e, cur_BT=cur_BT, xin=xin, apb=apb):
                            for i in range(4):
                                ins = e.matmul(apb[:, i * 128:(i + 1) * 128],
                                               cur_BT[:, i * 128:(i + 1) * 128],
                                               xin[:, i * 128:(i + 1) * 128],
                                               start=True, stop=True)
                            return ins
                        w_ap = [cur_ready, xc_at]
                        if ap_free.get(k % 2) is not None:
                            w_ap.append(ap_free[k % 2])
                        if k % 2 == 1:
                            w_ap.append(ps6_free)
                        step("tensor", apf, waits=w_ap)
                        ap_at = pt.at()

                        def apv(e, xin=xin, xout=xout, apb=apb, k=k):
                            return e.tensor_tensor(
                                xout, xin, apb[:],
                                ALU.subtract if k == 0 else ALU.add)
                        step("vector", apv, waits=[ap_at])
                        xc_at = pv.at()
                        ap_free[k % 2] = xc_at
                        if k % 2 == 1:
                            ps6_free = xc_at

                        if k < NL:
                            lvb = (1 + (k % 2) * 2, 2 + (k % 2) * 2)
                            need_B = (k + 1) < NL
                            Bnxt, BTnxt = nxt[k % 2]

                            def lvf(e, cur_B=cur_B, cur_BT=cur_BT, lvb=lvb,
                                    need_B=need_B):
                                for i in range(4):
                                    if need_B:
                                        e.matmul(
                                            PS[lvb[0]][:, i * 128:(i + 1) * 128],
                                            cur_BT[:, i * 128:(i + 1) * 128],
                                            cur_B[:, i * 128:(i + 1) * 128],
                                            start=True, stop=True)
                                    ins = e.matmul(
                                        PS[lvb[1]][:, i * 128:(i + 1) * 128],
                                        cur_B[:, i * 128:(i + 1) * 128],
                                        cur_BT[:, i * 128:(i + 1) * 128],
                                        start=True, stop=True)
                                return ins
                            w_lv = [cur_ready]
                            if lv_free.get(k % 2) is not None:
                                w_lv.append(lv_free[k % 2])
                            step("tensor", lvf, waits=w_lv)
                            lv_at = pt.at()
                            if k == 0:
                                lv0_last = lv_at

                            def lvev(e, lvb=lvb, Bnxt=Bnxt, BTnxt=BTnxt,
                                     need_B=need_B):
                                if need_B:
                                    e.copy(Bnxt, PS[lvb[0]][:])
                                return e.copy(BTnxt, PS[lvb[1]][:])
                            step("scalar", lvev, par=True, waits=[lv_at])
                            lvev_at = pa.at()
                            lv_free[k % 2] = lvev_at
                            cur_B, cur_BT = Bnxt, BTnxt
                            cur_ready = lvev_at
                    v_new = Xc_l[:, 512:1024]
                    vnew_at = xc_at

                    def qSf(e, qf=qf, heads=heads, pairs=pairs):
                        for i, h in enumerate(heads):
                            p = pairs[i // 2]
                            ins = e.matmul(
                                PS[6][:, i * 128:(i + 1) * 128],
                                qf[:, p * 128:(p + 1) * 128],
                                s.S_b.ap()[:, h * 128:(h + 1) * 128],
                                start=True, stop=True)
                        return ins
                    w_qs = [ps6_free, ap_free[1]]
                    if (j - 1, g) in sb_ready:
                        w_qs.append(sb_ready[(j - 1, g)])
                    else:
                        w_qs.append(sinit_at)
                    step("tensor", qSf, waits=w_qs)
                    qs_at = pt.at()

                    def t1f(e, j=j, heads=heads):
                        for i, h in enumerate(heads):
                            ins = e.tensor_scalar(
                                t1_l[:, i * 128:(i + 1) * 128],
                                PS[6][:, i * 128:(i + 1) * 128],
                                s.pp(s.egjc_s, j, h), None, ALU.mult)
                        return ins
                    step("vector", t1f, par=True, waits=[qs_at])
                    t1_at = pv.at()

                    def outf(e, v_new=v_new, iT_l=iT_l):
                        for i in range(4):
                            ins = e.matmul(PS[6][:, i * 128:(i + 1) * 128],
                                           iT_l[:, i * 128:(i + 1) * 128],
                                           v_new[:, i * 128:(i + 1) * 128],
                                           start=True, stop=True)
                        return ins
                    step("tensor", outf, waits=[t1_at, vnew_at, a0_at])
                    out_at = pt.at()
                    out_last = out_at

                    def kdf(e, j=j, heads=heads, kt=kt, pairs=pairs, kd_l=kd_l):
                        for i, h in enumerate(heads):
                            p = pairs[i // 2]
                            ins = e.tensor_scalar(
                                kd_l[:, i * 128:(i + 1) * 128],
                                kt[:, p * 128:(p + 1) * 128],
                                s.pp(s.egLmg_s, j, h), None, ALU.mult)
                        return ins
                    w_kd = [ld_at]
                    if pss_last is not None:
                        w_kd.append(pss_last)
                    step("vector", kdf, par=True, waits=w_kd)
                    kd_at = pv.at()

                    def psSf(e, v_new=v_new, kd_l=kd_l):
                        for i in range(4):
                            ins = e.matmul(PS[7][:, i * 128:(i + 1) * 128],
                                           kd_l[:, i * 128:(i + 1) * 128],
                                           v_new[:, i * 128:(i + 1) * 128],
                                           start=True, stop=True)
                        return ins
                    w_pss = [kd_at, vnew_at]
                    if state_free is not None:
                        w_pss.append(state_free)
                    step("tensor", psSf, waits=w_pss)
                    pss_at = pt.at()
                    pss_last = pss_at

                    def supd1(e, j=j, heads=heads):
                        for i, h in enumerate(heads):
                            ins = e.scalar_tensor_tensor(
                                s.S_f.ap()[:, h * 128:(h + 1) * 128],
                                s.S_f.ap()[:, h * 128:(h + 1) * 128],
                                s.pp(s.egL_s, j, h),
                                PS[7][:, i * 128:(i + 1) * 128], ALU.mult, ALU.add)
                        return ins
                    S.step("vector", supd1, [pss_at], incs=[(pv, 1)], par=True)

                    def supd(e, heads=heads):
                        return e.tensor_copy(
                            s.S_b.ap()[:, heads[0] * 128:(heads[0] + 4) * 128],
                            s.S_f.ap()[:, heads[0] * 128:(heads[0] + 4) * 128])
                    step("vector", supd, waits=[])
                    sb_ready[(j, g)] = pv.at()
                    state_free = pv.at()

                    def xgf1(e):
                        for i in range(4):
                            ins = e.scalar_tensor_tensor(
                                t1_l[:, i * 128:(i + 1) * 128],
                                PS[6][:, i * 128:(i + 1) * 128],
                                C_SCALE,
                                t1_l[:, i * 128:(i + 1) * 128], ALU.mult, ALU.add)
                        return ins
                    S.step("vector", xgf1, [out_at, t1_at], incs=[(pv, 1)], par=True)

                    def xgf(e, g=g, zt=zt):
                        return e.tensor_tensor(
                            xg_l, t1_l, zt[:, g * 512:(g + 1) * 512], ALU.mult)
                    step("vector", xgf, waits=[])
                    xg_at = pv.at()
                    ps6_free = xg_at

                    step("scalar", lambda e: e.activation(em_l, xg_l, AF.Square),
                         waits=[xg_at])
                    sq_at = pa.at()

                    def redf(e):
                        e.tensor_reduce(rs_l[:, 0:4],
                                        em_l.rearrange("p (s c) -> p s c", s=4),
                                        AX.X, ALU.add)
                        return e.tensor_scalar(rs_l[:, 0:4], rs_l[:, 0:4],
                                               RMS_EPS * UV, None, ALU.add)
                    step("vector", redf, waits=[sq_at])
                    red_at = pv.at()

                    def rsqf(e):
                        e.activation(rs_l[:, 4:8], rs_l[:, 0:4], AF.Ln,
                                     scale=1.0 / UV)
                        return e.activation(rs_l[:, 4:8], rs_l[:, 4:8], AF.Exp,
                                            scale=-0.5)
                    step("scalar", rsqf, waits=[red_at])
                    rsq_at = pa.at()

                    def nrmf(e):
                        for i in range(4):
                            ins = e.scalar_tensor_tensor(
                                nm_l[:, i * 128:(i + 1) * 128],
                                xg_l[:, i * 128:(i + 1) * 128],
                                rs_l[:, 4 + i:5 + i],
                                s.normw_bc[:], ALU.mult, ALU.mult)
                        return ins
                    w_nrm = [rsq_at]
                    if ntp_last is not None:
                        w_nrm.append(ntp_last)
                    step("vector", nrmf, par=True, waits=w_nrm)
                    nrm_at = pv.at()

                    ntpp = ntp2[gpar]
                    w_ntp = [nrm_at]
                    if gpar in ntp_prev:
                        w_ntp.append(ntp_prev[gpar])
                    if wout_last is not None:
                        w_ntp.append(wout_last)
                    half = j % 2

                    def ntpf(e, g=g, half=half):
                        return [e.dma_start_transpose(
                            nfm[:, half * 4096 + (4 * g + i) * 128:
                                half * 4096 + (4 * g + i + 1) * 128],
                            nm_l[:, i * 128:(i + 1) * 128]) for i in range(4)]
                    dma("sync", ntpf, waits=w_ntp, pipe=ntpp, n=4)
                    ntp_prev[gpar] = ntpp.at()
                    ntp_last = ntpp.at()
                    globals()
                while a3_pending:
                    a3_pending.pop(0)()
                a3_pending.append(a3_tail)


            while a3_pending:
                a3_pending.pop(0)()
            ld_done[j] = [(pt, pt.n), (pv, pv.n)]
            nm_done[j] = (ntp2[0].at(), ntp2[1].at())

            if j % 2 == 1:
                m2 = j // 2
                w_nfm = [nm_done[j - 1][0], nm_done[j - 1][1],
                         nm_done[j][0], nm_done[j][1]]
                ev_at = None
                for ds in range(8):
                    for vq in range(8):
                        wpar = vq % 2
                        wobp = wob2[wpar]
                        wsl = wob_l[:, wpar * 1024:(wpar + 1) * 1024]
                        w_wob = []
                        if (m2 * 64 + ds * 8 + vq) >= 2:
                            w_wob.append(wob_prev[wpar])

                        def wobl(e, ds=ds, vq=vq, wsl=wsl):
                            return e.dma_start(
                                wsl.rearrange("p (v c) -> p v c", v=4),
                                s.s_wob2[vq, :, :, ds * 256:(ds + 1) * 256])
                        dma("sync", wobl, waits=w_wob, pipe=wobp)
                        wob_at = wobp.at()

                        def wmm(e, ds=ds, vq=vq, wsl=wsl):
                            tgb = [PS[7], PS[1]]
                            for i in range(4):
                                for tg in range(2):
                                    ins = e.matmul(
                                        tgb[tg][:, 0:256],
                                        nfm[:, tg * 4096 +
                                            (vq * 4 + i) * 128:
                                            tg * 4096 + (vq * 4 + i + 1) * 128],
                                        wsl[:, i * 256:(i + 1) * 256],
                                        start=(vq == 0 and i == 0),
                                        stop=(vq == 7 and i == 3))
                            return ins
                        w_mm = [wob_at] + w_nfm
                        if vq == 0 and ev_at is not None:
                            w_mm.append(ev_at)
                        if vq == 0 and att_free is not None:
                            w_mm.append(att_free)
                        if ds == 0 and vq == 0:
                            w_mm += [v for v in (lv_free.get(0), lv_free.get(1),
                                                 state_free)
                                     if v is not None]
                        step("tensor", wmm, waits=w_mm)
                        wob_prev[wpar] = (pt, pt.n)
                    wmm_at = pt.at()

                    def wev(e, ds=ds):
                        e.copy(att[:, 0 * 2048 + ds * 256:
                                   0 * 2048 + (ds + 1) * 256],
                               PS[7][:, 0:256])
                        return e.copy(att[:, 1 * 2048 + ds * 256:
                                          1 * 2048 + (ds + 1) * 256],
                                      PS[1][:, 0:256])
                    step("scalar", wev, waits=[wmm_at])
                    ev_at = pa.at()
                wout_last = pt.at()
                lv_free[0] = ev_at
                state_free = ev_at

                for tg in range(2):
                    grp = 2 * m2 + tg
                    xpar = grp % 2
                    xlp = xld2[xpar]
                    xrl = xr_l[:, xpar * D:(xpar + 1) * D]
                    y1l = xrl
                    w_x = []
                    if xpar in y1_prev:
                        w_x.append(y1_prev[xpar])

                    def xload(e, grp=grp, xrl=xrl):
                        return e.dma_start(
                            xrl, s.x_in[grp * 128:(grp + 1) * 128, :])
                    dma("sync", xload, waits=w_x, pipe=xlp)
                    x_at = xlp.at()
                    atg = att[:, tg * 2048:(tg + 1) * 2048]

                    def lnsq(e, atg=atg):
                        return e.activation(sq2, atg, AF.Square)
                    step("scalar", lnsq, waits=[ev_at])
                    lsq_at = pa.at()

                    def lnst(e, atg=atg):
                        e.tensor_reduce(rs_l[:, 8:9], atg, AX.X, ALU.add)
                        e.tensor_reduce(rs_l[:, 9:10], sq2, AX.X, ALU.add)
                        e.tensor_scalar(rs_l[:, 10:11], rs_l[:, 8:9], 1.0 / D,
                                        None, ALU.mult)
                        e.tensor_scalar(rs_l[:, 11:12], rs_l[:, 9:10], 1.0 / D,
                                        None, ALU.mult)
                        e.tensor_tensor(rs_l[:, 12:13], rs_l[:, 10:11],
                                        rs_l[:, 10:11], ALU.mult)
                        e.tensor_tensor(rs_l[:, 11:12], rs_l[:, 11:12],
                                        rs_l[:, 12:13], ALU.subtract)
                        e.tensor_scalar(rs_l[:, 11:12], rs_l[:, 11:12], LN_EPS,
                                        None, ALU.add)
                        return e.tensor_scalar(rs_l[:, 10:11], rs_l[:, 10:11],
                                               -1.0, None, ALU.mult)
                    step("vector", lnst, waits=[lsq_at])
                    lst_at = pv.at()

                    def lnrs(e):
                        e.activation(rs_l[:, 11:12], rs_l[:, 11:12], AF.Ln)
                        return e.activation(rs_l[:, 11:12], rs_l[:, 11:12],
                                            AF.Exp, scale=-0.5)
                    step("scalar", lnrs, waits=[lst_at])
                    lrs_at = pa.at()

                    def lnap(e, atg=atg, xrl=xrl):
                        e.tensor_scalar(atg, atg, rs_l[:, 10:11],
                                        rs_l[:, 11:12], ALU.add, ALU.mult)
                        return e.tensor_tensor(xrl, atg, xrl, ALU.add)
                    w_lap = [lrs_at, x_at]
                    if xpar in y1_prev:
                        w_lap.append(y1_prev[xpar])
                    step("vector", lnap, waits=w_lap)
                    lap_at = pv.at()
                    if tg == 1:
                        att_free = lap_at

                    def y1st(e, grp=grp, y1l=y1l):
                        return e.dma_start(
                            s.s_y1[grp * 128:(grp + 1) * 128, :], y1l)
                    dma("sync", y1st, waits=[lap_at], pipe=y1p2[xpar])
                    y1_prev[xpar] = y1p2[xpar].at()
                    self.y1_done.append(y1p2[xpar].at())

    def ph_b(self):
        s, S = self, self.S
        off = 0
        xn2, off = self.carve_b(off, 16 * 1024)
        oacc, off = self.carve_b(off, 16 * 1024)
        pt_off = off
        Pt, off = self.carve_b(off, 16 * 1024)
        sq2 = self.arena.ap()[:, pt_off:pt_off + D]
        y1t, off = self.carve(off, 2 * D)
        y1t_off = off - 2 * D
        x2tm, off = self.carve_b(off, D)
        st3, off = self.carve(off, 32)
        w1c, off = self.carve_b(off, 2 * 2048)
        wgc, off = self.carve_b(off, 2 * 2048)
        w2c, off = self.carve_b(off, 2 * 2048)
        wbc, off = self.carve_b(off, 2 * 1024)
        gw_f, off = self.carve(off, 128)
        gw_b, off = self.carve_b(off, 128)
        gl_t, off = self.carve(off, 10 * 8)
        l9b, off = self.carve_b(off, 8)
        t1b, off = self.carve_b(off, 512)
        gbb, off = self.carve_b(off, 512)
        t1b2 = self.arena.ap().bitcast(BF16)[:, y1t_off * 2:y1t_off * 2 + 512]
        gbb2 = self.arena.ap().bitcast(BF16)[:,
                                            y1t_off * 2 + 512:y1t_off * 2 + 1024]
        assert off <= self.ARENA_W, off
        xn2_3 = xn2.rearrange("p (k t) -> p k t", k=16)
        oacc3 = oacc.rearrange("p (k t) -> p k t", k=16)
        Pt3 = Pt.rearrange("p (k t) -> p k t", k=16)

        pt, pv, pa = S.pipe("b_pt"), S.pipe("b_pv"), S.pipe("b_pa")
        yld2 = [S.pipe("b_yld0"), S.pipe("b_yld1")]
        xtp = S.pipe("b_xtp")
        wld2 = [S.pipe("b_wld0"), S.pipe("b_wld1")]
        w2ld2 = [S.pipe("b_w2ld0"), S.pipe("b_w2ld1")]
        wgl = S.pipe("b_wgl")
        wwr = S.pipe("b_wwr")
        wbc2p = [S.pipe("b_wbc0"), S.pipe("b_wbc1")]
        yst2 = [S.pipe("b_yst0"), S.pipe("b_yst1")]
        PS = self.psum

        bar = [p.at() for p in S._pipes.values() if p.n > 0]
        first = {e: True for e in ENGINES}

        def step(eng, fn, waits=(), par=False):
            w = list(waits)
            if first[eng]:
                w = w + bar
                first[eng] = False
            pp = {"tensor": pt, "vector": pv, "scalar": pa}.get(eng)
            return S.step(eng, fn, w, incs=[(pp, 1)] if pp else [], par=par)

        def dma(eng, fn, waits=(), pipe=None, n=1):
            w = list(waits)
            if first[eng]:
                w = w + bar
                first[eng] = False
            return S.dma(eng, fn, w, pipe=pipe, n=n)

        dma("sync", lambda e: e.dma_start(
            gw_f.rearrange("p (k n) -> p k n", k=16),
            s.gate_w.rearrange("(k p) n -> p k n", p=128)), pipe=wgl)
        step("vector", lambda e: e.tensor_copy(gw_b, gw_f), waits=[wgl.at()])
        gwc_at = pv.at()
        gw_b3 = gw_b.rearrange("p (k n) -> p k n", k=16)

        y1_prev = {}
        wld_prev = {}
        w2_prev = {}
        wbc_prev = {}
        yst_prev = {}
        p0_at = p1_at = None
        xtp_last = None
        pe_xn_last = None
        oacc_free = None

        for t in range(NFT):
            for gg in range(8):
                grp = t * 8 + gg
                par = grp % 2
                ylp = yld2[par]
                yl = y1t[:, par * D:(par + 1) * D]
                w_y = [y1_prev[par]] if par in y1_prev else []

                def yload(e, grp=grp, yl=yl):
                    return e.dma_start(yl, s.s_y1[grp * 128:(grp + 1) * 128, :])
                dma("sync", yload, waits=w_y, pipe=ylp)
                y_at = ylp.at()

                w_sq = [y_at]
                if pa.n:
                    w_sq.append((pa, pa.n))
                if pe_xn_last is not None:
                    w_sq.append(pe_xn_last)
                step("scalar", lambda e, yl=yl: e.activation(sq2, yl, AF.Square),
                     waits=w_sq)
                sq_at = pa.at()

                def lnst(e, yl=yl):
                    e.tensor_reduce(st3[:, 0:1], yl, AX.X, ALU.add)
                    e.tensor_reduce(st3[:, 1:2], sq2, AX.X, ALU.add)
                    e.tensor_scalar(st3[:, 2:3], st3[:, 0:1], 1.0 / D, None,
                                    ALU.mult)
                    e.tensor_scalar(st3[:, 3:4], st3[:, 1:2], 1.0 / D, None,
                                    ALU.mult)
                    e.tensor_tensor(st3[:, 4:5], st3[:, 2:3], st3[:, 2:3],
                                    ALU.mult)
                    e.tensor_tensor(st3[:, 5:6], st3[:, 3:4], st3[:, 4:5],
                                    ALU.subtract)
                    e.tensor_scalar(st3[:, 5:6], st3[:, 5:6], LN_EPS, None,
                                    ALU.add)
                    return e.tensor_scalar(st3[:, 6:7], st3[:, 2:3], -1.0, None,
                                           ALU.mult)
                step("vector", lnst, waits=[sq_at])
                lst_at = pv.at()

                def lnrs(e):
                    e.activation(st3[:, 5:6], st3[:, 5:6], AF.Ln)
                    return e.activation(st3[:, 5:6], st3[:, 5:6], AF.Exp,
                                        scale=-0.5)
                step("scalar", lnrs, waits=[lst_at])
                lrs_at = pa.at()

                w_ap = [lrs_at]
                if xtp_last is not None:
                    w_ap.append(xtp_last)
                step("vector", lambda e, yl=yl: e.tensor_scalar(
                    x2tm, yl, st3[:, 6:7], st3[:, 5:6], ALU.add, ALU.mult),
                    waits=w_ap)
                ap_at = pv.at()

                w_tp = [ap_at]
                if gg == 0 and pe_xn_last is not None:
                    w_tp.append(pe_xn_last)

                def xtpf(e, gg=gg):
                    return [e.dma_start_transpose(
                        xn2_3[:, k, gg * 128:(gg + 1) * 128],
                        x2tm[:, k * 128:(k + 1) * 128]) for k in range(16)]
                dma("sync", xtpf, waits=w_tp, pipe=xtp, n=16)
                xtp_last = xtp.at()

                def gmm(e, gg=gg):
                    for k in range(16):
                        ins = e.matmul(PS[6][:, 0:8],
                                       xn2_3[:, k, gg * 128:(gg + 1) * 128],
                                       gw_b3[:, k, :], start=(k == 0),
                                       stop=(k == 15))
                    return ins
                w_gmm = [xtp_last, gwc_at]
                if pv.n:
                    w_gmm.append((pv, pv.n))
                step("tensor", gmm, waits=w_gmm)
                gmm_at = pt.at()

                L = lambda i: gl_t[:, i * 8:(i + 1) * 8]

                def gch1(e):
                    e.tensor_tensor(L(0), PS[6][:, 0:8], s.gateb_bc[:], ALU.add)
                    e.tensor_reduce(L(1)[:, 0:1], L(0), AX.X, ALU.max)
                    return e.tensor_scalar(L(1)[:, 1:2], L(1)[:, 0:1], -1.0,
                                           None, ALU.mult)
                step("vector", gch1, waits=[gmm_at, (pa, pa.n)])
                g1_at = pv.at()
                step("scalar", lambda e: e.activation(
                    L(2), L(0), AF.Exp, bias=L(1)[:, 1:2]), waits=[g1_at])
                g2_at = pa.at()

                def gch2(e):
                    e.tensor_reduce(L(1)[:, 2:3], L(2), AX.X, ALU.add)
                    e.reciprocal(L(1)[:, 3:4], L(1)[:, 2:3])
                    e.tensor_scalar(L(3), L(2), L(1)[:, 3:4], None, ALU.mult)
                    e.tensor_tensor(L(4), L(3), s.biases_bc[:], ALU.add)
                    e.tensor_reduce(L(1)[:, 4:5], L(4), AX.X, ALU.max)
                    e.tensor_scalar(L(5), L(4), L(1)[:, 4:5], None, ALU.is_ge)
                    e.scalar_tensor_tensor(L(6), L(5), -BIG, L(4), ALU.mult,
                                           ALU.add)
                    e.tensor_reduce(L(1)[:, 5:6], L(6), AX.X, ALU.max)
                    e.tensor_scalar(L(7), L(6), L(1)[:, 5:6], None, ALU.is_ge)
                    e.scalar_tensor_tensor(L(6), L(7), -BIG, L(6), ALU.mult,
                                           ALU.add)
                    e.tensor_reduce(L(1)[:, 6:7], L(6), AX.X, ALU.max)
                    e.tensor_scalar(L(8), L(4), L(1)[:, 6:7], None, ALU.is_ge)
                    e.tensor_tensor(L(9), L(8), L(3), ALU.mult)
                    e.tensor_reduce(L(1)[:, 7:8], L(9), AX.X, ALU.add)
                    e.reciprocal(L(1)[:, 7:8], L(1)[:, 7:8])
                    e.tensor_scalar(L(9), L(9), L(1)[:, 7:8], None, ALU.mult)
                    return e.tensor_copy(l9b, L(9))
                step("vector", gch2, waits=[g2_at, wwr.at()])
                g3_at = pv.at()

                def wwrite(e, t=t, grp=grp, gg=gg):
                    return [e.dma_start(
                        s.s_wg8[t, ee, gg * 128:(gg + 1) * 128],
                        l9b[:, ee:ee + 1]) for ee in range(8)]
                dma("sync", wwrite, waits=[g3_at], pipe=wwr, n=8)
                y1_prev[par] = wwr.at()
            ln_done = [(pv, pv.n), (pa, pa.n), xtp_last, wwr.at()]

            for e_ in range(8):
                shared = (e_ == 0)
                wpar = e_ % 2
                wsl = wbc[:, wpar * 1024:(wpar + 1) * 1024]
                if not shared:
                    w_wbc = [wwr.at()]
                    if wbc_prev.get(wpar) is not None:
                        w_wbc.append(wbc_prev[wpar])

                    def wbcl(e, t=t, e_=e_, wsl=wsl):
                        return e.dma_start(
                            wsl, s.s_wg8[t, e_ - 1:e_, :].to_broadcast(
                                [128, 1024]))
                    dma("gpsimd", wbcl, waits=w_wbc, pipe=wbc2p[wpar])
                    wbc_at = wbc2p[wpar].at()
                else:
                    wbc_at = None

                slots = ((14, 15, 7) if shared
                         else (e_ - 1, NEXP + e_ - 1, e_ - 1))
                for hb in range(16):
                    u = e_ * 16 + hb
                    lpar = u % 2
                    w1l = w1c[:, lpar * 2048:(lpar + 1) * 2048]
                    wgll = wgc[:, lpar * 2048:(lpar + 1) * 2048]
                    wlp = wld2[lpar]
                    w_wl = [wld_prev[lpar]] if lpar in wld_prev else []

                    def wload(e, slots=slots, hb=hb, w1l=w1l, wgll=wgll):
                        return [
                            e.dma_start(w1l, s.s_fab[slots[0], hb]),
                            e.dma_start(wgll, s.s_fab[slots[1], hb])]
                    dma("sync", wload, waits=w_wl, pipe=wlp, n=2)
                    wl_at = wlp.at()

                    w1l3 = w1l.rearrange("p (k n) -> p k n", k=16)
                    wgl3 = wgll.rearrange("p (k n) -> p k n", k=16)

                    def mmA0G0(e, w1l3=w1l3, wgl3=wgl3):
                        for k in range(16):
                            e.matmul(PS[0][:], w1l3[:, k, :],
                                     xn2_3[:, k, 0:512],
                                     start=(k == 0), stop=(k == 15))
                        for k in range(16):
                            ins = e.matmul(PS[2][:], wgl3[:, k, :],
                                           xn2_3[:, k, 0:512],
                                           start=(k == 0), stop=(k == 15))
                        return ins
                    w_mm = [wl_at, ln_done[2]]
                    if u >= 1:
                        w_mm.append(p0_at)
                    if u == 0:
                        w_mm += ln_done[:2]
                    step("tensor", mmA0G0, waits=w_mm)
                    mm_at = pt.at()

                    def mmA1G1(e, w1l3=w1l3, wgl3=wgl3):
                        for k in range(16):
                            e.matmul(PS[1][:], w1l3[:, k, :],
                                     xn2_3[:, k, 512:1024],
                                     start=(k == 0), stop=(k == 15))
                        for k in range(16):
                            ins = e.matmul(PS[3][:], wgl3[:, k, :],
                                           xn2_3[:, k, 512:1024],
                                           start=(k == 0), stop=(k == 15))
                        return ins
                    w_mm1 = [wl_at]
                    if u >= 1:
                        w_mm1.append(p1_at)
                    step("tensor", mmA1G1, waits=w_mm1)
                    mm1_at = pt.at()
                    wld_prev[lpar] = (pt, pt.n)

                    def silu0(e):
                        return e.activation(t1b, PS[0][:], AF.Silu)
                    w_s0 = [mm_at]
                    if u >= 1:
                        w_s0.append(p0_at)
                    step("scalar", silu0, waits=w_s0)
                    s0_at = pa.at()

                    def gp0(e, hb=hb, wsl=wsl, shared=shared):
                        if shared:
                            e.tensor_copy(gbb, PS[2][:])
                        else:
                            e.tensor_tensor(gbb, PS[2][:], wsl[:, 0:512],
                                            ALU.mult)
                        return e.tensor_tensor(Pt3[:, hb, 0:512], t1b, gbb,
                                               ALU.mult)
                    w_g0 = [mm_at, s0_at]
                    if wbc_at is not None:
                        w_g0.append(wbc_at)
                    step("vector", gp0, waits=w_g0)
                    p0_at = pv.at()

                    def silu1(e):
                        return e.activation(t1b2, PS[1][:], AF.Silu)
                    w_s1 = [mm1_at]
                    if u >= 1:
                        w_s1.append(p1_at)
                    if u == 0:
                        w_s1.append(ln_done[3])
                    step("scalar", silu1, waits=w_s1)
                    s1_at = pa.at()

                    def gp1(e, hb=hb, wsl=wsl, shared=shared):
                        if shared:
                            e.tensor_copy(gbb2, PS[3][:])
                        else:
                            e.tensor_tensor(gbb2, PS[3][:], wsl[:, 512:1024],
                                            ALU.mult)
                        return e.tensor_tensor(Pt3[:, hb, 512:1024], t1b2,
                                               gbb2, ALU.mult)
                    step("vector", gp1, waits=[s1_at])
                    p1_at = pv.at()
                    pbuild_at = pv.at()
                P_at = pbuild_at
                wbc_prev[e_ % 2] = (pv, pv.n)

                for db in range(16):
                    u2 = e_ * 16 + db
                    lpar = u2 % 2
                    w2l = w2c[:, lpar * 2048:(lpar + 1) * 2048]
                    w2p = w2ld2[lpar]
                    w_w2 = [w2_prev[lpar]] if lpar in w2_prev else []

                    def w2load(e, slots=slots, db=db, w2l=w2l):
                        return e.dma_start(w2l, s.s_f2b[slots[2], db])
                    dma("sync", w2load, waits=w_w2, pipe=w2p)
                    w2_at = w2p.at()
                    w2l3 = w2l.rearrange("p (k n) -> p k n", k=16)

                    def mmO(e, w2l3=w2l3):
                        for k in range(16):
                            for half in range(2):
                                ins = e.matmul(
                                    PS[4 + half][:], w2l3[:, k, :],
                                    Pt3[:, k, half * 512:(half + 1) * 512],
                                    start=(k == 0), stop=(k == 15))
                        return ins
                    w_mo = [w2_at, P_at]
                    if db >= 1 or e_ >= 1:
                        w_mo.append(oadd_at)
                    if oacc_free is not None and e_ == 0 and db == 0:
                        w_mo.append(oacc_free)
                    step("tensor", mmO, waits=w_mo)
                    mo_at = pt.at()
                    w2_prev[lpar] = (pt, pt.n)

                    def oadd(e, db=db, shared=shared):
                        if shared:
                            e.tensor_copy(oacc3[:, db, 0:512], PS[4][:])
                            return e.tensor_copy(oacc3[:, db, 512:1024],
                                                 PS[5][:])
                        e.tensor_tensor(oacc3[:, db, 0:512],
                                        oacc3[:, db, 0:512], PS[4][:], ALU.add)
                        return e.tensor_tensor(oacc3[:, db, 512:1024],
                                               oacc3[:, db, 512:1024],
                                               PS[5][:], ALU.add)
                    step("vector", oadd, par=True, waits=[mo_at])
                    oadd_at = pv.at()
            pe_xn_last = (pt, pt.n)
            if "dbg_oacc" in self.debug:
                if not hasattr(self, "_dbg_oacc"):
                    self._dbg_oacc = self.dram("dbg_oacc", [NFT, 128, 16 * 1024],
                                               BF16)
                    self._dbg_op = S.pipe("dbg_oacc_p")
                dma("gpsimd", lambda e, t=t: e.dma_start(
                    self._dbg_oacc[t], oacc), waits=[oadd_at],
                    pipe=self._dbg_op)

            for gg in range(8):
                grp = t * 8 + gg
                par = grp % 2
                ylp = yld2[par]
                yl = y1t[:, par * D:(par + 1) * D]

                def yload2(e, grp=grp, yl=yl):
                    return e.dma_start(yl, s.s_y1[grp * 128:(grp + 1) * 128, :])
                dma("sync", yload2, waits=[y1_prev[par], oadd_at], pipe=ylp)
                y_at = ylp.at()

                def otpf(e, gg=gg):
                    return [e.dma_start_transpose(
                        x2tm[:, k * 128:(k + 1) * 128],
                        oacc3[:, k, gg * 128:(gg + 1) * 128]) for k in range(16)]
                w_otp = [oadd_at, (pv, pv.n)]
                if xtp_last is not None:
                    w_otp.append(xtp_last)
                dma("sync", otpf, waits=w_otp, pipe=xtp, n=16)
                xtp_last = xtp.at()

                def cst(e):
                    return e.tensor_copy(sq2, x2tm)
                w_cst = [xtp_last]
                if pa.n:
                    w_cst.append((pa, pa.n))
                step("vector", cst, waits=w_cst)
                cst_at = pv.at()

                def lnsq2(e):
                    return e.activation(st3[:, 8:9] if False else sq2, sq2,
                                        AF.Square, accum_out=None)
                def lnst2a(e):
                    return e.tensor_reduce(st3[:, 8:9], sq2, AX.X, ALU.add)
                step("vector", lnst2a, waits=[cst_at])

                def lnsq2b(e):
                    return e.activation(y1t[:, 0:0] if False else sq2, sq2,
                                        AF.Square)
                step("scalar", lnsq2b, waits=[(pv, pv.n)])
                sqb_at = pa.at()

                def lnst2b(e):
                    e.tensor_reduce(st3[:, 9:10], sq2, AX.X, ALU.add)
                    e.tensor_scalar(st3[:, 10:11], st3[:, 8:9], 1.0 / D, None,
                                    ALU.mult)
                    e.tensor_scalar(st3[:, 11:12], st3[:, 9:10], 1.0 / D, None,
                                    ALU.mult)
                    e.tensor_tensor(st3[:, 12:13], st3[:, 10:11], st3[:, 10:11],
                                    ALU.mult)
                    e.tensor_tensor(st3[:, 11:12], st3[:, 11:12], st3[:, 12:13],
                                    ALU.subtract)
                    e.tensor_scalar(st3[:, 11:12], st3[:, 11:12], LN_EPS, None,
                                    ALU.add)
                    return e.tensor_scalar(st3[:, 10:11], st3[:, 10:11], -1.0,
                                           None, ALU.mult)
                step("vector", lnst2b, waits=[sqb_at])
                l2b_at = pv.at()

                def lnrs2(e):
                    e.activation(st3[:, 11:12], st3[:, 11:12], AF.Ln)
                    return e.activation(st3[:, 11:12], st3[:, 11:12], AF.Exp,
                                        scale=-0.5)
                step("scalar", lnrs2, waits=[l2b_at])
                lrs2_at = pa.at()

                def fin(e, yl=yl):
                    e.tensor_copy(sq2, x2tm)
                    e.tensor_scalar(sq2, sq2, st3[:, 10:11], st3[:, 11:12],
                                    ALU.add, ALU.mult)
                    return e.tensor_tensor(yl, sq2, yl, ALU.add)
                w_fin = [lrs2_at, y_at]
                if par in yst_prev:
                    w_fin.append(yst_prev[par])
                step("vector", fin, waits=w_fin)
                fin_at = pv.at()

                def yst(e, grp=grp, yl=yl):
                    return e.dma_start(s.y_out[grp * 128:(grp + 1) * 128, :],
                                       yl)
                ystp = yst2[par]
                dma("sync", yst, waits=[fin_at], pipe=ystp)
                y1_prev[par] = ystp.at()
                yst_prev[par] = ystp.at()
            oacc_free = (pv, pv.n)
        self.fin_waits = [yst2[0].at(), yst2[1].at()]

    def ph_final(self):
        s, S = self, self.S
        if "dbg_scal" in self.debug:
            dbg = self.dram("dbg_scal", [7, 128, NG * 64], F32)
            dpipe = S.pipe("dbg")
            waits = [p.at() for p in S._pipes.values() if p.n > 0]
            tens = [s.beta_s, s.negsc1_s, s.egjc_s, s.egLmg_s, s.egL_s, s.gj_s,
                    s.ba_tm]
            S.dma("gpsimd", lambda e: [
                e.dma_start(dbg[i, :, :t.ap().shape[1]], t.ap()[:])
                for i, t in enumerate(tens)], waits=waits, pipe=dpipe, n=7)
        waits = [p.at() for p in S._pipes.values() if p.n > 0]
        S.step("sync", lambda e: e.nop(), waits=waits)


def stage_inputs(inputs):
    f32 = np.float32
    sl = np.tril(np.ones((128, 128), f32), -1)
    ui = np.triu(np.ones((128, 128), f32))
    c_masks = np.stack([
        BIG * (1.0 - sl),
        -BIG * (1.0 - ui),
        ui,
        np.eye(128, dtype=f32),
    ]).astype(f32)
    pad_col = np.zeros((D, 1), f32)
    neg = np.full((1,), -1.0e30, f32)
    shared = {
        "w_qkv": np.ascontiguousarray(inputs["L0_attn_W_qkv"]),
        "w_z": np.ascontiguousarray(inputs["L0_attn_W_z"]),
        "w_ba": np.ascontiguousarray(
            np.concatenate([inputs["L0_attn_W_b"], inputs["L0_attn_W_a"]], 1)),
        "conv_w": np.ascontiguousarray(inputs["L0_attn_conv_weight"]),
        "dt_bias_r": np.ascontiguousarray(inputs["L0_attn_dt_bias"][None, :]),
        "a_log_r": np.ascontiguousarray(inputs["L0_attn_A_log"][None, :]),
        "norm_w_r": np.ascontiguousarray(inputs["L0_attn_norm_w"][None, :]),
        "w_out": np.ascontiguousarray(inputs["L0_attn_W_out"]),
        "ffn_w1": np.ascontiguousarray(inputs["L0_ffn_W1"]),
        "ffn_wg": np.ascontiguousarray(inputs["L0_ffn_Wg"]),
        "ffn_w2": np.ascontiguousarray(inputs["L0_ffn_W2"]),
        "ffn_ws1": np.ascontiguousarray(inputs["L0_ffn_Ws1"]),
        "ffn_wsg": np.ascontiguousarray(inputs["L0_ffn_Wsg"]),
        "ffn_ws2": np.ascontiguousarray(inputs["L0_ffn_Ws2"]),
        "gate_w": np.ascontiguousarray(
            np.concatenate([inputs["L0_ffn_gate_w"], pad_col], 1)),
        "gate_b_r": np.ascontiguousarray(
            np.concatenate([inputs["L0_ffn_gate_b"], neg])[None, :]),
        "biases_r": np.ascontiguousarray(
            np.concatenate([inputs["L0_ffn_biases"], neg])[None, :]),
        "c_masks": c_masks,
    }
    shared = {k: np.asarray(v, f32) for k, v in shared.items()}
    hid = np.asarray(inputs["L0_attn_hidden"], f32)
    in_maps = []
    for c in range(8):
        m = dict(shared)
        m["hidden"] = np.ascontiguousarray(hid[c])
        in_maps.append(m)
    return in_maps


_NC_CACHE = {}


def kernel(**inputs):
    key = "full"
    if key not in _NC_CACHE:
        _NC_CACHE[key] = Builder().build()
    nc = _NC_CACHE[key]
    in_maps = stage_inputs(inputs)
    res = run_bass_kernel_spmd(nc, in_maps, core_ids=list(range(8)))
    return np.stack([np.asarray(r["y"], np.float32) for r in res.results])


